# version 16
# baseline (speedup 1.0000x reference)
from contextlib import ExitStack

import numpy as np

import concourse.bass as bass
import concourse.mybir as mybir
from concourse.bass_utils import run_bass_kernel_spmd

F32 = mybir.dt.float32
BF16 = mybir.dt.bfloat16
FP16 = mybir.dt.float16
I32 = mybir.dt.int32
AF = mybir.ActivationFunctionType
ALU = mybir.AluOpType
AX = mybir.AxisListType.X

D4 = 64.0 ** 0.25
LN16 = float(np.log(16.0))
EPS16 = 1e-6 / 16.0
LN_EPS = 1e-5
RBIAS = 1023.5
TBIG = 1e9

T = 2048
NT = 16
NJ = 4
NC = 8
H = 16
I_FF = 2816
NI = 22

ENGINES = ("tensor", "vector", "scalar", "gpsimd", "sync")
ESEM = {"tensor": "T", "vector": "V", "scalar": "S", "gpsimd": "G", "sync": "SY"}


class Prog:
    def __init__(self):
        self.ops = {e: [] for e in ENGINES}
        self.counts = {}
        self.hw = {}

    def emit(self, engine, fn, waits=(), dma=False, inc=True, sem=None):
        by = 1
        if sem is None:
            if dma:
                sem = "DG" if engine == "gpsimd" else "DS"
                by = 16
            elif inc:
                sem = ESEM[engine]
        elif dma:
            by = 16
        waits = list(waits)
        if dma and self.counts.get(sem, 0) > 0:
            waits.append((sem, self.counts[sem]))
        need = []
        for ev in waits:
            if ev is None:
                continue
            s, v = ev
            if self.hw.get((engine, s), 0) < v:
                need.append((s, v))
                self.hw[(engine, s)] = v
        ev_out = None
        if sem is not None:
            self.counts[sem] = self.counts.get(sem, 0) + by
            ev_out = (sem, self.counts[sem])
        self.ops[engine].append((need, fn, sem, by))
        return ev_out


def build_nc(upto="ALL", dbg=()):
    nc = bass.Bass("TRN2", target_bir_lowering=False, debug=False)
    p = Prog()

    def din(name, shape):
        return nc.dram_tensor(name, shape, F32, kind="ExternalInput").ap()

    d_xfm = din("x_fm", [128, 8 * T])
    d_xtm = din("x_tm", [T, 1024])
    d_wqt = din("wqt", [128, 8192])
    d_wkt = din("wkt", [128, 8192])
    d_wvt = din("wvt", [128, 8192])
    d_wot = din("wot", [128, 8192])
    d_orft = din("orft", [64, 256])
    d_ind = din("indh", [128, 128])
    d_ident = din("identh", [128, 128])
    d_bq = din("bqh", [128, 8])
    d_bk = din("bkh", [128, 8])
    d_bv = din("bvh", [128, 8])
    d_bo = din("boh", [1, 1024])
    d_clipx = din("clip_x", [1, 1])
    d_clipd = din("clip_down", [1, 1])
    d_wgm = din("wgq_my", [384, 1024])
    d_wum = din("wuq_my", [384, 1024])
    d_wdm = din("wdq_my", [128, I_FF])
    d_out = nc.dram_tensor("out", [T, 1024], F32, kind="ExternalOutput").ap()

    h_cc1i = nc.dram_tensor("cc1i", [128, 2080], F32)
    h_cc1o = nc.dram_tensor("cc1o", [128, 2080], F32)
    d_cc1i, d_cc1o = h_cc1i.ap(), h_cc1o.ap()
    d_mx = nc.dram_tensor("mxd", [128], F32).ap()
    h_ccmi = nc.dram_tensor("ccmi", [1], F32)
    h_ccmo = nc.dram_tensor("ccmo", [1], F32)
    d_mx2 = nc.dram_tensor("mxd2", [128], F32).ap()
    h_ccmi2 = nc.dram_tensor("ccmi2", [1], F32)
    h_ccmo2 = nc.dram_tensor("ccmo2", [1], F32)
    d_scal = nc.dram_tensor("scald", [8], F32).ap()
    d_x2 = nc.dram_tensor("x2d", [T, 1024], F32).ap()
    d_h = nc.dram_tensor("hd", [NI * 128, T], F32).ap()
    d_swd = nc.dram_tensor("swdd", [1024], F32).ap()
    h_wqi = nc.dram_tensor("wqi", [128, 8960], BF16)
    h_wqo = nc.dram_tensor("wqo", [1024, 8960], BF16)
    h_sci = nc.dram_tensor("sci", [128, 8], F32)
    h_sco = nc.dram_tensor("sco", [1024, 8], F32)
    d_wqi, d_wqo = h_wqi.ap(), h_wqo.ap()
    d_sci, d_sco = h_sci.ap(), h_sco.ap()

    es = ExitStack()
    inner_scopes = []

    def track(sc):
        inner_scopes.append(sc)
        return sc

    def SBL(name, shape, dt):
        return es.enter_context(nc.sbuf_tensor(name, shape, dt, side="left"))

    sems = {}
    for s in ("T", "V", "S", "G", "SY", "DG", "DS", "CC", "D1", "D2", "D3", "D4"):
        sems[s] = es.enter_context(nc.semaphore(s))

    bank = [es.enter_context(nc.psum_tensor(f"pb{i}", [128, 512], F32)) for i in range(6)]
    tbank = [es.enter_context(nc.psum_tensor(f"tb{i}", [128, 1024], BF16)) for i in range(2)]
    bwr = {}
    brd = {}

    cst = SBL("cst", [128, 8], F32)
    ones_col = SBL("ones_col", [128, 1], BF16)
    ones_row = SBL("ones_row", [1, 512], BF16)
    idb = SBL("idb", [128, 128], BF16)
    ind = SBL("ind", [128, 128], BF16)
    orf2x = SBL("orf2x", [128, 256], BF16)
    bqf = SBL("bqf", [128, 8], F32)
    bkf = SBL("bkf", [128, 8], F32)
    bvb = SBL("bvb", [128, 8], BF16)
    bo_sb = SBL("bo_sb", [1, 1024], BF16)
    bo_adj = SBL("bo_adj", [1, 1024], BF16)
    sig_s = SBL("sig_s", [1, 2], F32)
    clip_sb = SBL("clip_sb", [1, 2], F32)
    scal = SBL("scal", [128, 4], F32)
    scal1 = SBL("scal1", [1, 8], F32)
    mrow = SBL("mrow", [1, 128], F32)
    runmax = SBL("runmax", [128, NT], F32)
    runmax2 = SBL("runmax2", [128, NI * NJ], F32)
    rmx1 = SBL("rmx1", [128, 2], F32)
    swg_all = SBL("swg_all", [128, NI], F32)
    swu_all = SBL("swu_all", [128, NI], F32)
    sxswg = SBL("sxswg", [128, NI], F32)
    sxswu = SBL("sxswu", [128, NI], F32)
    swd_all = SBL("swd_all", [128, 8], F32)
    st_t = SBL("st_t", [128, 16], F32)
    r_t = [SBL(f"rt{r}", [128, 1024], F32) for r in (0, 1)]
    rsq_t = [SBL(f"rsq{r}", [128, 1024], F32) for r in (0, 1)]
    wst = SBL("wst", [128, 4], F32)

    G, SY, TE, VE, SE = "gpsimd", "sync", "tensor", "vector", "scalar"
    dbg_evs = []

    def DBG(name, ap, after_list):
        if name in dbg:
            d = nc.dram_tensor("dbg_" + name, list(ap.shape), F32,
                               kind="ExternalOutput").ap()
            q = SY if ap.dtype == F32 else G
            dbg_evs.append(p.emit(q, lambda e, dd=d, aa=ap: e.dma_start(out=dd, in_=aa),
                                  waits=after_list, dma=True))

    def dma(q, out, in_, waits=(), stream=None):
        return p.emit(q, lambda e, o=out, i=in_: e.dma_start(out=o, in_=i),
                      waits=waits, dma=True, sem=stream)

    scope_Q = track(ExitStack())
    q_sb = scope_Q.enter_context(nc.sbuf_tensor("q_sb", [128, 8 * T], BF16, side="left"))
    scope_K = track(ExitStack())
    k_sb = scope_K.enter_context(nc.sbuf_tensor("k_sb", [128, 8 * T], BF16, side="left"))
    scope_R0 = track(ExitStack())
    sbr0 = lambda name, shape, dt: scope_R0.enter_context(
        nc.sbuf_tensor(name, shape, dt, side="right"))
    expnq = sbr0("expnq", [16, 2048], BF16)
    negnk = sbr0("negnk", [128, 256], F32)
    vs_eps = sbr0("vs_eps", [1, 1040], BF16)
    bufs_eps = sbr0("bufs_eps", [1, 1040], BF16)
    buf_sb = sbr0("buf_sb", [128, 2080], F32)
    buf_bf = sbr0("buf_bf", [128, 2080], BF16)
    scope_RV = track(ExitStack())
    v_ext = scope_RV.enter_context(nc.sbuf_tensor("v_ext", [128, NT * 1040], BF16, side="right"))
    scope_P1 = track(ExitStack())
    x_fm = scope_P1.enter_context(nc.sbuf_tensor("x_fm_s", [128, 8 * T], BF16, side="left"))
    orf_f = scope_P1.enter_context(nc.sbuf_tensor("orf_f", [128, 256], F32, side="left"))
    sqq = [scope_P1.enter_context(nc.sbuf_tensor(f"sqq{r}", [128, 512], BF16, side="left"))
           for r in range(8)]
    sqk = sqq
    scope_P2 = track(ExitStack())
    wqk_sb = scope_P2.enter_context(nc.sbuf_tensor("wqk_sb", [128, 8192], BF16, side="left"))

    ld_xfm = dma(G, x_fm[:, :], d_xfm[:, :])
    ld_wq = dma(G, wqk_sb[:, :], d_wqt[:, :])
    ld_orf0 = dma(G, orf_f[0:64, :], d_orft[:, :])
    ld_orf1 = dma(G, orf_f[64:128, :], d_orft[:, :])
    ld_ind = dma(G, ind[:, :], d_ind[:, :])
    ld_id = dma(G, idb[:, :], d_ident[:, :])
    ld_bq = dma(G, bqf[:, :], d_bq[:, :])
    ld_bk = dma(G, bkf[:, :], d_bk[:, :])
    ld_bv = dma(G, bvb[:, :], d_bv[:, :])
    ld_bo = dma(G, bo_sb[:, :], d_bo[:, :])
    ld_cl = dma(G, clip_sb[0:1, 0:1], d_clipx[:, :])
    ld_cl2 = dma(G, clip_sb[0:1, 1:2], d_clipd[:, :])

    ev_c = []
    for col, val in ((0, LN16), (1, -LN16), (2, LN_EPS), (3, RBIAS), (4, -RBIAS)):
        ev_c.append(p.emit(VE, lambda e, c=col, v=val: e.memset(cst[:, c:c + 1], v)))
    ev_c.append(p.emit(VE, lambda e: e.memset(ones_col[:, :], 1.0)))
    ev_c.append(p.emit(VE, lambda e: e.memset(ones_row[:, :], 1.0)))
    ev_cst = ev_c[-1]

    ev_wmb = p.emit(VE, lambda e: e.memset(buf_bf[:, 0:512], 0.0))
    warm_last = None
    for wi in range(120):
        warm_last = p.emit(TE, lambda e: e.matmul(bank[5][:, :], idb[:, 0:128],
                                                  buf_bf[:, 0:512],
                                                  start=True, stop=True),
                           waits=([ld_id, ev_wmb] if wi == 0 else []),
                           inc=(wi == 119))
    brd[5] = warm_last

    ev_orf = p.emit(SE, lambda e: e.activation(orf2x[:, :], orf_f[:, :], AF.Identity,
                                               bias=0.0, scale=1.0 / D4),
                    waits=[ld_orf0, ld_orf1])
    ev_sig = p.emit(SE, lambda e: e.activation(sig_s[0:1, 0:2], clip_sb[0:1, 0:2],
                                               AF.Sigmoid, bias=0.0, scale=1.0),
                    waits=[ld_cl, ld_cl2])

    sq_rel = [None] * 8
    last_q_ev = None
    last_k_ev = None
    last_qk_mm = None
    ev_expnq = None
    ev_negnk = None
    ld_wk = None
    for wpass in ("q", "k"):
        if wpass == "k":
            ld_wk = dma(G, wqk_sb[:, :], d_wkt[:, :], waits=[last_qk_mm])
        for j in range(NJ):
            sq_evs = []
            for c in range(NC):
                bidx = (j * NC + c) % 2 + (0 if wpass == "q" else 2)
                b_ap = bqf if wpass == "q" else bkf
                dst = q_sb if wpass == "q" else k_sb
                sqt = sqq[c]
                ldw = ld_wq if wpass == "q" else ld_wk
                mm = None
                for kc in range(NC):
                    st = kc == 0
                    w = [ld_xfm, ldw]
                    if st:
                        w.append(brd.get(bidx))
                    mm = p.emit(TE, lambda e, b=bidx, kk=kc, cc=c, jj=j,
                                s=st, sp=(kc == NC - 1):
                                e.matmul(bank[b][:, :],
                                         wqk_sb[:, 1024 * cc + 128 * kk:1024 * cc + 128 * (kk + 1)],
                                         x_fm[:, 2048 * kk + 512 * jj:2048 * kk + 512 * (jj + 1)],
                                         start=s, stop=sp),
                                waits=w, inc=(kc == NC - 1))
                bwr[bidx] = mm
                last_qk_mm = mm
                e1 = p.emit(SE, lambda e, b=bidx, d=dst, ba=b_ap, cc=c, jj=j:
                            e.activation(d[:, 2048 * cc + 512 * jj:2048 * cc + 512 * (jj + 1)],
                                         bank[b][:, :], AF.Identity,
                                         bias=ba[:, cc:cc + 1], scale=1.0),
                            waits=[mm, ld_bq, ld_bk])
                e2 = p.emit(VE, lambda e, d=dst, cc=c, jj=j, sq=sqt:
                            e.tensor_tensor(sq[:, :],
                                            d[:, 2048 * cc + 512 * jj:2048 * cc + 512 * (jj + 1)],
                                            d[:, 2048 * cc + 512 * jj:2048 * cc + 512 * (jj + 1)],
                                            ALU.mult),
                            waits=[e1, sq_rel[c]])
                brd[bidx] = e1
                sq_evs.append(e2)
                if wpass == "q":
                    last_q_ev = e1
                else:
                    last_k_ev = e1
            if wpass == "q":
                n_last = None
                for c in range(NC):
                    n_last = p.emit(TE, lambda e, cc=c, sq=sqq[c], s=(c == 0):
                                    e.matmul(bank[4][0:16, :], ind[:, 16 * cc:16 * (cc + 1)],
                                             sq[:, :], start=s, stop=(cc == NC - 1)),
                                    waits=[sq_evs[c], ld_ind, brd.get(4) if c == 0 else None],
                                    inc=(c == NC - 1))
                for c in range(NC):
                    sq_rel[c] = n_last
                bwr[4] = n_last
                ev_expnq = p.emit(SE, lambda e, jj=j:
                                  e.activation(expnq[:, 512 * jj:512 * (jj + 1)],
                                               bank[4][0:16, :], AF.Exp,
                                               bias=cst[0:16, 0:1], scale=1.0 / 16.0),
                                  waits=[n_last, ev_cst])
                brd[4] = ev_expnq
            else:
                n_lasts = [None] * 4
                for c in range(NC):
                    for ts_ in range(4):
                        nb = (4, 5, 0, 1)[ts_]
                        n_lasts[ts_] = p.emit(TE, lambda e, cc=c, sq=sqq[c], ts=ts_, b4=nb,
                                              s=(c == 0):
                                              e.matmul(bank[b4][:, 0:16],
                                                       sq[:, 128 * ts:128 * (ts + 1)],
                                                       ind[:, 16 * cc:16 * (cc + 1)],
                                                       start=s, stop=(cc == NC - 1)),
                                              waits=[sq_evs[c], ld_ind,
                                                     brd.get(nb) if c == 0 else None],
                                              inc=(c == NC - 1))
                    sq_rel[c] = n_lasts[3] if c == NC - 1 else sq_rel[c]
                for c in range(NC):
                    sq_rel[c] = n_lasts[3]
                for ts_ in range(4):
                    nb = (4, 5, 0, 1)[ts_]
                    bwr[nb] = n_lasts[ts_]
                    ev_negnk = p.emit(SE, lambda e, jj=j, ts=ts_, b4=nb:
                                      e.activation(negnk[:, 64 * jj + 16 * ts:64 * jj + 16 * (ts + 1)],
                                                   bank[b4][:, 0:16], AF.Identity,
                                                   bias=cst[:, 1:2], scale=-1.0 / 16.0),
                                      waits=[n_lasts[ts_], ev_cst])
                    brd[nb] = ev_negnk
    DBG("q_sb", q_sb[:, :], [last_q_ev])
    DBG("k_sb", k_sb[:, :], [last_k_ev])
    DBG("expnq", expnq[:, :], [ev_expnq])
    DBG("negnk", negnk[:, :], [ev_negnk])

    scope_P2.close()
    scope_P3 = track(ExitStack())
    wv_sb = scope_P3.enter_context(nc.sbuf_tensor("wv_sb", [128, 8192], BF16, side="left"))
    ld_wv = dma(G, wv_sb[:, :], d_wvt[:, :], waits=[last_qk_mm])

    last_vevac = None
    ev_vdone = None
    vq = []
    vs_state = [None]

    def emit_vsum(ii0, vev):
        for seg, (lo, n, bk2) in enumerate(((0, 512, 4), (512, 512, 5), (1024, 16, 3))):
            w = [vev, ev_cst]
            if ii0 == 0:
                w.append(brd.get(bk2))
            vs_state[0] = p.emit(TE, lambda e, ii=ii0, l=lo, nn=n, b2=bk2, s=(ii0 == 0):
                                 e.matmul(bank[b2][0:1, 0:nn], ones_col[:, :],
                                          v_ext[:, 1040 * ii + l:1040 * ii + l + nn],
                                          start=s, stop=(ii == NT - 1)),
                                 waits=w, inc=(ii0 == NT - 1 and seg == 2))

    for i in range(NT):
        evs2 = []
        for dg in range(2):
            bidx = (2 * i + dg) % 2
            mm = None
            for kc in range(NC):
                st = kc == 0
                w = [ld_xfm, ld_wv]
                if st:
                    w.append(brd.get(bidx))
                mm = p.emit(TE, lambda e, b=bidx, kk=kc, ii=i, d=dg, s=st,
                            sp=(kc == NC - 1):
                            e.matmul(bank[b][:, :],
                                     x_fm[:, 2048 * kk + 128 * ii:2048 * kk + 128 * (ii + 1)],
                                     wv_sb[:, 1024 * kk + 512 * d:1024 * kk + 512 * (d + 1)],
                                     start=s, stop=sp),
                            waits=w, inc=(kc == NC - 1))
            bwr[bidx] = mm
            base = 1040 * i + 520 * dg
            e1 = p.emit(VE, lambda e, b=bidx, bs=base:
                        e.tensor_copy(
                            v_ext[:, bs:bs + 520].rearrange("p (h d) -> p h d", d=65)[:, :, 0:64],
                            bank[b][:, :].rearrange("p (h d) -> p h d", d=64)),
                        waits=[mm])
            e2 = p.emit(VE, lambda e, bs=base:
                        e.memset(v_ext[:, bs:bs + 520].rearrange("p (h d) -> p h d", d=65)[:, :, 64:65], 1.0),
                        waits=[e1])
            brd[bidx] = e2
            evs2.append(e2)
        last_vevac = evs2[-1]
        vq.append(evs2[-1])
        if i >= 1:
            emit_vsum(i - 1, vq[i - 1])
    emit_vsum(NT - 1, vq[NT - 1])
    ev_vdone = vs_state[0]
    for bk2 in (4, 5, 3):
        bwr[bk2] = ev_vdone
    p.emit(SE, lambda e: e.activation(vs_eps[0:1, 0:512], bank[4][0:1, 0:512],
                                      AF.Identity, bias=0.0, scale=EPS16),
           waits=[ev_vdone])
    p.emit(SE, lambda e: e.activation(vs_eps[0:1, 512:1024], bank[5][0:1, 0:512],
                                      AF.Identity, bias=0.0, scale=EPS16),
           waits=[ev_vdone])
    ev_vseps = p.emit(SE, lambda e: e.activation(vs_eps[0:1, 1024:1040], bank[3][0:1, 0:16],
                                                 AF.Identity, bias=0.0, scale=EPS16),
                      waits=[ev_vdone])
    brd[4] = ev_vseps
    brd[5] = ev_vseps
    brd[3] = ev_vseps
    DBG("v_ext", v_ext[:, :], [last_vevac])
    DBG("vs_eps", vs_eps[:, :], [ev_vseps])

    if upto == "P":
        _finish(p, nc, es, sems, inner_scopes)
        return nc

    scope_P3.close()
    scope_P1.close()

    scope_A = track(ExitStack())
    kf_t = [scope_A.enter_context(nc.sbuf_tensor(f"kf{r}", [128, 256], BF16, side="right"))
            for r in (0, 1, 2)]
    kf_rel = [None, None, None]
    ev_bufs = None
    buf_done = None
    for h in range(H):
        b = h % 2
        c = h // 2
        pend = None
        for i in range(NT):
            r = i % 3
            pb_i = i % 2
            mm = p.emit(TE, lambda e, bb=b, cc=c, ii=i, pbi=pb_i:
                        e.matmul(bank[pbi][:, 0:256],
                                 k_sb[64 * bb:64 * (bb + 1),
                                      2048 * cc + 128 * ii:2048 * cc + 128 * (ii + 1)],
                                 orf2x[64 * bb:64 * (bb + 1), :],
                                 start=True, stop=True),
                        waits=[ev_orf, last_k_ev, ev_negnk, brd.get(pb_i), kf_rel[r]])
            bwr[pb_i] = mm
            ef = p.emit(SE, lambda e, rr=r, pbi=pb_i, ii=i, hh=h:
                        e.activation(kf_t[rr][:, :], bank[pbi][:, 0:256], AF.Exp,
                                     bias=negnk[:, 16 * ii + hh:16 * ii + hh + 1],
                                     scale=1.0),
                        waits=[mm])
            brd[pb_i] = ef
            if pend is not None:
                pi, pr, pef = pend
                for mc in range(2):
                    w = [pef, last_vevac]
                    if pi == 0:
                        w.append(brd.get(2 + mc))
                    buf_done = p.emit(TE, lambda e, rr=pr, m=mc, ii=pi, hh=h,
                                      s=(pi == 0):
                                      e.matmul(bank[2 + m][:, 0:65],
                                               kf_t[rr][:, 128 * m:128 * (m + 1)],
                                               v_ext[:, 1040 * ii + 65 * hh:1040 * ii + 65 * (hh + 1)],
                                               start=s, stop=False),
                                      waits=w, inc=(mc == 1))
                kf_rel[pr] = buf_done
            pend = (i, r, ef)
        pi, pr, pef = pend
        for mc in range(2):
            w = [pef, last_vevac]
            buf_done = p.emit(TE, lambda e, rr=pr, m=mc, ii=pi, hh=h:
                              e.matmul(bank[2 + m][:, 0:65],
                                       kf_t[rr][:, 128 * m:128 * (m + 1)],
                                       v_ext[:, 1040 * ii + 65 * hh:1040 * ii + 65 * (hh + 1)],
                                       start=False, stop=False),
                              waits=w, inc=(mc == 1))
            kf_rel[pr] = buf_done
        for mc in range(2):
            buf_done = p.emit(TE, lambda e, m=mc, hh=h:
                              e.matmul(bank[2 + m][:, 0:65],
                                       ones_row[0:1, 0:128],
                                       vs_eps[0:1, 65 * hh:65 * (hh + 1)],
                                       start=False, stop=True),
                              waits=[ev_vseps], inc=True)
            bwr[2 + mc] = buf_done
        ev_bufs = None
        for mc in range(2):
            ev_bufs = p.emit(VE, lambda e, hh=h, m=mc:
                             e.tensor_copy(buf_sb[:, 130 * hh + 65 * m:130 * hh + 65 * (m + 1)],
                                           bank[2 + m][:, 0:65]),
                             waits=[bwr[2 + mc]])
            brd[2 + mc] = ev_bufs
    last_bufmm = buf_done
    DBG("buf_loc", buf_sb[:, :], [ev_bufs])

    if upto == "A":
        _finish(p, nc, es, sems, inner_scopes)
        return nc

    scope_A.close()
    scope_RV.close()
    scope_K.close()

    st1 = dma(SY, d_cc1i[:, :], buf_sb[:, :], waits=[ev_bufs], stream="D2")
    cc1 = p.emit(G, lambda e: e.collective_compute(
        "AllReduce", ALU.add,
        replica_groups=[[0, 1], [2, 3], [4, 5], [6, 7]],
        ins=[h_cc1i.ap()], outs=[h_cc1o.ap()]),
        waits=[st1], sem="CC")
    ld1 = dma(SY, buf_sb[:, :], d_cc1o[:, :], waits=[cc1, ev_bufs], stream="D2")
    ev_bufbf = p.emit(VE, lambda e: e.tensor_copy(buf_bf[:, :], buf_sb[:, :]),
                      waits=[ld1, warm_last])
    DBG("buf_bf", buf_bf[:, :], [ev_bufbf])

    scope_QW = track(ExitStack())
    sbq = lambda name, shape, dt: scope_QW.enter_context(
        nc.sbuf_tensor(name, shape, dt, side="right"))
    S3 = 3
    qw_ld = [sbq(f"qwld{r}", [128, 1024], F32) for r in range(S3)]
    qru = [sbq(f"qru{r}", [128, 1024], F32) for r in range(S3)]
    qri = [sbq(f"qri{r}", [128, 1024], I32) for r in range(S3)]
    qrf = [sbq(f"qrf{r}", [128, 1024], BF16) for r in range(S3)]
    qrth = [sbq(f"qrth{r}", [128, 1024], BF16) for r in range(S3)]
    qrr = [sbq(f"qrr{r}", [128, 1024], F32) for r in range(S3)]
    qwib = sbq("qwib", [128, I_FF], BF16)
    qstage = [sbq(f"qstage{r}", [128, 1024], BF16) for r in (0, 1)]
    qsc = sbq("qsc", [128, 8], F32)
    qst = sbq("qst", [128, 20], F32)
    qwld_d = sbq("qwld_d", [128, I_FF], F32)
    qwdT = sbq("qwdT", [128, 22 * 128], BF16)

    QR = 9
    qrev = [[None] * 6 for _ in range(QR)]
    qld_ev = [None] * QR
    qch_ev = [None] * QR
    qest_ev = [None] * QR
    qsc_evs = []
    stage_dmas = []
    qstage_rel = [None, None]

    def q_meta(r):
        if r < 6:
            kind = "g" if r < 3 else "u"
            s3 = r % 3
            return kind, s3, 0, 1024
        lo, n = ((0, 1024), (1024, 1024), (2048, 768))[r - 6]
        return "d", r - 6, lo, n

    def q_ld(r):
        w = [last_bufmm, ev_bufs]
        if r >= S3:
            w.append(qrev[r - S3][0])
        kind, s3, _, _ = q_meta(r)
        srcd = d_wgm if kind == "g" else d_wum
        qld_ev[r] = dma(SY, qw_ld[r % S3][:, :], srcd[128 * s3:128 * (s3 + 1), :],
                        waits=w, stream="D2")

    def q_chain(r):
        kind, s3, lo, n = q_meta(r)
        if kind == "d":
            if r != 6:
                qch_ev[r] = qch_ev[6]
                return
            e1 = p.emit(VE, lambda e: e.reduce_max(qst[:, 10:11], qwld_d[:, :], axis=AX,
                                                   apply_absolute_value=True),
                        waits=[qld_ev[6]])
            qsc_evs.append(p.emit(VE, lambda e: e.tensor_scalar(qsc[:, 6:7], qst[:, 10:11],
                                                                1.0 / 127.0, None, ALU.mult),
                                  waits=[e1]))
            e3 = p.emit(VE, lambda e: e.reciprocal(qst[:, 11:12], qst[:, 10:11]), waits=[e1])
            qch_ev[r] = p.emit(VE, lambda e: e.tensor_scalar(qst[:, 11:12], qst[:, 11:12],
                                                             127.0, None, ALU.mult),
                               waits=[e3])
            return
        sc = (0 if kind == "g" else 3) + s3
        e1 = p.emit(VE, lambda e, sl=r % S3: e.reduce_max(qst[:, r:r + 1], qw_ld[sl][:, :],
                                                          axis=AX, apply_absolute_value=True),
                    waits=[qld_ev[r]])
        qsc_evs.append(p.emit(VE, lambda e, c=sc: e.tensor_scalar(qsc[:, c:c + 1],
                                                                  qst[:, r:r + 1],
                                                                  1.0 / 127.0, None, ALU.mult),
                              waits=[e1]))
        e3 = p.emit(VE, lambda e: e.reciprocal(qst[:, 12 + r:13 + r], qst[:, r:r + 1]),
                    waits=[e1])
        qch_ev[r] = p.emit(VE, lambda e: e.tensor_scalar(qst[:, 12 + r:13 + r],
                                                         qst[:, 12 + r:13 + r], 127.0,
                                                         None, ALU.mult),
                           waits=[e3])

    qwinv = [None] * QR

    def q_st(r, s):
        kind, s3, lo, n = q_meta(r)
        sl = r % S3
        u = qru[sl][:, 0:n]
        i_ = qri[sl][:, 0:n]
        nf = qrf[sl][:, 0:n]
        th = qrth[sl][:, 0:n]
        r1 = qrr[sl][:, 0:n]
        rel = lambda st2: (qrev[r - S3][st2] if r >= S3 else None)
        if s == 0:
            src = qw_ld[sl][:, :] if kind != "d" else qwld_d[:, lo:lo + n]
            scol = qst[:, 12 + r:13 + r] if kind != "d" else qst[:, 11:12]
            return p.emit(SE, lambda e: e.activation(u, src, AF.Identity,
                                                     bias=cst[:, 3:4], scale=scol),
                          waits=[qch_ev[r], ev_cst, rel(2)])
        if s == 1:
            return p.emit(G, lambda e: e.tensor_copy(i_, u), waits=[qrev[r][0], rel(4)])
        if s == 2:
            return p.emit(VE, lambda e: e.scalar_tensor_tensor(nf, i_, 1.0, u,
                                                               ALU.mult, ALU.subtract),
                          waits=[qrev[r][1], rel(3)])
        if s == 3:
            return p.emit(SE, lambda e: e.activation(th, nf, AF.Tanh, bias=0.0, scale=TBIG),
                          waits=[qrev[r][2]])
        if s == 4:
            return p.emit(VE, lambda e: e.scalar_tensor_tensor(r1, th, -0.5, i_,
                                                               ALU.mult, ALU.add),
                          waits=[qrev[r][3], rel(5)])
        w5 = [qrev[r][4]]
        if r >= 1 and (r < 6 or r == 6):
            w5.append(qest_ev[r - 1] if r < 6 else qest_ev[5])
        return p.emit(SE, lambda e: e.activation(qwib[:, lo:lo + n], r1, AF.Identity,
                                                 bias=cst[:, 4:5], scale=1.0),
                      waits=w5)

    def q_trs(r):
        kind, s3, _, _ = q_meta(r)
        trs = None
        for kc in range(NC):
            trs = p.emit(TE, lambda e, kk=kc:
                         e.transpose(tbank[0][:, 128 * kk:128 * (kk + 1)],
                                     qwib[:, 128 * kk:128 * (kk + 1)], idb[:, :]),
                         waits=[qrev[r][5], ld_id, brd.get(6) if kc == 0 else None],
                         inc=(kc == NC - 1))
        bwr[6] = trs
        est = p.emit(VE, lambda e, sl=r % 2: e.tensor_copy(qstage[sl][:, :], tbank[0][:, :]),
                     waits=[trs, qstage_rel[r % 2]])
        brd[6] = est
        qest_ev[r] = est
        col0 = 1024 * s3 if kind == "g" else 3072 + 1024 * s3
        sd = dma(SY, d_wqi[:, col0:col0 + 1024], qstage[r % 2][:, :],
                 waits=[est], stream="D3")
        stage_dmas.append(sd)
        qstage_rel[r % 2] = sd

    for r0 in range(min(S3, QR)):
        q_ld(r0)
    ldwd_ev = dma(SY, qwld_d[:, :], d_wdm[:, :],
                  waits=[last_bufmm, ev_bufs], stream="D1")
    qld_ev[6] = ldwd_ev
    qld_ev[7] = ldwd_ev
    qld_ev[8] = ldwd_ev
    for t in range(QR + 6):
        if 0 <= t - 6 < 6:
            q_trs(t - 6)
        if t < QR:
            q_chain(t)
        for s in range(5, -1, -1):
            r = t - s
            if 0 <= r < QR:
                qrev[r][s] = q_st(r, s)
        if t + S3 < 6:
            q_ld(t + S3)
    wq_done = qrev[8][5]
    for bt in range(3):
        ics = list(range(8 * bt, min(8 * (bt + 1), NI)))
        trs = None
        for idx, ic in enumerate(ics):
            trs = p.emit(TE, lambda e, c2=ic, ix=idx:
                         e.transpose(tbank[0][:, 128 * ix:128 * (ix + 1)],
                                     qwib[:, 128 * c2:128 * (c2 + 1)], idb[:, :]),
                         waits=[wq_done, ld_id, brd.get(6) if idx == 0 else None],
                         inc=(idx == len(ics) - 1))
        bwr[6] = trs
        nb = len(ics)
        ewt = p.emit(VE, lambda e, b0=8 * bt, nn=nb:
                     e.tensor_copy(qwdT[:, 128 * b0:128 * (b0 + nn)],
                                   tbank[0][:, 0:128 * nn]),
                     waits=[trs])
        brd[6] = ewt
    sd = dma(SY, d_wqi[:, 6144:8960], qwdT[:, :], waits=[ewt], stream="D3")
    stage_dmas.append(sd)
    ev_q7 = p.emit(VE, lambda e: e.memset(qsc[:, 7:8], 0.0), waits=[ev_bufs])
    sdc = dma(SY, d_sci[:, :], qsc[:, :], waits=qsc_evs + [ev_q7], stream="D3")
    qw_mem_rel = [stage_dmas[-1], sdc]
    scope_QW.close()

    scope_B0 = track(ExitStack())
    wo_sb = scope_B0.enter_context(nc.sbuf_tensor("wo_sb", [128, 8192], BF16, side="right"))
    attn_fm = scope_B0.enter_context(nc.sbuf_tensor("attn_fm", [128, 8 * T], BF16, side="right"))
    ld_wo = dma(G, wo_sb[:, :], d_wot[:, :],
                waits=[last_bufmm, ev_bufs] + qw_mem_rel)

    ev_boadj = None
    for g2 in range(2):
        mm = None
        for cc in range(NC):
            w = [ld_bv, ld_wo]
            if cc == 0:
                w.append(brd.get(4))
            mm = p.emit(TE, lambda e, c2=cc, g=g2, s=(cc == 0):
                        e.matmul(bank[4][0:1, 0:512], bvb[:, c2:c2 + 1],
                                 wo_sb[:, 1024 * c2 + 512 * g:1024 * c2 + 512 * (g + 1)],
                                 start=s, stop=(c2 == NC - 1)),
                        waits=w, inc=(cc == NC - 1))
        bwr[4] = mm
        ev_boadj = p.emit(VE, lambda e, g=g2:
                          e.scalar_tensor_tensor(bo_adj[0:1, 512 * g:512 * (g + 1)],
                                                 bank[4][0:1, 0:512], 1.0,
                                                 bo_sb[0:1, 512 * g:512 * (g + 1)],
                                                 ALU.mult, ALU.add),
                          waits=[mm, ld_bo])
        brd[4] = ev_boadj


    scope_B = track(ExitStack())
    sbr = lambda name, shape, dt: scope_B.enter_context(
        nc.sbuf_tensor(name, shape, dt, side="right"))
    qf_t2 = [[sbr(f"qf{ps}_{b}", [128, 2 * T], BF16) for b in (0, 1)] for ps in (0, 1)]
    flat_p2 = [[sbr(f"flatp{ps}_{r}", [1, T], BF16) for r in (0, 1)] for ps in (0, 1)]
    attn_pair = [sbr(f"apair{r}", [128, 128], BF16) for r in (0, 1)]
    rec_t = [sbr(f"rec{r}", [128, 2], F32) for r in (0, 1)]
    flat_rel = [None, None]
    qf_rel = [None, None]
    apair_rel = [None, None]
    ev_attn = None
    qf_done_ps = [[None, None], [None, None]]
    fls_ps = [[None, None], [None, None]]

    def emit_qf(hp):
        ps = hp % 2
        for b2 in range(2):
            h = 2 * hp + b2
            fls_ps[ps][b2] = dma(G, flat_p2[ps][b2][0:1, :], expnq[h:h + 1, :],
                                 waits=[ev_expnq, qf_rel[ps]] + qw_mem_rel)
            for mc in range(2):
                for j in range(NJ):
                    pbi = (mc * NJ + j) % 2
                    mm = p.emit(TE, lambda e, bb=b2, m=mc, jj=j, cc=hp, pbi2=pbi:
                                e.matmul(bank[pbi2][:, :],
                                         orf2x[64 * bb:64 * (bb + 1), 128 * m:128 * (m + 1)],
                                         q_sb[64 * bb:64 * (bb + 1),
                                              2048 * cc + 512 * jj:2048 * cc + 512 * (jj + 1)],
                                         start=True, stop=True),
                                waits=[ev_orf, last_q_ev, brd.get(pbi), qf_rel[ps]])
                    bwr[pbi] = mm
                    qf_done_ps[ps][b2] = p.emit(SE, lambda e, bb=b2, m=mc, jj=j, pbi2=pbi, ps2=ps:
                                                e.activation(qf_t2[ps2][bb][:, 2048 * m + 512 * jj:2048 * m + 512 * (jj + 1)],
                                                             bank[pbi2][:, :], AF.Exp,
                                                             bias=0.0, scale=1.0),
                                                waits=[mm])
                    brd[pbi] = qf_done_ps[ps][b2]

    emit_qf(0)
    ev_bufse = None
    for h in range(H):
        mm = None
        for mc in range(2):
            w = [ev_bufbf, ev_cst]
            if mc == 0:
                w.append(brd.get(4))
            mm = p.emit(TE, lambda e, hh=h, m=mc, s=(mc == 0):
                        e.matmul(bank[4][0:1, 0:65], ones_col[:, :],
                                 buf_bf[:, 130 * hh + 65 * m:130 * hh + 65 * (m + 1)],
                                 start=s, stop=(m == 1)),
                        waits=w, inc=(mc == 1))
        bwr[4] = mm
        ev_bufse = p.emit(SE, lambda e, hh=h:
                          e.activation(bufs_eps[0:1, 65 * hh:65 * (hh + 1)],
                                       bank[4][0:1, 0:65], AF.Identity,
                                       bias=0.0, scale=EPS16),
                          waits=[mm])
        brd[4] = ev_bufse

    for hp in range(8):
        pend_tr = None

        def emit_tr(ii0, div_ev):
            nonlocal ev_attn
            ar2 = ii0 % 2
            tb_i = ii0 % 2
            tr = p.emit(TE, lambda e, rr=ar2, tbi=tb_i:
                        e.transpose(tbank[tbi][:, 0:128], attn_pair[rr][:, :], idb[:, :]),
                        waits=[div_ev, ld_id, brd.get(6 + tb_i)])
            bwr[6 + tb_i] = tr
            e4 = p.emit(VE, lambda e, tbi=tb_i, cc=hp, ii=ii0:
                        e.tensor_copy(attn_fm[:, 2048 * cc + 128 * ii:2048 * cc + 128 * (ii + 1)],
                                      tbank[tbi][:, 0:128]),
                        waits=[tr])
            brd[6 + tb_i] = e4
            apair_rel[ar2] = e4
            ev_attn = e4

        if hp + 1 < 8:
            emit_qf(hp + 1)
        ps = hp % 2
        qf_done = qf_done_ps[ps]
        fls = fls_ps[ps]
        for i in range(NT):
            ob = 2 + i % 2
            od = None
            for b2 in range(2):
                h = 2 * hp + b2
                for mc in range(2):
                    w = [qf_done[b2], ev_bufbf]
                    if b2 == 0 and mc == 0:
                        w.append(brd.get(ob))
                    od = p.emit(TE, lambda e, bb=b2, m=mc, ii=i, hh=h, o=ob, s=(mc == 0), ps2=ps:
                                e.matmul(bank[o][:, 65 * bb:65 * (bb + 1)],
                                         qf_t2[ps2][bb][:, 2048 * m + 128 * ii:2048 * m + 128 * (ii + 1)],
                                         buf_bf[:, 130 * hh + 65 * m:130 * hh + 65 * (m + 1)],
                                         start=s, stop=False),
                                waits=w, inc=False)
                od = p.emit(TE, lambda e, bb=b2, ii=i, hh=h, o=ob, ps2=ps:
                            e.matmul(bank[o][:, 65 * bb:65 * (bb + 1)],
                                     flat_p2[ps2][bb][0:1, 128 * ii:128 * (ii + 1)],
                                     bufs_eps[0:1, 65 * hh:65 * (hh + 1)],
                                     start=False, stop=True),
                            waits=[fls[b2], ev_bufse], inc=(b2 == 1))
            bwr[ob] = od
            ar = i % 2
            e1 = p.emit(VE, lambda e, o=ob, rr=ar: e.reciprocal(rec_t[rr][:, 0:1], bank[o][:, 64:65]),
                        waits=[od, apair_rel[ar]])
            e1b = p.emit(VE, lambda e, o=ob, rr=ar: e.reciprocal(rec_t[rr][:, 1:2], bank[o][:, 129:130]),
                         waits=[od])
            e2 = p.emit(VE, lambda e, o=ob, rr=ar:
                        e.tensor_scalar(attn_pair[rr][:, 0:64], bank[o][:, 0:64],
                                        rec_t[rr][:, 0:1], None, ALU.mult),
                        waits=[e1])
            e3 = p.emit(VE, lambda e, o=ob, rr=ar:
                        e.tensor_scalar(attn_pair[rr][:, 64:128], bank[o][:, 65:129],
                                        rec_t[rr][:, 1:2], None, ALU.mult),
                        waits=[e1b])
            brd[ob] = e3
            if pend_tr is not None:
                emit_tr(*pend_tr)
            pend_tr = (i, e3)
        emit_tr(*pend_tr)
        qf_rel[0] = ev_attn
        qf_rel[1] = ev_attn
        flat_rel[0] = ev_attn
        flat_rel[1] = ev_attn
    DBG("attn_fm", attn_fm[:, :], [ev_attn])

    if upto == "B":
        _finish(p, nc, es, sems, inner_scopes)
        return nc

    scope_B.close()
    scope_Q.close()

    cc_w = p.emit(G, lambda e: e.collective_compute(
        "AllGather", ALU.bypass,
        replica_groups=[[0, 1, 2, 3, 4, 5, 6, 7]],
        ins=[h_wqi.ap()], outs=[h_wqo.ap()]),
        waits=stage_dmas + [sdc], sem="CC")
    cc_s = p.emit(G, lambda e: e.collective_compute(
        "AllGather", ALU.bypass,
        replica_groups=[[0, 1, 2, 3, 4, 5, 6, 7]],
        ins=[h_sci.ap()], outs=[h_sco.ap()]),
        waits=[sdc], sem="CC")
    sc_evs = []
    for s3 in range(3):
        cnt = 8 if s3 < 2 else 6
        sc_evs.append(p.emit(G, lambda e, s4=s3, c4=cnt: e.dma_start(
            out=swg_all[:, 8 * s4:8 * s4 + c4],
            in_=d_sco[0:128 * c4, s4:s4 + 1].rearrange("(c p) one -> p (c one)", p=128),
            allow_slow_non_contiguous=True),
            waits=[cc_s], dma=True, sem="DG"))
        sc_evs.append(p.emit(G, lambda e, s4=s3, c4=cnt: e.dma_start(
            out=swu_all[:, 8 * s4:8 * s4 + c4],
            in_=d_sco[0:128 * c4, 3 + s4:4 + s4].rearrange("(c p) one -> p (c one)", p=128),
            allow_slow_non_contiguous=True),
            waits=[cc_s], dma=True, sem="DG"))
    sc_evs.append(p.emit(G, lambda e: e.dma_start(
        out=swd_all[:, :],
        in_=d_sco[:, 6:7].rearrange("(c p) one -> p (c one)", p=128),
        allow_slow_non_contiguous=True),
        waits=[cc_s], dma=True, sem="DG"))
    ev_scales = sc_evs[-1]
    ev_swd = ev_scales

    scope_L = track(ExitStack())
    xs_t = [scope_L.enter_context(nc.sbuf_tensor(f"xs{r}", [128, 1024], F32, side="right"))
            for r in (0, 1)]
    x2_t = [scope_L.enter_context(nc.sbuf_tensor(f"x2t{r}", [128, 1024], F32, side="right"))
            for r in (0, 1)]
    xs_rel = [None, None]
    r_rel = [None, None]
    x2_rel = [(), ()]
    st_last2 = [None, None]
    last_stat_ev = [None, None]
    ev_ln1_dma = None
    ev_ln1 = None
    y1b_cycle = ((4, 5), (0, 1))

    for i in range(NT):
        rr = i % 2
        ldx = dma(SY, xs_t[rr][:, :], d_xtm[128 * i:128 * (i + 1), :],
                  waits=[xs_rel[rr], ev_attn if i < 2 else None],
                  stream=("D1" if rr == 0 else "D2"))
        y_banks = y1b_cycle[i % 2]
        add_evs = []
        for g2 in range(2):
            bidx = y_banks[g2]
            mm = None
            for cc in range(NC):
                st = cc == 0
                w = [ld_wo, ev_attn]
                if st:
                    w.append(brd.get(bidx))
                mm = p.emit(TE, lambda e, b3=bidx, c2=cc, g=g2, ii=i, s=st:
                            e.matmul(bank[b3][:, :],
                                     attn_fm[:, 2048 * c2 + 128 * ii:2048 * c2 + 128 * (ii + 1)],
                                     wo_sb[:, 1024 * c2 + 512 * g:1024 * c2 + 512 * (g + 1)],
                                     start=s, stop=False),
                            waits=w, inc=False)
            mm = p.emit(TE, lambda e, b3=bidx, g=g2:
                        e.matmul(bank[b3][:, :], ones_row[0:1, 0:128],
                                 bo_adj[0:1, 512 * g:512 * (g + 1)],
                                 start=False, stop=True),
                        waits=[ev_boadj])
            bwr[bidx] = mm
            e1 = p.emit(VE, lambda e, b3=bidx, g=g2, r2=rr:
                        e.tensor_tensor(r_t[r2][:, 512 * g:512 * (g + 1)],
                                        bank[b3][:, :],
                                        xs_t[r2][:, 512 * g:512 * (g + 1)], ALU.add),
                        waits=[mm, ldx, r_rel[rr]])
            brd[bidx] = e1
            add_evs.append(e1)
        xs_rel[rr] = add_evs[-1]
        sb = 8 * rr
        s1 = p.emit(VE, lambda e, r2=rr, c0=sb: e.reduce_sum(st_t[:, c0:c0 + 1],
                                                             r_t[r2][:, :], axis=AX),
                    waits=add_evs + [last_stat_ev[rr]])
        sq = p.emit(SE, lambda e, r2=rr: e.activation(rsq_t[r2][:, :], r_t[r2][:, :],
                                                      AF.Square, bias=0.0, scale=1.0),
                    waits=add_evs)
        s2 = p.emit(VE, lambda e, r2=rr, c0=sb: e.reduce_sum(st_t[:, c0 + 1:c0 + 2],
                                                             rsq_t[r2][:, :], axis=AX),
                    waits=[sq])
        m1 = p.emit(VE, lambda e, c0=sb: e.tensor_scalar(st_t[:, c0 + 2:c0 + 3],
                                                         st_t[:, c0:c0 + 1],
                                                         1.0 / 1024.0, None, ALU.mult),
                    waits=[s1])
        m2 = p.emit(VE, lambda e, c0=sb: e.tensor_scalar(st_t[:, c0 + 3:c0 + 4],
                                                         st_t[:, c0 + 1:c0 + 2],
                                                         1.0 / 1024.0, None, ALU.mult),
                    waits=[s2])
        m3 = p.emit(VE, lambda e, c0=sb: e.tensor_tensor(st_t[:, c0 + 4:c0 + 5],
                                                         st_t[:, c0 + 2:c0 + 3],
                                                         st_t[:, c0 + 2:c0 + 3], ALU.mult),
                    waits=[m1])
        m4 = p.emit(VE, lambda e, c0=sb: e.tensor_tensor(st_t[:, c0 + 5:c0 + 6],
                                                         st_t[:, c0 + 3:c0 + 4],
                                                         st_t[:, c0 + 4:c0 + 5], ALU.subtract),
                    waits=[m2, m3])
        l1 = p.emit(SE, lambda e, c0=sb: e.activation(st_t[:, c0 + 6:c0 + 7],
                                                      st_t[:, c0 + 5:c0 + 6], AF.Ln,
                                                      bias=cst[:, 2:3], scale=1.0),
                    waits=[m4, ev_cst])
        l2 = p.emit(SE, lambda e, c0=sb: e.activation(st_t[:, c0 + 7:c0 + 8],
                                                      st_t[:, c0 + 6:c0 + 7], AF.Exp,
                                                      bias=0.0, scale=-0.5),
                    waits=[l1])
        mrs = p.emit(VE, lambda e, c0=sb: e.scalar_tensor_tensor(st_t[:, c0 + 4:c0 + 5],
                                                                 st_t[:, c0 + 2:c0 + 3],
                                                                 -1.0, st_t[:, c0 + 7:c0 + 8],
                                                                 ALU.mult, ALU.mult),
                     waits=[l2])
        ap1 = p.emit(SE, lambda e, r2=rr, c0=sb: e.activation(x2_t[r2][:, :], r_t[r2][:, :],
                                                              AF.Identity,
                                                              bias=st_t[:, c0 + 4:c0 + 5],
                                                              scale=st_t[:, c0 + 7:c0 + 8]),
                     waits=[mrs, s2] + list(x2_rel[rr]))
        last_stat_ev[rr] = ap1
        r_rel[rr] = ap1
        stx = dma(SY, d_x2[128 * i:128 * (i + 1), :], x2_t[rr][:, :], waits=[ap1],
                  stream=("D3" if rr == 0 else "D4"))
        mx = p.emit(VE, lambda e, r2=rr, ii=i:
                    e.reduce_max(runmax[:, ii:ii + 1], x2_t[r2][:, :], axis=AX,
                                 apply_absolute_value=True),
                    waits=[ap1])
        x2_rel[rr] = (stx, mx)
        st_last2[rr] = stx
        ev_ln1 = mx
    ev_ln1_dma = st_last2[1]
    ev_ln1_dma2 = st_last2[0]
    wa_last = None
    for wi in range(350):
        wa_last = p.emit(TE, lambda e: e.matmul(bank[5][:, :], ones_row[0:1, 0:128],
                                                ones_row[0:1, 0:512],
                                                start=True, stop=True),
                         waits=([brd.get(5), ev_ln1] if wi == 0 else []),
                         inc=(wi == 349))
    brd[5] = wa_last

    if "x2" in dbg:
        dx2o = nc.dram_tensor("dbg_x2", [T, 1024], F32, kind="ExternalOutput").ap()
        dbg_evs.append(dma(G, dx2o[:, :], d_x2[:, :], waits=[ev_ln1_dma]))

    rm1 = p.emit(VE, lambda e: e.reduce_max(rmx1[:, 0:1], runmax[:, :], axis=AX),
                 waits=[ev_ln1])
    sm1 = dma(SY, d_mx[:], rmx1[:, 0], waits=[rm1])
    lm1 = dma(SY, mrow[0:1, :], d_mx[None, :], waits=[sm1])
    rm2 = p.emit(VE, lambda e: e.reduce_max(scal1[0:1, 4:5], mrow[0:1, :], axis=AX),
                 waits=[lm1])
    sm2 = dma(SY, h_ccmi.ap()[None, :], scal1[0:1, 4:5], waits=[rm2])
    cc2 = p.emit(G, lambda e: e.collective_compute(
        "AllReduce", ALU.max,
        replica_groups=[[0, 1, 2, 3, 4, 5, 6, 7]],
        ins=[h_ccmi.ap()], outs=[h_ccmo.ap()]),
        waits=[sm2], sem="CC")
    lm2 = dma(SY, scal1[0:1, 5:6], h_ccmo.ap()[None, :], waits=[cc2])
    t1 = p.emit(VE, lambda e: e.tensor_tensor(scal1[0:1, 6:7], scal1[0:1, 5:6],
                                              sig_s[0:1, 0:1], ALU.mult),
                waits=[lm2, ev_sig])
    t2 = p.emit(VE, lambda e: e.tensor_scalar(scal1[0:1, 0:1], scal1[0:1, 6:7],
                                              1.0 / 127.0, None, ALU.mult),
                waits=[t1])
    t3 = p.emit(VE, lambda e: e.reciprocal(scal1[0:1, 1:2], scal1[0:1, 0:1]),
                waits=[t2])
    sm3 = dma(SY, d_scal[None, 0:2], scal1[0:1, 0:2], waits=[t3])
    bc1 = dma(SY, scal[:, 1:2], d_scal[None, 0:1].partition_broadcast(128), waits=[sm3])
    bc2 = dma(SY, scal[:, 0:1], d_scal[None, 1:2].partition_broadcast(128), waits=[sm3])
    ev_sx = bc1
    ev_sxinv = bc2
    if "scals" in dbg:
        dsc = nc.dram_tensor("dbg_scals", [1, 8], F32, kind="ExternalOutput").ap()
        dbg_evs.append(dma(SY, dsc[:, :], scal1[0:1, :], waits=[t3, rm2, lm2]))

    if upto == "LN1":
        _finish(p, nc, es, sems, inner_scopes)
        return nc

    scope_L.close()
    scope_B0.close()
    scope_R0.close()

    scope_F = track(ExitStack())
    sbf = lambda name, shape, dt: scope_F.enter_context(
        nc.sbuf_tensor(name, shape, dt, side="right"))
    xint_fm = sbf("xint_fm", [128, 8 * T], BF16)
    NR = 4
    ru_t = [sbf(f"ru{r}", [128, 1024], F32) for r in range(NR)]
    ri_t = [sbf(f"ri{r}", [128, 1024], I32) for r in range(NR)]
    rf_t = [sbf(f"rf{r}", [128, 1024], BF16) for r in range(NR)]
    rth_t = [sbf(f"rth{r}", [128, 1024], BF16) for r in range(NR)]
    rr1_t = [sbf(f"rr1{r}", [128, 1024], F32) for r in range(NR)]
    xq16 = [sbf(f"xq16{r}", [128, 1024], F32) for r in range(4)]
    xib = [sbf(f"xib{r}", [128, 1024], BF16) for r in (0, 1)]
    xb16a = [sbf(f"xb16a{r}", [128, 1024], BF16) for r in range(NR)]
    wgT = [sbf(f"wgT{r}", [128, 1024], BF16) for r in (0, 1, 2)]
    wuT = [sbf(f"wuT{r}", [128, 1024], BF16) for r in (0, 1, 2)]
    silu_t = [sbf(f"silu{r}", [128, 512], F32) for r in (0, 1)]
    hstg = [sbf(f"hstg{r}", [128, 2048], F32) for r in (0, 1)]
    NXR = NT
    xrev = [[None] * 9 for _ in range(NXR)]
    xld_ev = [None] * NXR
    xint_jev = [None] * NJ

    def x_ld(r):
        w = [ev_ln1_dma, ev_ln1_dma2]
        if r >= NR:
            w.append(xrev[r - NR][0])
        xld_ev[r] = dma(SY, xq16[r % NR][:, :], d_x2[128 * r:128 * (r + 1), :],
                        waits=w, stream=("D1" if r % 2 == 0 else "D4"))

    def x_st(r, s):
        sl = r % NR
        tb2 = r % 2
        u, i_, nf, th, r1 = ru_t[sl], ri_t[sl], rf_t[sl], rth_t[sl], rr1_t[sl]
        rel = lambda st2: (xrev[r - NR][st2] if r >= NR else None)
        if s == 0:
            return p.emit(SE, lambda e: e.activation(u[:, :], xq16[sl][:, :], AF.Identity,
                                                     bias=cst[:, 3:4], scale=scal[:, 0:1]),
                          waits=[xld_ev[r], ev_sx, ev_sxinv, ev_cst, rel(2)])
        if s == 1:
            return p.emit(G, lambda e: e.tensor_copy(i_[:, :], u[:, :]),
                          waits=[xrev[r][0], rel(4)])
        if s == 2:
            return p.emit(VE, lambda e: e.scalar_tensor_tensor(nf[:, :], i_[:, :], 1.0,
                                                               u[:, :], ALU.mult, ALU.subtract),
                          waits=[xrev[r][1], rel(3)])
        if s == 3:
            return p.emit(SE, lambda e: e.activation(th[:, :], nf[:, :], AF.Tanh,
                                                     bias=0.0, scale=TBIG),
                          waits=[xrev[r][2]])
        if s == 4:
            return p.emit(VE, lambda e: e.scalar_tensor_tensor(r1[:, :], th[:, :], -0.5,
                                                               i_[:, :], ALU.mult, ALU.add),
                          waits=[xrev[r][3], rel(5)])
        if s == 5:
            return p.emit(SE, lambda e: e.activation(xb16a[sl][:, :], r1[:, :], AF.Identity,
                                                     bias=cst[:, 4:5], scale=1.0),
                          waits=[xrev[r][4], rel(6)])
        if s == 6:
            w = [xrev[r][5]]
            if r >= 2:
                w.append(xrev[r - 2][7])
            return p.emit(VE, lambda e: e.tensor_scalar(xib[tb2][:, :], xb16a[sl][:, :],
                                                        127.0, None, ALU.min),
                          waits=w)
        if s == 7:
            trs = None
            for kc in range(NC):
                trs = p.emit(TE, lambda e, kk=kc:
                             e.transpose(tbank[tb2][:, 128 * kk:128 * (kk + 1)],
                                         xib[tb2][:, 128 * kk:128 * (kk + 1)], idb[:, :]),
                             waits=[xrev[r][6], ld_id,
                                    brd.get(6 + tb2) if kc == 0 else None],
                             inc=(kc == NC - 1))
            bwr[6 + tb2] = trs
            return trs
        ev = p.emit(VE, lambda e:
                    e.tensor_scalar(
                        xint_fm[:, :].rearrange("p (kc t) -> p kc t", t=2048)[:, :, 128 * r:128 * (r + 1)],
                        tbank[tb2][:, :].rearrange("p (kc t) -> p kc t", t=128),
                        -128.0, None, ALU.max),
                    waits=[xrev[r][7]])
        brd[6 + tb2] = ev
        return ev

    for r0 in range(min(NR, NXR)):
        x_ld(r0)
    for t in range(NXR + 9):
        for s in range(8, -1, -1):
            r = t - s
            if 0 <= r < NXR:
                xrev[r][s] = x_st(r, s)
        if t + NR < NXR:
            x_ld(t + NR)
    for j in range(NJ):
        xint_jev[j] = xrev[4 * j + 3][8]
    ev_xint = xrev[NXR - 1][8]
    DBG("xint_fm", xint_fm[:, :], [ev_xint])

    wgT_rel = [None, None, None]
    wuT_rel = [None, None, None]
    silu_rel = [None, None]
    hstg_rel = [None, None]
    ev_h = None
    ev_hstore = None

    e_sg = p.emit(VE, lambda e: e.tensor_scalar(sxswg[:, :], swg_all[:, :],
                                                scal[:, 1:2], None, ALU.mult),
                  waits=[ev_scales, ev_sx])
    e_su = p.emit(VE, lambda e: e.tensor_scalar(sxswu[:, :], swu_all[:, :],
                                                scal[:, 1:2], None, ALU.mult),
                  waits=[ev_scales, ev_sx])

    for ic in range(NI):
        slot = ic % 3
        cr = ic % 8
        sl3 = ic // 8
        ldg = dma(SY, wgT[slot][:, :],
                  d_wqo[128 * cr:128 * (cr + 1), 1024 * sl3:1024 * (sl3 + 1)],
                  waits=[cc_w, wgT_rel[slot]], stream="D2")
        ldu = dma(SY, wuT[slot][:, :],
                  d_wqo[128 * cr:128 * (cr + 1), 3072 + 1024 * sl3:3072 + 1024 * (sl3 + 1)],
                  waits=[cc_w, wuT_rel[slot]], stream="D3")
        h_tile_evs = []
        gmm = None
        umm = None
        for j in range(NJ):
            gb = (2 * (ic * NJ + j)) % 4
            ub = gb + 1
            for kc in range(NC):
                st = kc == 0
                w = [ldg, xint_jev[j]]
                if st:
                    w.append(brd.get(gb))
                gmm = p.emit(TE, lambda e, b3=gb, kk=kc, jj=j, sl=slot, s=st:
                             e.matmul(bank[b3][:, :],
                                      wgT[sl][:, 128 * kk:128 * (kk + 1)],
                                      xint_fm[:, 2048 * kk + 512 * jj:2048 * kk + 512 * (jj + 1)],
                                      start=s, stop=(kk == NC - 1)),
                             waits=w, inc=(kc == NC - 1))
            bwr[gb] = gmm
            for kc in range(NC):
                st = kc == 0
                w = [ldu, xint_jev[j]]
                if st:
                    w.append(brd.get(ub))
                umm = p.emit(TE, lambda e, b3=ub, kk=kc, jj=j, sl=slot, s=st:
                             e.matmul(bank[b3][:, :],
                                      wuT[sl][:, 128 * kk:128 * (kk + 1)],
                                      xint_fm[:, 2048 * kk + 512 * jj:2048 * kk + 512 * (jj + 1)],
                                      start=s, stop=(kk == NC - 1)),
                             waits=w, inc=(kc == NC - 1))
            bwr[ub] = umm
            sr = j % 2
            es1 = p.emit(SE, lambda e, b3=gb, c2=ic, sl2=sr:
                         e.activation(silu_t[sl2][:, :], bank[b3][:, :], AF.Silu,
                                      bias=0.0, scale=sxswg[:, c2:c2 + 1]),
                         waits=[gmm, e_sg, silu_rel[sr]])
            brd[gb] = es1
            hsl = ic % 2
            eh = p.emit(VE, lambda e, b3=ub, c2=ic, jj=j, sl2=sr, hs=hsl:
                        e.scalar_tensor_tensor(hstg[hs][:, 512 * jj:512 * (jj + 1)],
                                               bank[b3][:, :], sxswu[:, c2:c2 + 1],
                                               silu_t[sl2][:, :], ALU.mult, ALU.mult),
                        waits=[umm, es1, e_su] + ([hstg_rel[hsl]] if j == 0 else []))
            brd[ub] = eh
            silu_rel[sr] = eh
            h_tile_evs.append(eh)
            emx = p.emit(VE, lambda e, c2=ic, jj=j, hs=hsl:
                         e.reduce_max(runmax2[:, 4 * c2 + jj:4 * c2 + jj + 1],
                                      hstg[hs][:, 512 * jj:512 * (jj + 1)],
                                      axis=AX, apply_absolute_value=True),
                         waits=[eh])
            ev_h = emx
        ev_hstore = dma(SY, d_h[128 * ic:128 * (ic + 1), :], hstg[ic % 2][:, :],
                        waits=h_tile_evs + [ev_h], stream="D3")
        hstg_rel[ic % 2] = ev_hstore
        wgT_rel[slot] = gmm
        wuT_rel[slot] = umm
    if "hd" in dbg:
        dho = nc.dram_tensor("dbg_hd", [NI * 128, T], F32, kind="ExternalOutput").ap()
        dbg_evs.append(dma(G, dho[:, :], d_h[:, :], waits=[ev_hstore]))

    if upto == "GATE":
        _finish(p, nc, es, sems, inner_scopes)
        return nc

    rm1b = p.emit(VE, lambda e: e.reduce_max(rmx1[:, 1:2], runmax2[:, :], axis=AX),
                  waits=[ev_h])
    sm1b = dma(SY, d_mx2[:], rmx1[:, 1], waits=[rm1b])
    lm1b = dma(SY, mrow[0:1, :], d_mx2[None, :], waits=[sm1b, rm2])
    rm2b = p.emit(VE, lambda e: e.reduce_max(scal1[0:1, 4:5], mrow[0:1, :], axis=AX),
                  waits=[lm1b, t1, sm2])
    wb_last = None
    for wi in range(500):
        wb_last = p.emit(TE, lambda e: e.matmul(bank[5][:, :], ones_row[0:1, 0:128],
                                                ones_row[0:1, 0:512],
                                                start=True, stop=True),
                         waits=([brd.get(5), ev_h] if wi == 0 else []),
                         inc=(wi == 499))
    brd[5] = wb_last

    sm2b = dma(SY, h_ccmi2.ap()[None, :], scal1[0:1, 4:5], waits=[rm2b])
    cc3 = p.emit(G, lambda e: e.collective_compute(
        "AllReduce", ALU.max,
        replica_groups=[[0, 1, 2, 3, 4, 5, 6, 7]],
        ins=[h_ccmi2.ap()], outs=[h_ccmo2.ap()]),
        waits=[sm2b], sem="CC")
    lm2b = dma(SY, scal1[0:1, 5:6], h_ccmo2.ap()[None, :], waits=[cc3, t1])
    t1b = p.emit(VE, lambda e: e.tensor_tensor(scal1[0:1, 6:7], scal1[0:1, 5:6],
                                               sig_s[0:1, 1:2], ALU.mult),
                 waits=[lm2b, ev_sig, t2])
    t2b = p.emit(VE, lambda e: e.tensor_scalar(scal1[0:1, 2:3], scal1[0:1, 6:7],
                                               1.0 / 127.0, None, ALU.mult),
                 waits=[t1b])
    t3b = p.emit(VE, lambda e: e.reciprocal(scal1[0:1, 3:4], scal1[0:1, 2:3]),
                 waits=[t2b])
    sm3b = dma(SY, d_scal[None, 2:4], scal1[0:1, 2:4], waits=[t3b])
    bc3 = dma(SY, scal[:, 3:4], d_scal[None, 2:3].partition_broadcast(128), waits=[sm3b])
    bc4 = dma(SY, scal[:, 2:3], d_scal[None, 3:4].partition_broadcast(128), waits=[sm3b])
    ev_sh = bc3
    ev_shinv = bc4

    scope_F.close()

    scope_D = track(ExitStack())
    sbd = lambda name, shape, dt: scope_D.enter_context(
        nc.sbuf_tensor(name, shape, dt, side="right"))
    NDR = 4
    wdT_all = sbd("wdT_all", [128, NI * 1024], BF16)
    swd_row = sbd("swd_row", [128, 1024], F32)
    hint_tb = [sbd(f"hint_tb{r}", [128, NI * 512], BF16) for r in (0, 1)]
    du_t = [sbd(f"du{r}", [128, 512], F32) for r in range(NDR)]
    di_t = [sbd(f"di{r}", [128, 512], I32) for r in range(NDR)]
    df_t = [sbd(f"dfr{r}", [128, 512], BF16) for r in range(NDR)]
    dth_t = [sbd(f"dth{r}", [128, 512], BF16) for r in range(NDR)]
    dr1_t = [sbd(f"dr1{r}", [128, 512], F32) for r in range(NDR)]
    hsl_t = [sbd(f"hsl{r}", [128, 512], F32) for r in range(NDR)]
    hb16a = [sbd(f"hb16a{r}", [128, 512], BF16) for r in range(NDR)]
    hb16b = [sbd(f"hb16b{r}", [128, 512], BF16) for r in range(NDR)]
    y2s4 = [sbd(f"y2s{r}", [128, 1024], F32) for r in range(4)]
    x216 = [sbd(f"x216{r}", [128, 1024], F32) for r in range(4)]
    out_t = [sbd(f"outt{r}", [128, 1024], F32) for r in (0, 1)]

    smd = p.emit(SY, lambda e: e.dma_start(out=d_swd[:].rearrange("(c q) -> q c", q=128),
                                           in_=swd_all[:, :],
                                           allow_slow_non_contiguous=True),
                 waits=[ev_swd], dma=True, sem="D2")
    ev_wdT = None
    for cr in range(8):
        ev_wdT = dma(SY, wdT_all[:, :].rearrange("p (ic d) -> p ic d", d=1024)[:, :, 128 * cr:128 * (cr + 1)],
                     d_wqo[128 * cr:128 * (cr + 1), 6144:8960].rearrange("p (ic d) -> p ic d", d=128),
                     waits=[cc_w, ev_h, ev_hstore], stream="D2")
    DBG("wdT_all", wdT_all[:, :], [ev_wdT])
    bcd = dma(SY, swd_row[:, :], d_swd[None, :].partition_broadcast(128),
              waits=[smd, ev_hstore], stream="D2")
    ev_swdrow = bcd

    NRND = 4 * NI
    rev = [[None] * 8 for _ in range(NRND)]
    ld_ev = [None] * NRND
    dn_last = [None] * 4
    dmm_ev = [[None] * 8 for _ in range(4)]
    ey_ev = [[None] * 8 for _ in range(4)]
    x2_ld = [[None] * 4 for _ in range(4)]
    er_ev = [[None] * 4 for _ in range(4)]
    out_rel = [None, None]
    r2_rel = r_rel
    last_stat2 = [None]
    BPAIRS = ((0, 1), (2, 3), (4, 5))

    def emit_ld(r):
        tb, ic = r // NI, r % NI
        w = [ev_hstore]
        if r >= NDR:
            w.append(rev[r - NDR][0])
        ld_ev[r] = dma(SY, hsl_t[r % NDR][:, :],
                       d_h[128 * ic:128 * (ic + 1), 512 * tb:512 * (tb + 1)],
                       waits=w, stream=("D1", "D2", "D3", "D4")[r % 4])

    def st_h(r, s):
        tb, ic = r // NI, r % NI
        sl = r % NDR
        u, i_, nf, th, r1 = du_t[sl], di_t[sl], df_t[sl], dth_t[sl], dr1_t[sl]
        rel = lambda st2: (rev[r - NDR][st2] if r >= NDR else None)
        if s == 0:
            return p.emit(SE, lambda e: e.activation(u[:, :], hsl_t[sl][:, :], AF.Identity,
                                                     bias=cst[:, 3:4], scale=scal[:, 2:3]),
                          waits=[ld_ev[r], ev_sh, ev_shinv, rel(2)])
        if s == 1:
            return p.emit(G, lambda e: e.tensor_copy(i_[:, :], u[:, :]),
                          waits=[rev[r][0], rel(4)])
        if s == 2:
            return p.emit(VE, lambda e: e.scalar_tensor_tensor(nf[:, :], i_[:, :], 1.0,
                                                               u[:, :], ALU.mult, ALU.subtract),
                          waits=[rev[r][1], rel(3)])
        if s == 3:
            return p.emit(SE, lambda e: e.activation(th[:, :], nf[:, :], AF.Tanh,
                                                     bias=0.0, scale=TBIG),
                          waits=[rev[r][2]])
        if s == 4:
            return p.emit(VE, lambda e: e.scalar_tensor_tensor(r1[:, :], th[:, :], -0.5,
                                                               i_[:, :], ALU.mult, ALU.add),
                          waits=[rev[r][3], rel(5)])
        if s == 5:
            return p.emit(SE, lambda e: e.activation(hb16a[sl][:, :], r1[:, :], AF.Identity,
                                                     bias=cst[:, 4:5], scale=1.0),
                          waits=[rev[r][4], rel(6)])
        if s == 6:
            return p.emit(VE, lambda e: e.tensor_scalar(hb16b[sl][:, :], hb16a[sl][:, :],
                                                        127.0, None, ALU.min),
                          waits=[rev[r][5], rel(7)])
        w7 = [rev[r][6]]
        if tb >= 2:
            w7.append(dn_last[tb - 2])
        return p.emit(VE, lambda e: e.tensor_scalar(hint_tb[tb % 2][:, 512 * ic:512 * (ic + 1)],
                                                    hb16b[sl][:, :], -128.0, None, ALU.max),
                      waits=w7)

    def mk_down_group(tb, k):
        def go():
            j, dg = k // 2, k % 2
            i = 4 * tb + j
            bidx = BPAIRS[i % 3][dg]
            mm = None
            for ic in range(NI):
                st = ic == 0
                w = [rev[NI * tb + ic][7], ev_wdT]
                if st:
                    w.append(brd.get(bidx))
                mm = p.emit(TE, lambda e, b3=bidx, c2=ic, g=dg, tb2=tb, j2=j, s=st:
                            e.matmul(bank[b3][:, :],
                                     hint_tb[tb2 % 2][:, 512 * c2 + 128 * j2:512 * c2 + 128 * (j2 + 1)],
                                     wdT_all[:, 1024 * c2 + 512 * g:1024 * c2 + 512 * (g + 1)],
                                     start=s, stop=(c2 == NI - 1)),
                            waits=w, inc=(ic == NI - 1))
            bwr[bidx] = mm
            dmm_ev[tb][k] = mm
            if k == 7:
                dn_last[tb] = mm
        return go

    def mk_ey(tb, k):
        def go():
            j, dg = k // 2, k % 2
            i = 4 * tb + j
            bidx = BPAIRS[i % 3][dg]
            w = [dmm_ev[tb][k], ev_sh, ev_swdrow]
            if tb >= 1:
                w.append(er_ev[tb - 1][j])
            ey = p.emit(VE, lambda e: e.scalar_tensor_tensor(y2s4[j][:, 512 * dg:512 * (dg + 1)],
                                                             bank[bidx][:, :], scal[:, 3:4],
                                                             swd_row[:, 512 * dg:512 * (dg + 1)],
                                                             ALU.mult, ALU.mult),
                        waits=w)
            brd[bidx] = ey
            ey_ev[tb][k] = ey
        return go

    def mk_ldx2(tb, j):
        def go():
            i = 4 * tb + j
            w = [ev_ln1_dma, ev_ln1_dma2, ev_hstore]
            if tb >= 1:
                w.append(er_ev[tb - 1][j])
            x2_ld[tb][j] = dma(SY, x216[j][:, :], d_x2[128 * i:128 * (i + 1), :],
                               waits=w, stream="D1")
        return go

    def mk_ln2(tb, j):
        def go():
            i = 4 * tb + j
            rr = i % 2
            er = p.emit(G, lambda e: e.tensor_tensor(r_t[rr][:, :], y2s4[j][:, :],
                                                     x216[j][:, :], ALU.add),
                        waits=[ey_ev[tb][2 * j], ey_ev[tb][2 * j + 1], x2_ld[tb][j],
                               r2_rel[rr]])
            er_ev[tb][j] = er
            s1 = p.emit(VE, lambda e: e.reduce_sum(st_t[:, 0:1], r_t[rr][:, :], axis=AX),
                        waits=[er, last_stat2[0]])
            sq = p.emit(SE, lambda e: e.activation(rsq_t[rr][:, :], r_t[rr][:, :],
                                                   AF.Square, bias=0.0, scale=1.0),
                        waits=[er])
            s2 = p.emit(VE, lambda e: e.reduce_sum(st_t[:, 1:2], rsq_t[rr][:, :], axis=AX),
                        waits=[sq])
            m1 = p.emit(VE, lambda e: e.tensor_scalar(st_t[:, 2:3], st_t[:, 0:1],
                                                      1.0 / 1024.0, None, ALU.mult),
                        waits=[s1])
            m2 = p.emit(VE, lambda e: e.tensor_scalar(st_t[:, 3:4], st_t[:, 1:2],
                                                      1.0 / 1024.0, None, ALU.mult),
                        waits=[s2])
            m3 = p.emit(VE, lambda e: e.tensor_tensor(st_t[:, 4:5], st_t[:, 2:3],
                                                      st_t[:, 2:3], ALU.mult),
                        waits=[m1])
            m4 = p.emit(VE, lambda e: e.tensor_tensor(st_t[:, 5:6], st_t[:, 3:4],
                                                      st_t[:, 4:5], ALU.subtract),
                        waits=[m2, m3])
            l1 = p.emit(SE, lambda e: e.activation(st_t[:, 6:7], st_t[:, 5:6], AF.Ln,
                                                   bias=cst[:, 2:3], scale=1.0),
                        waits=[m4])
            l2 = p.emit(SE, lambda e: e.activation(st_t[:, 7:8], st_t[:, 6:7], AF.Exp,
                                                   bias=0.0, scale=-0.5),
                        waits=[l1])
            mrs = p.emit(VE, lambda e: e.scalar_tensor_tensor(st_t[:, 4:5], st_t[:, 2:3],
                                                              -1.0, st_t[:, 7:8],
                                                              ALU.mult, ALU.mult),
                         waits=[l2])
            ap = p.emit(SE, lambda e: e.activation(out_t[i % 2][:, :], r_t[rr][:, :],
                                                   AF.Identity, bias=st_t[:, 4:5],
                                                   scale=st_t[:, 7:8]),
                        waits=[mrs, out_rel[i % 2]])
            last_stat2[0] = ap
            r2_rel[rr] = ap
            fs = dma(SY, d_out[128 * i:128 * (i + 1), :], out_t[i % 2][:, :],
                     waits=[ap], stream="D4")
            out_rel[i % 2] = fs
        return go

    trig = [NI * tb + NI + 7 for tb in range(4)]
    sched = {}

    def at(t, fn):
        sched.setdefault(t, []).append(fn)

    for tb in range(4):
        for k in range(8):
            at(trig[tb] + 2 * k, mk_down_group(tb, k))
            at(trig[tb] + 2 * k + 3, mk_ey(tb, k))
        for j in range(4):
            at(trig[tb] + 5 * j + 1, mk_ldx2(tb, j))
        for j in range(4):
            at(trig[tb] + NI + 5 * j, mk_ln2(tb, j))
    maxt = max(sched)
    for r0 in range(min(NDR, NRND)):
        emit_ld(r0)
    for t in range(max(NRND + 8, maxt + 1)):
        for fn in sched.get(t, ()):
            fn()
        for s in range(7, -1, -1):
            r = t - s
            if 0 <= r < NRND:
                rev[r][s] = st_h(r, s)
        if t + NDR < NRND:
            emit_ld(t + NDR)


    _finish(p, nc, es, sems, inner_scopes)
    return nc


def _finish(p, nc, es, sems, inner_scopes=()):
    for sc in reversed(list(inner_scopes)):
        sc.close()
    with nc.Block() as block:
        def mk_runner(engine_name):
            def _run(eng):
                for need, fn, sem, by in p.ops[engine_name]:
                    for s, v in need:
                        eng.wait_ge(sems[s], v)
                    ins = fn(eng)
                    if sem is not None:
                        ins.then_inc(sems[sem], by)
                if engine_name == "sync":
                    for s in ("DS", "DG", "D1", "D2", "D3", "D4"):
                        if p.counts.get(s):
                            eng.wait_ge(sems[s], p.counts[s])
            return _run
        block.tensor(mk_runner("tensor"))
        block.vector(mk_runner("vector"))
        block.scalar(mk_runner("scalar"))
        block.gpsimd(mk_runner("gpsimd"))
        block.sync(mk_runner("sync"))
    es.close()


_NC_CACHE = {}


def _get_nc(upto="ALL", dbg=()):
    key = (upto, tuple(sorted(dbg)))
    if key not in _NC_CACHE:
        _NC_CACHE[key] = build_nc(upto, dbg)
    return _NC_CACHE[key]


def _host_prep(inputs):
    x = np.asarray(inputs["L0_attn_x"], np.float32)
    Wq = np.asarray(inputs["L0_attn_Wq"], np.float32)
    Wk = np.asarray(inputs["L0_attn_Wk"], np.float32)
    Wv = np.asarray(inputs["L0_attn_Wv"], np.float32)
    Wo = np.asarray(inputs["L0_attn_Wo"], np.float32)
    orf = np.asarray(inputs["L0_attn_orf"], np.float32)

    def prep_wqk(W):
        a = W.T.reshape(8, 128, 8, 128)
        return np.ascontiguousarray(a.transpose(1, 2, 0, 3).reshape(128, 8192))

    def prep_wvo(W):
        a = W.T.reshape(8, 128, 1024)
        return np.ascontiguousarray(a.transpose(1, 0, 2).reshape(128, 8192))

    indh = np.zeros((128, 128), np.float32)
    for p_ in range(128):
        for c in range(8):
            indh[p_, 16 * c + 2 * c + (1 if p_ >= 64 else 0)] = 1.0

    shared = {
        "wqt": prep_wqk(Wq),
        "wkt": prep_wqk(Wk),
        "wvt": prep_wvo(Wv),
        "wot": prep_wvo(Wo),
        "orft": np.ascontiguousarray(orf.T),
        "identh": np.eye(128, dtype=np.float32),
        "indh": indh,
        "bqh": np.ascontiguousarray(np.asarray(inputs["L0_attn_bq"], np.float32).reshape(8, 128).T),
        "bkh": np.ascontiguousarray(np.asarray(inputs["L0_attn_bk"], np.float32).reshape(8, 128).T),
        "bvh": np.ascontiguousarray(np.asarray(inputs["L0_attn_bv"], np.float32).reshape(8, 128).T),
        "boh": np.asarray(inputs["L0_attn_bo"], np.float32).reshape(1, 1024),
        "clip_x": np.asarray(inputs["L0_ffn_clip_x"], np.float32).reshape(1, 1),
        "clip_down": np.asarray(inputs["L0_ffn_clip_down"], np.float32).reshape(1, 1),
    }
    wg = np.asarray(inputs["L0_ffn_w_gate"], np.float32)
    wu = np.asarray(inputs["L0_ffn_w_up"], np.float32)
    wd = np.asarray(inputs["L0_ffn_w_down"], np.float32)

    in_maps = []
    for core in range(8):
        b = core // 2
        hf = core % 2
        xs = np.ascontiguousarray(x[b, T * hf:T * (hf + 1), :])
        xfm = np.ascontiguousarray(xs.T.reshape(8, 128, T).transpose(1, 0, 2).reshape(128, 8 * T))
        m = dict(shared)
        m["x_tm"] = xs
        m["x_fm"] = xfm
        wgm = np.ones((384, 1024), np.float32)
        wum = np.ones((384, 1024), np.float32)
        for s in range(3):
            ic = core + 8 * s
            if ic < NI:
                wgm[128 * s:128 * (s + 1)] = wg[128 * ic:128 * (ic + 1)]
                wum[128 * s:128 * (s + 1)] = wu[128 * ic:128 * (ic + 1)]
        m["wgq_my"] = wgm
        m["wuq_my"] = wum
        m["wdq_my"] = np.ascontiguousarray(wd[128 * core:128 * (core + 1)])
        in_maps.append(m)
    return in_maps


def kernel(**inputs):
    in_maps = _host_prep(inputs)
    nc = _get_nc("ALL", ())
    res = run_bass_kernel_spmd(nc, in_maps, core_ids=list(range(8)))
    outs = [res.results[c]["out"] for c in range(8)]
    full = np.stack(outs, axis=0).reshape(4, 2, T, 1024).reshape(4, 4096, 1024)
    return np.ascontiguousarray(full.astype(np.float32))


# version 18
# speedup vs baseline: 1.0142x; 1.0142x over previous
from contextlib import ExitStack

import numpy as np

import concourse.bass as bass
import concourse.mybir as mybir
from concourse.bass_utils import run_bass_kernel_spmd

F32 = mybir.dt.float32
BF16 = mybir.dt.bfloat16
FP16 = mybir.dt.float16
I32 = mybir.dt.int32
AF = mybir.ActivationFunctionType
ALU = mybir.AluOpType
AX = mybir.AxisListType.X

D4 = 64.0 ** 0.25
LN16 = float(np.log(16.0))
EPS16 = 1e-6 / 16.0
LN_EPS = 1e-5
RBIAS = 1023.5
TBIG = 1e9

T = 2048
NT = 16
NJ = 4
NC = 8
H = 16
I_FF = 2816
NI = 22

ENGINES = ("tensor", "vector", "scalar", "gpsimd", "sync")
ESEM = {"tensor": "T", "vector": "V", "scalar": "S", "gpsimd": "G", "sync": "SY"}


class Prog:
    def __init__(self):
        self.ops = {e: [] for e in ENGINES}
        self.counts = {}
        self.hw = {}

    def emit(self, engine, fn, waits=(), dma=False, inc=True, sem=None):
        by = 1
        if sem is None:
            if dma:
                sem = "DG" if engine == "gpsimd" else "DS"
                by = 16
            elif inc:
                sem = ESEM[engine]
        elif dma:
            by = 16
        waits = list(waits)
        if dma and self.counts.get(sem, 0) > 0:
            waits.append((sem, self.counts[sem]))
        need = []
        for ev in waits:
            if ev is None:
                continue
            s, v = ev
            if self.hw.get((engine, s), 0) < v:
                need.append((s, v))
                self.hw[(engine, s)] = v
        ev_out = None
        if sem is not None:
            self.counts[sem] = self.counts.get(sem, 0) + by
            ev_out = (sem, self.counts[sem])
        self.ops[engine].append((need, fn, sem, by))
        return ev_out


def build_nc(upto="ALL", dbg=()):
    nc = bass.Bass("TRN2", target_bir_lowering=False, debug=False)
    p = Prog()

    def din(name, shape):
        return nc.dram_tensor(name, shape, F32, kind="ExternalInput").ap()

    d_xfm = din("x_fm", [128, 8 * T])
    d_xtm = din("x_tm", [T, 1024])
    d_wqt = din("wqt", [128, 8192])
    d_wkt = din("wkt", [128, 8192])
    d_wvt = din("wvt", [128, 8192])
    d_wot = din("wot", [128, 8192])
    d_orft = din("orft", [64, 256])
    d_ind = din("indh", [128, 128])
    d_ident = din("identh", [128, 128])
    d_bq = din("bqh", [128, 8])
    d_bk = din("bkh", [128, 8])
    d_bv = din("bvh", [128, 8])
    d_bo = din("boh", [1, 1024])
    d_clipx = din("clip_x", [1, 1])
    d_clipd = din("clip_down", [1, 1])
    d_wgm = din("wgq_my", [384, 1024])
    d_wum = din("wuq_my", [384, 1024])
    d_wdm = din("wdq_my", [128, I_FF])
    d_out = nc.dram_tensor("out", [T, 1024], F32, kind="ExternalOutput").ap()

    h_cc1i = nc.dram_tensor("cc1i", [128, 2080], F32)
    h_cc1o = nc.dram_tensor("cc1o", [128, 2080], F32)
    d_cc1i, d_cc1o = h_cc1i.ap(), h_cc1o.ap()
    d_mx = nc.dram_tensor("mxd", [128], F32).ap()
    h_ccmi = nc.dram_tensor("ccmi", [1], F32)
    h_ccmo = nc.dram_tensor("ccmo", [1], F32)
    d_mx2 = nc.dram_tensor("mxd2", [128], F32).ap()
    h_ccmi2 = nc.dram_tensor("ccmi2", [1], F32)
    h_ccmo2 = nc.dram_tensor("ccmo2", [1], F32)
    d_scal = nc.dram_tensor("scald", [8], F32).ap()
    d_x2 = nc.dram_tensor("x2d", [T, 1024], F32).ap()
    d_h = nc.dram_tensor("hd", [NI * 128, T], F32).ap()
    d_swd = nc.dram_tensor("swdd", [1024], F32).ap()
    h_wqi = nc.dram_tensor("wqi", [128, 8960], BF16)
    h_wqo = nc.dram_tensor("wqo", [1024, 8960], BF16)
    h_sci = nc.dram_tensor("sci", [128, 8], F32)
    h_sco = nc.dram_tensor("sco", [1024, 8], F32)
    d_wqi, d_wqo = h_wqi.ap(), h_wqo.ap()
    d_sci, d_sco = h_sci.ap(), h_sco.ap()

    es = ExitStack()
    inner_scopes = []

    def track(sc):
        inner_scopes.append(sc)
        return sc

    def SBL(name, shape, dt):
        return es.enter_context(nc.sbuf_tensor(name, shape, dt, side="left"))

    sems = {}
    for s in ("T", "V", "S", "G", "SY", "DG", "DS", "CC", "D1", "D2", "D3", "D4"):
        sems[s] = es.enter_context(nc.semaphore(s))

    bank = [es.enter_context(nc.psum_tensor(f"pb{i}", [128, 512], F32)) for i in range(6)]
    tbank = [es.enter_context(nc.psum_tensor(f"tb{i}", [128, 1024], BF16)) for i in range(2)]
    bwr = {}
    brd = {}

    cst = SBL("cst", [128, 8], F32)
    ones_col = SBL("ones_col", [128, 1], BF16)
    ones_row = SBL("ones_row", [1, 128], BF16)
    idb = SBL("idb", [128, 128], BF16)
    ind = SBL("ind", [128, 128], BF16)
    orf2x = SBL("orf2x", [128, 256], BF16)
    bqf = SBL("bqf", [128, 8], F32)
    bkf = SBL("bkf", [128, 8], F32)
    bvb = SBL("bvb", [128, 8], BF16)
    bo_sb = SBL("bo_sb", [1, 1024], BF16)
    bo_adj = SBL("bo_adj", [1, 1024], BF16)
    sig_s = SBL("sig_s", [1, 2], F32)
    clip_sb = SBL("clip_sb", [1, 2], F32)
    scal = SBL("scal", [128, 4], F32)
    scal1 = SBL("scal1", [1, 8], F32)
    mrow = SBL("mrow", [1, 128], F32)
    runmax = SBL("runmax", [128, NT], F32)
    runmax2 = SBL("runmax2", [128, NI * NJ], F32)
    rmx1 = SBL("rmx1", [128, 2], F32)
    swg_all = SBL("swg_all", [128, NI], F32)
    swu_all = SBL("swu_all", [128, NI], F32)
    sxswg = SBL("sxswg", [128, NI], F32)
    sxswu = SBL("sxswu", [128, NI], F32)
    swd_all = SBL("swd_all", [128, 8], F32)
    st_t = SBL("st_t", [128, 16], F32)
    r_t = [SBL(f"rt{r}", [128, 1024], F32) for r in (0, 1)]
    rsq_t = [SBL(f"rsq{r}", [128, 1024], F32) for r in (0, 1)]
    wst = SBL("wst", [128, 4], F32)

    G, SY, TE, VE, SE = "gpsimd", "sync", "tensor", "vector", "scalar"
    dbg_evs = []

    def DBG(name, ap, after_list):
        if name in dbg:
            d = nc.dram_tensor("dbg_" + name, list(ap.shape), F32,
                               kind="ExternalOutput").ap()
            q = SY if ap.dtype == F32 else G
            dbg_evs.append(p.emit(q, lambda e, dd=d, aa=ap: e.dma_start(out=dd, in_=aa),
                                  waits=after_list, dma=True))

    def dma(q, out, in_, waits=(), stream=None):
        return p.emit(q, lambda e, o=out, i=in_: e.dma_start(out=o, in_=i),
                      waits=waits, dma=True, sem=stream)

    scope_Q = track(ExitStack())
    q_sb = scope_Q.enter_context(nc.sbuf_tensor("q_sb", [128, 8 * T], BF16, side="left"))
    scope_K = track(ExitStack())
    k_sb = scope_K.enter_context(nc.sbuf_tensor("k_sb", [128, 8 * T], BF16, side="left"))
    scope_R0 = track(ExitStack())
    sbr0 = lambda name, shape, dt: scope_R0.enter_context(
        nc.sbuf_tensor(name, shape, dt, side="right"))
    expnq = sbr0("expnq", [16, 2048], BF16)
    negnk = sbr0("negnk", [128, 256], F32)
    vs_eps = sbr0("vs_eps", [1, 1040], BF16)
    bufs_eps = sbr0("bufs_eps", [1, 1040], BF16)
    buf_sb = sbr0("buf_sb", [128, 2080], F32)
    buf_bf = sbr0("buf_bf", [128, 2080], BF16)
    scope_RV = track(ExitStack())
    v_ext = scope_RV.enter_context(nc.sbuf_tensor("v_ext", [128, NT * 1040], BF16, side="right"))
    scope_P1 = track(ExitStack())
    x_fm = scope_P1.enter_context(nc.sbuf_tensor("x_fm_s", [128, 8 * T], BF16, side="left"))
    orf_f = scope_P1.enter_context(nc.sbuf_tensor("orf_f", [128, 256], F32, side="left"))
    sqq = [scope_P1.enter_context(nc.sbuf_tensor(f"sqq{r}", [128, 512], BF16, side="left"))
           for r in range(8)]
    sqk = sqq
    scope_P2 = track(ExitStack())
    wqk_sb = scope_P2.enter_context(nc.sbuf_tensor("wqk_sb", [128, 8192], BF16, side="left"))

    ld_xfm = dma(G, x_fm[:, :], d_xfm[:, :])
    ld_wq = dma(G, wqk_sb[:, :], d_wqt[:, :])
    ld_orf0 = dma(G, orf_f[0:64, :], d_orft[:, :])
    ld_orf1 = dma(G, orf_f[64:128, :], d_orft[:, :])
    ld_ind = dma(G, ind[:, :], d_ind[:, :])
    ld_id = dma(G, idb[:, :], d_ident[:, :])
    ld_bq = dma(G, bqf[:, :], d_bq[:, :])
    ld_bk = dma(G, bkf[:, :], d_bk[:, :])
    ld_bv = dma(G, bvb[:, :], d_bv[:, :])
    ld_bo = dma(G, bo_sb[:, :], d_bo[:, :])
    ld_cl = dma(G, clip_sb[0:1, 0:1], d_clipx[:, :])
    ld_cl2 = dma(G, clip_sb[0:1, 1:2], d_clipd[:, :])

    ev_c = []
    for col, val in ((0, LN16), (1, -LN16), (2, LN_EPS), (3, RBIAS), (4, -RBIAS)):
        ev_c.append(p.emit(VE, lambda e, c=col, v=val: e.memset(cst[:, c:c + 1], v)))
    ev_c.append(p.emit(VE, lambda e: e.memset(ones_col[:, :], 1.0)))
    ev_c.append(p.emit(VE, lambda e: e.memset(ones_row[:, :], 1.0)))
    ev_cst = ev_c[-1]

    ev_wmb = p.emit(VE, lambda e: e.memset(buf_bf[:, 0:512], 0.0))
    warm_last = None
    for wi in range(90):
        warm_last = p.emit(TE, lambda e: e.matmul(bank[5][:, :], idb[:, 0:128],
                                                  buf_bf[:, 0:512],
                                                  start=True, stop=True),
                           waits=([ld_id, ev_wmb] if wi == 0 else []),
                           inc=(wi == 89))
    brd[5] = warm_last

    ev_orf = p.emit(SE, lambda e: e.activation(orf2x[:, :], orf_f[:, :], AF.Identity,
                                               bias=0.0, scale=1.0 / D4),
                    waits=[ld_orf0, ld_orf1])
    ev_sig = p.emit(SE, lambda e: e.activation(sig_s[0:1, 0:2], clip_sb[0:1, 0:2],
                                               AF.Sigmoid, bias=0.0, scale=1.0),
                    waits=[ld_cl, ld_cl2])

    sq_rel = [None] * 8
    last_q_ev = None
    last_k_ev = None
    last_qk_mm = None
    ev_expnq = None
    ev_negnk = None
    ld_wk = None
    for wpass in ("q", "k"):
        if wpass == "k":
            ld_wk = dma(G, wqk_sb[:, :], d_wkt[:, :], waits=[last_qk_mm])
        for j in range(NJ):
            sq_evs = []
            for c in range(NC):
                bidx = (j * NC + c) % 2 + (0 if wpass == "q" else 2)
                b_ap = bqf if wpass == "q" else bkf
                dst = q_sb if wpass == "q" else k_sb
                sqt = sqq[c]
                ldw = ld_wq if wpass == "q" else ld_wk
                mm = None
                for kc in range(NC):
                    st = kc == 0
                    w = [ld_xfm, ldw]
                    if st:
                        w.append(brd.get(bidx))
                    mm = p.emit(TE, lambda e, b=bidx, kk=kc, cc=c, jj=j,
                                s=st, sp=(kc == NC - 1):
                                e.matmul(bank[b][:, :],
                                         wqk_sb[:, 1024 * cc + 128 * kk:1024 * cc + 128 * (kk + 1)],
                                         x_fm[:, 2048 * kk + 512 * jj:2048 * kk + 512 * (jj + 1)],
                                         start=s, stop=sp),
                                waits=w, inc=(kc == NC - 1))
                bwr[bidx] = mm
                last_qk_mm = mm
                e1 = p.emit(SE, lambda e, b=bidx, d=dst, ba=b_ap, cc=c, jj=j:
                            e.activation(d[:, 2048 * cc + 512 * jj:2048 * cc + 512 * (jj + 1)],
                                         bank[b][:, :], AF.Identity,
                                         bias=ba[:, cc:cc + 1], scale=1.0),
                            waits=[mm, ld_bq, ld_bk])
                e2 = p.emit(VE, lambda e, d=dst, cc=c, jj=j, sq=sqt:
                            e.tensor_tensor(sq[:, :],
                                            d[:, 2048 * cc + 512 * jj:2048 * cc + 512 * (jj + 1)],
                                            d[:, 2048 * cc + 512 * jj:2048 * cc + 512 * (jj + 1)],
                                            ALU.mult),
                            waits=[e1, sq_rel[c]])
                brd[bidx] = e1
                sq_evs.append(e2)
                if wpass == "q":
                    last_q_ev = e1
                else:
                    last_k_ev = e1
            if wpass == "q":
                n_last = None
                for c in range(NC):
                    n_last = p.emit(TE, lambda e, cc=c, sq=sqq[c], s=(c == 0):
                                    e.matmul(bank[4][0:16, :], ind[:, 16 * cc:16 * (cc + 1)],
                                             sq[:, :], start=s, stop=(cc == NC - 1)),
                                    waits=[sq_evs[c], ld_ind, brd.get(4) if c == 0 else None],
                                    inc=(c == NC - 1))
                for c in range(NC):
                    sq_rel[c] = n_last
                bwr[4] = n_last
                ev_expnq = p.emit(SE, lambda e, jj=j:
                                  e.activation(expnq[:, 512 * jj:512 * (jj + 1)],
                                               bank[4][0:16, :], AF.Exp,
                                               bias=cst[0:16, 0:1], scale=1.0 / 16.0),
                                  waits=[n_last, ev_cst])
                brd[4] = ev_expnq
            else:
                n_lasts = [None] * 4
                for c in range(NC):
                    for ts_ in range(4):
                        nb = (4, 5, 0, 1)[ts_]
                        n_lasts[ts_] = p.emit(TE, lambda e, cc=c, sq=sqq[c], ts=ts_, b4=nb,
                                              s=(c == 0):
                                              e.matmul(bank[b4][:, 0:16],
                                                       sq[:, 128 * ts:128 * (ts + 1)],
                                                       ind[:, 16 * cc:16 * (cc + 1)],
                                                       start=s, stop=(cc == NC - 1)),
                                              waits=[sq_evs[c], ld_ind,
                                                     brd.get(nb) if c == 0 else None],
                                              inc=(c == NC - 1))
                    sq_rel[c] = n_lasts[3] if c == NC - 1 else sq_rel[c]
                for c in range(NC):
                    sq_rel[c] = n_lasts[3]
                for ts_ in range(4):
                    nb = (4, 5, 0, 1)[ts_]
                    bwr[nb] = n_lasts[ts_]
                    ev_negnk = p.emit(SE, lambda e, jj=j, ts=ts_, b4=nb:
                                      e.activation(negnk[:, 64 * jj + 16 * ts:64 * jj + 16 * (ts + 1)],
                                                   bank[b4][:, 0:16], AF.Identity,
                                                   bias=cst[:, 1:2], scale=-1.0 / 16.0),
                                      waits=[n_lasts[ts_], ev_cst])
                    brd[nb] = ev_negnk
    DBG("q_sb", q_sb[:, :], [last_q_ev])
    DBG("k_sb", k_sb[:, :], [last_k_ev])
    DBG("expnq", expnq[:, :], [ev_expnq])
    DBG("negnk", negnk[:, :], [ev_negnk])

    scope_P2.close()
    scope_P3 = track(ExitStack())
    wv_sb = scope_P3.enter_context(nc.sbuf_tensor("wv_sb", [128, 8192], BF16, side="left"))
    ld_wv = dma(G, wv_sb[:, :], d_wvt[:, :], waits=[last_qk_mm])

    last_vevac = None
    ev_vdone = None
    vq = []
    vs_state = [None]

    def emit_vsum(ii0, vev):
        for seg, (lo, n, bk2) in enumerate(((0, 512, 4), (512, 512, 5), (1024, 16, 3))):
            w = [vev, ev_cst]
            if ii0 == 0:
                w.append(brd.get(bk2))
            vs_state[0] = p.emit(TE, lambda e, ii=ii0, l=lo, nn=n, b2=bk2, s=(ii0 == 0):
                                 e.matmul(bank[b2][0:1, 0:nn], ones_col[:, :],
                                          v_ext[:, 1040 * ii + l:1040 * ii + l + nn],
                                          start=s, stop=(ii == NT - 1)),
                                 waits=w, inc=(ii0 == NT - 1 and seg == 2))

    for i in range(NT):
        evs2 = []
        for dg in range(2):
            bidx = (2 * i + dg) % 2
            mm = None
            for kc in range(NC):
                st = kc == 0
                w = [ld_xfm, ld_wv]
                if st:
                    w.append(brd.get(bidx))
                mm = p.emit(TE, lambda e, b=bidx, kk=kc, ii=i, d=dg, s=st,
                            sp=(kc == NC - 1):
                            e.matmul(bank[b][:, :],
                                     x_fm[:, 2048 * kk + 128 * ii:2048 * kk + 128 * (ii + 1)],
                                     wv_sb[:, 1024 * kk + 512 * d:1024 * kk + 512 * (d + 1)],
                                     start=s, stop=sp),
                            waits=w, inc=(kc == NC - 1))
            bwr[bidx] = mm
            base = 1040 * i + 520 * dg
            e1 = p.emit(VE, lambda e, b=bidx, bs=base:
                        e.tensor_copy(
                            v_ext[:, bs:bs + 520].rearrange("p (h d) -> p h d", d=65)[:, :, 0:64],
                            bank[b][:, :].rearrange("p (h d) -> p h d", d=64)),
                        waits=[mm])
            e2 = p.emit(VE, lambda e, bs=base:
                        e.memset(v_ext[:, bs:bs + 520].rearrange("p (h d) -> p h d", d=65)[:, :, 64:65], 1.0),
                        waits=[e1])
            brd[bidx] = e2
            evs2.append(e2)
        last_vevac = evs2[-1]
        vq.append(evs2[-1])
        if i >= 1:
            emit_vsum(i - 1, vq[i - 1])
    emit_vsum(NT - 1, vq[NT - 1])
    ev_vdone = vs_state[0]
    for bk2 in (4, 5, 3):
        bwr[bk2] = ev_vdone
    p.emit(SE, lambda e: e.activation(vs_eps[0:1, 0:512], bank[4][0:1, 0:512],
                                      AF.Identity, bias=0.0, scale=EPS16),
           waits=[ev_vdone])
    p.emit(SE, lambda e: e.activation(vs_eps[0:1, 512:1024], bank[5][0:1, 0:512],
                                      AF.Identity, bias=0.0, scale=EPS16),
           waits=[ev_vdone])
    ev_vseps = p.emit(SE, lambda e: e.activation(vs_eps[0:1, 1024:1040], bank[3][0:1, 0:16],
                                                 AF.Identity, bias=0.0, scale=EPS16),
                      waits=[ev_vdone])
    brd[4] = ev_vseps
    brd[5] = ev_vseps
    brd[3] = ev_vseps
    DBG("v_ext", v_ext[:, :], [last_vevac])
    DBG("vs_eps", vs_eps[:, :], [ev_vseps])

    if upto == "P":
        _finish(p, nc, es, sems, inner_scopes)
        return nc

    scope_P3.close()
    scope_P1.close()

    scope_A = track(ExitStack())
    kf_t = [scope_A.enter_context(nc.sbuf_tensor(f"kf{r}", [128, 256], BF16, side="right"))
            for r in (0, 1, 2)]
    kf_rel = [None, None, None]
    ev_bufs = None
    buf_done = None
    for h in range(H):
        b = h % 2
        c = h // 2
        pend = None
        for i in range(NT):
            r = i % 3
            pb_i = i % 2
            mm = p.emit(TE, lambda e, bb=b, cc=c, ii=i, pbi=pb_i:
                        e.matmul(bank[pbi][:, 0:256],
                                 k_sb[64 * bb:64 * (bb + 1),
                                      2048 * cc + 128 * ii:2048 * cc + 128 * (ii + 1)],
                                 orf2x[64 * bb:64 * (bb + 1), :],
                                 start=True, stop=True),
                        waits=[ev_orf, last_k_ev, ev_negnk, brd.get(pb_i), kf_rel[r]])
            bwr[pb_i] = mm
            ef = p.emit(SE, lambda e, rr=r, pbi=pb_i, ii=i, hh=h:
                        e.activation(kf_t[rr][:, :], bank[pbi][:, 0:256], AF.Exp,
                                     bias=negnk[:, 16 * ii + hh:16 * ii + hh + 1],
                                     scale=1.0),
                        waits=[mm])
            brd[pb_i] = ef
            if pend is not None:
                pi, pr, pef = pend
                for mc in range(2):
                    w = [pef, last_vevac]
                    if pi == 0:
                        w.append(brd.get(2 + mc))
                    buf_done = p.emit(TE, lambda e, rr=pr, m=mc, ii=pi, hh=h,
                                      s=(pi == 0):
                                      e.matmul(bank[2 + m][:, 0:65],
                                               kf_t[rr][:, 128 * m:128 * (m + 1)],
                                               v_ext[:, 1040 * ii + 65 * hh:1040 * ii + 65 * (hh + 1)],
                                               start=s, stop=False),
                                      waits=w, inc=(mc == 1))
                kf_rel[pr] = buf_done
            pend = (i, r, ef)
        pi, pr, pef = pend
        for mc in range(2):
            w = [pef, last_vevac]
            buf_done = p.emit(TE, lambda e, rr=pr, m=mc, ii=pi, hh=h:
                              e.matmul(bank[2 + m][:, 0:65],
                                       kf_t[rr][:, 128 * m:128 * (m + 1)],
                                       v_ext[:, 1040 * ii + 65 * hh:1040 * ii + 65 * (hh + 1)],
                                       start=False, stop=False),
                              waits=w, inc=(mc == 1))
            kf_rel[pr] = buf_done
        for mc in range(2):
            buf_done = p.emit(TE, lambda e, m=mc, hh=h:
                              e.matmul(bank[2 + m][:, 0:65],
                                       ones_row[0:1, :],
                                       vs_eps[0:1, 65 * hh:65 * (hh + 1)],
                                       start=False, stop=True),
                              waits=[ev_vseps], inc=True)
            bwr[2 + mc] = buf_done
        ev_bufs = None
        for mc in range(2):
            ev_bufs = p.emit(VE, lambda e, hh=h, m=mc:
                             e.tensor_copy(buf_sb[:, 130 * hh + 65 * m:130 * hh + 65 * (m + 1)],
                                           bank[2 + m][:, 0:65]),
                             waits=[bwr[2 + mc]])
            brd[2 + mc] = ev_bufs
    last_bufmm = buf_done
    DBG("buf_loc", buf_sb[:, :], [ev_bufs])

    if upto == "A":
        _finish(p, nc, es, sems, inner_scopes)
        return nc

    scope_A.close()
    scope_RV.close()
    scope_K.close()

    st1 = dma(SY, d_cc1i[:, :], buf_sb[:, :], waits=[ev_bufs], stream="D2")
    cc1 = p.emit(G, lambda e: e.collective_compute(
        "AllReduce", ALU.add,
        replica_groups=[[0, 1], [2, 3], [4, 5], [6, 7]],
        ins=[h_cc1i.ap()], outs=[h_cc1o.ap()]),
        waits=[st1], sem="CC")
    ld1 = dma(SY, buf_sb[:, :], d_cc1o[:, :], waits=[cc1, ev_bufs], stream="D2")
    ev_bufbf = p.emit(VE, lambda e: e.tensor_copy(buf_bf[:, :], buf_sb[:, :]),
                      waits=[ld1, warm_last])
    DBG("buf_bf", buf_bf[:, :], [ev_bufbf])

    scope_QW = track(ExitStack())
    sbq = lambda name, shape, dt: scope_QW.enter_context(
        nc.sbuf_tensor(name, shape, dt, side="right"))
    S3 = 3
    qw_ld = [sbq(f"qwld{r}", [128, 1024], F32) for r in range(S3)]
    qru = [sbq(f"qru{r}", [128, 1024], F32) for r in range(S3)]
    qri = [sbq(f"qri{r}", [128, 1024], I32) for r in range(S3)]
    qrf = [sbq(f"qrf{r}", [128, 1024], BF16) for r in range(S3)]
    qrth = [sbq(f"qrth{r}", [128, 1024], BF16) for r in range(S3)]
    qrr = [sbq(f"qrr{r}", [128, 1024], F32) for r in range(S3)]
    qwib = sbq("qwib", [128, I_FF], BF16)
    qstage = [sbq(f"qstage{r}", [128, 1024], BF16) for r in (0, 1)]
    qsc = sbq("qsc", [128, 8], F32)
    qst = sbq("qst", [128, 20], F32)
    qwld_d = sbq("qwld_d", [128, I_FF], F32)
    qwdT = sbq("qwdT", [128, 22 * 128], BF16)

    QR = 9
    qrev = [[None] * 6 for _ in range(QR)]
    qld_ev = [None] * QR
    qch_ev = [None] * QR
    qest_ev = [None] * QR
    qsc_evs = []
    stage_dmas = []
    qstage_rel = [None, None]

    def q_meta(r):
        if r < 6:
            kind = "g" if r < 3 else "u"
            s3 = r % 3
            return kind, s3, 0, 1024
        lo, n = ((0, 1024), (1024, 1024), (2048, 768))[r - 6]
        return "d", r - 6, lo, n

    def q_ld(r):
        w = [last_bufmm, ev_bufs]
        if r >= S3:
            w.append(qrev[r - S3][0])
        kind, s3, _, _ = q_meta(r)
        srcd = d_wgm if kind == "g" else d_wum
        qld_ev[r] = dma(SY, qw_ld[r % S3][:, :], srcd[128 * s3:128 * (s3 + 1), :],
                        waits=w, stream="D2")

    def q_chain(r):
        kind, s3, lo, n = q_meta(r)
        if kind == "d":
            if r != 6:
                qch_ev[r] = qch_ev[6]
                return
            e1 = p.emit(VE, lambda e: e.reduce_max(qst[:, 10:11], qwld_d[:, :], axis=AX,
                                                   apply_absolute_value=True),
                        waits=[qld_ev[6]])
            qsc_evs.append(p.emit(VE, lambda e: e.tensor_scalar(qsc[:, 6:7], qst[:, 10:11],
                                                                1.0 / 127.0, None, ALU.mult),
                                  waits=[e1]))
            e3 = p.emit(VE, lambda e: e.reciprocal(qst[:, 11:12], qst[:, 10:11]), waits=[e1])
            qch_ev[r] = p.emit(VE, lambda e: e.tensor_scalar(qst[:, 11:12], qst[:, 11:12],
                                                             127.0, None, ALU.mult),
                               waits=[e3])
            return
        sc = (0 if kind == "g" else 3) + s3
        e1 = p.emit(VE, lambda e, sl=r % S3: e.reduce_max(qst[:, r:r + 1], qw_ld[sl][:, :],
                                                          axis=AX, apply_absolute_value=True),
                    waits=[qld_ev[r]])
        qsc_evs.append(p.emit(VE, lambda e, c=sc: e.tensor_scalar(qsc[:, c:c + 1],
                                                                  qst[:, r:r + 1],
                                                                  1.0 / 127.0, None, ALU.mult),
                              waits=[e1]))
        e3 = p.emit(VE, lambda e: e.reciprocal(qst[:, 12 + r:13 + r], qst[:, r:r + 1]),
                    waits=[e1])
        qch_ev[r] = p.emit(VE, lambda e: e.tensor_scalar(qst[:, 12 + r:13 + r],
                                                         qst[:, 12 + r:13 + r], 127.0,
                                                         None, ALU.mult),
                           waits=[e3])

    qwinv = [None] * QR

    def q_st(r, s):
        kind, s3, lo, n = q_meta(r)
        sl = r % S3
        u = qru[sl][:, 0:n]
        i_ = qri[sl][:, 0:n]
        nf = qrf[sl][:, 0:n]
        th = qrth[sl][:, 0:n]
        r1 = qrr[sl][:, 0:n]
        rel = lambda st2: (qrev[r - S3][st2] if r >= S3 else None)
        if s == 0:
            src = qw_ld[sl][:, :] if kind != "d" else qwld_d[:, lo:lo + n]
            scol = qst[:, 12 + r:13 + r] if kind != "d" else qst[:, 11:12]
            return p.emit(SE, lambda e: e.activation(u, src, AF.Identity,
                                                     bias=cst[:, 3:4], scale=scol),
                          waits=[qch_ev[r], ev_cst, rel(2)])
        if s == 1:
            return p.emit(G, lambda e: e.tensor_copy(i_, u), waits=[qrev[r][0], rel(4)])
        if s == 2:
            return p.emit(VE, lambda e: e.scalar_tensor_tensor(nf, i_, 1.0, u,
                                                               ALU.mult, ALU.subtract),
                          waits=[qrev[r][1], rel(3)])
        if s == 3:
            return p.emit(SE, lambda e: e.activation(th, nf, AF.Tanh, bias=0.0, scale=TBIG),
                          waits=[qrev[r][2]])
        if s == 4:
            return p.emit(VE, lambda e: e.scalar_tensor_tensor(r1, th, -0.5, i_,
                                                               ALU.mult, ALU.add),
                          waits=[qrev[r][3], rel(5)])
        w5 = [qrev[r][4]]
        if r >= 1 and (r < 6 or r == 6):
            w5.append(qest_ev[r - 1] if r < 6 else qest_ev[5])
        return p.emit(SE, lambda e: e.activation(qwib[:, lo:lo + n], r1, AF.Identity,
                                                 bias=cst[:, 4:5], scale=1.0),
                      waits=w5)

    def q_trs(r):
        kind, s3, _, _ = q_meta(r)
        trs = None
        for kc in range(NC):
            trs = p.emit(TE, lambda e, kk=kc:
                         e.transpose(tbank[0][:, 128 * kk:128 * (kk + 1)],
                                     qwib[:, 128 * kk:128 * (kk + 1)], idb[:, :]),
                         waits=[qrev[r][5], ld_id, brd.get(6) if kc == 0 else None],
                         inc=(kc == NC - 1))
        bwr[6] = trs
        est = p.emit(VE, lambda e, sl=r % 2: e.tensor_copy(qstage[sl][:, :], tbank[0][:, :]),
                     waits=[trs, qstage_rel[r % 2]])
        brd[6] = est
        qest_ev[r] = est
        col0 = 1024 * s3 if kind == "g" else 3072 + 1024 * s3
        sd = dma(SY, d_wqi[:, col0:col0 + 1024], qstage[r % 2][:, :],
                 waits=[est], stream="D3")
        stage_dmas.append(sd)
        qstage_rel[r % 2] = sd

    for r0 in range(min(S3, QR)):
        q_ld(r0)
    ldwd_ev = dma(SY, qwld_d[:, :], d_wdm[:, :],
                  waits=[last_bufmm, ev_bufs], stream="D1")
    qld_ev[6] = ldwd_ev
    qld_ev[7] = ldwd_ev
    qld_ev[8] = ldwd_ev
    for t in range(QR + 6):
        if 0 <= t - 6 < 6:
            q_trs(t - 6)
        if t < QR:
            q_chain(t)
        for s in range(5, -1, -1):
            r = t - s
            if 0 <= r < QR:
                qrev[r][s] = q_st(r, s)
        if t + S3 < 6:
            q_ld(t + S3)
    wq_done = qrev[8][5]
    for bt in range(3):
        ics = list(range(8 * bt, min(8 * (bt + 1), NI)))
        trs = None
        for idx, ic in enumerate(ics):
            trs = p.emit(TE, lambda e, c2=ic, ix=idx:
                         e.transpose(tbank[0][:, 128 * ix:128 * (ix + 1)],
                                     qwib[:, 128 * c2:128 * (c2 + 1)], idb[:, :]),
                         waits=[wq_done, ld_id, brd.get(6) if idx == 0 else None],
                         inc=(idx == len(ics) - 1))
        bwr[6] = trs
        nb = len(ics)
        ewt = p.emit(VE, lambda e, b0=8 * bt, nn=nb:
                     e.tensor_copy(qwdT[:, 128 * b0:128 * (b0 + nn)],
                                   tbank[0][:, 0:128 * nn]),
                     waits=[trs])
        brd[6] = ewt
    sd = dma(SY, d_wqi[:, 6144:8960], qwdT[:, :], waits=[ewt], stream="D3")
    stage_dmas.append(sd)
    ev_q7 = p.emit(VE, lambda e: e.memset(qsc[:, 7:8], 0.0), waits=[ev_bufs])
    sdc = dma(SY, d_sci[:, :], qsc[:, :], waits=qsc_evs + [ev_q7], stream="D3")
    qw_mem_rel = [stage_dmas[-1], sdc]
    scope_QW.close()

    scope_B0 = track(ExitStack())
    wo_sb = scope_B0.enter_context(nc.sbuf_tensor("wo_sb", [128, 8192], BF16, side="right"))
    attn_fm = scope_B0.enter_context(nc.sbuf_tensor("attn_fm", [128, 8 * T], BF16, side="right"))
    ld_wo = dma(G, wo_sb[:, :], d_wot[:, :],
                waits=[last_bufmm, ev_bufs] + qw_mem_rel)

    ev_boadj = None
    for g2 in range(2):
        mm = None
        for cc in range(NC):
            w = [ld_bv, ld_wo]
            if cc == 0:
                w.append(brd.get(4))
            mm = p.emit(TE, lambda e, c2=cc, g=g2, s=(cc == 0):
                        e.matmul(bank[4][0:1, 0:512], bvb[:, c2:c2 + 1],
                                 wo_sb[:, 1024 * c2 + 512 * g:1024 * c2 + 512 * (g + 1)],
                                 start=s, stop=(c2 == NC - 1)),
                        waits=w, inc=(cc == NC - 1))
        bwr[4] = mm
        ev_boadj = p.emit(VE, lambda e, g=g2:
                          e.scalar_tensor_tensor(bo_adj[0:1, 512 * g:512 * (g + 1)],
                                                 bank[4][0:1, 0:512], 1.0,
                                                 bo_sb[0:1, 512 * g:512 * (g + 1)],
                                                 ALU.mult, ALU.add),
                          waits=[mm, ld_bo])
        brd[4] = ev_boadj


    scope_B = track(ExitStack())
    sbr = lambda name, shape, dt: scope_B.enter_context(
        nc.sbuf_tensor(name, shape, dt, side="right"))
    qf_t2 = [[sbr(f"qf{ps}_{b}", [128, 2 * T], BF16) for b in (0, 1)] for ps in (0, 1)]
    flat_p2 = [[sbr(f"flatp{ps}_{r}", [1, T], BF16) for r in (0, 1)] for ps in (0, 1)]
    attn_pair = [sbr(f"apair{r}", [128, 128], BF16) for r in (0, 1)]
    rec_t = [sbr(f"rec{r}", [128, 2], F32) for r in (0, 1)]
    flat_rel = [None, None]
    qf_rel = [None, None]
    apair_rel = [None, None]
    ev_attn = None
    qf_done_ps = [[None, None], [None, None]]
    fls_ps = [[None, None], [None, None]]

    def emit_qf(hp):
        ps = hp % 2
        for b2 in range(2):
            h = 2 * hp + b2
            fls_ps[ps][b2] = dma(G, flat_p2[ps][b2][0:1, :], expnq[h:h + 1, :],
                                 waits=[ev_expnq, qf_rel[ps]] + qw_mem_rel)
            for mc in range(2):
                for j in range(NJ):
                    pbi = (mc * NJ + j) % 2
                    mm = p.emit(TE, lambda e, bb=b2, m=mc, jj=j, cc=hp, pbi2=pbi:
                                e.matmul(bank[pbi2][:, :],
                                         orf2x[64 * bb:64 * (bb + 1), 128 * m:128 * (m + 1)],
                                         q_sb[64 * bb:64 * (bb + 1),
                                              2048 * cc + 512 * jj:2048 * cc + 512 * (jj + 1)],
                                         start=True, stop=True),
                                waits=[ev_orf, last_q_ev, brd.get(pbi), qf_rel[ps]])
                    bwr[pbi] = mm
                    qf_done_ps[ps][b2] = p.emit(SE, lambda e, bb=b2, m=mc, jj=j, pbi2=pbi, ps2=ps:
                                                e.activation(qf_t2[ps2][bb][:, 2048 * m + 512 * jj:2048 * m + 512 * (jj + 1)],
                                                             bank[pbi2][:, :], AF.Exp,
                                                             bias=0.0, scale=1.0),
                                                waits=[mm])
                    brd[pbi] = qf_done_ps[ps][b2]

    emit_qf(0)
    ev_bufse = None
    for h in range(H):
        mm = None
        for mc in range(2):
            w = [ev_bufbf, ev_cst]
            if mc == 0:
                w.append(brd.get(4))
            mm = p.emit(TE, lambda e, hh=h, m=mc, s=(mc == 0):
                        e.matmul(bank[4][0:1, 0:65], ones_col[:, :],
                                 buf_bf[:, 130 * hh + 65 * m:130 * hh + 65 * (m + 1)],
                                 start=s, stop=(m == 1)),
                        waits=w, inc=(mc == 1))
        bwr[4] = mm
        ev_bufse = p.emit(SE, lambda e, hh=h:
                          e.activation(bufs_eps[0:1, 65 * hh:65 * (hh + 1)],
                                       bank[4][0:1, 0:65], AF.Identity,
                                       bias=0.0, scale=EPS16),
                          waits=[mm])
        brd[4] = ev_bufse

    for hp in range(8):
        pend_tr = None

        def emit_tr(ii0, div_ev):
            nonlocal ev_attn
            ar2 = ii0 % 2
            tb_i = ii0 % 2
            tr = p.emit(TE, lambda e, rr=ar2, tbi=tb_i:
                        e.transpose(tbank[tbi][:, 0:128], attn_pair[rr][:, :], idb[:, :]),
                        waits=[div_ev, ld_id, brd.get(6 + tb_i)])
            bwr[6 + tb_i] = tr
            e4 = p.emit(VE, lambda e, tbi=tb_i, cc=hp, ii=ii0:
                        e.tensor_copy(attn_fm[:, 2048 * cc + 128 * ii:2048 * cc + 128 * (ii + 1)],
                                      tbank[tbi][:, 0:128]),
                        waits=[tr])
            brd[6 + tb_i] = e4
            apair_rel[ar2] = e4
            ev_attn = e4

        if hp + 1 < 8:
            emit_qf(hp + 1)
        ps = hp % 2
        qf_done = qf_done_ps[ps]
        fls = fls_ps[ps]
        for i in range(NT):
            ob = 2 + i % 2
            od = None
            for b2 in range(2):
                h = 2 * hp + b2
                for mc in range(2):
                    w = [qf_done[b2], ev_bufbf]
                    if b2 == 0 and mc == 0:
                        w.append(brd.get(ob))
                    od = p.emit(TE, lambda e, bb=b2, m=mc, ii=i, hh=h, o=ob, s=(mc == 0), ps2=ps:
                                e.matmul(bank[o][:, 65 * bb:65 * (bb + 1)],
                                         qf_t2[ps2][bb][:, 2048 * m + 128 * ii:2048 * m + 128 * (ii + 1)],
                                         buf_bf[:, 130 * hh + 65 * m:130 * hh + 65 * (m + 1)],
                                         start=s, stop=False),
                                waits=w, inc=False)
                od = p.emit(TE, lambda e, bb=b2, ii=i, hh=h, o=ob, ps2=ps:
                            e.matmul(bank[o][:, 65 * bb:65 * (bb + 1)],
                                     flat_p2[ps2][bb][0:1, 128 * ii:128 * (ii + 1)],
                                     bufs_eps[0:1, 65 * hh:65 * (hh + 1)],
                                     start=False, stop=True),
                            waits=[fls[b2], ev_bufse], inc=(b2 == 1))
            bwr[ob] = od
            ar = i % 2
            e1 = p.emit(VE, lambda e, o=ob, rr=ar: e.reciprocal(rec_t[rr][:, 0:1], bank[o][:, 64:65]),
                        waits=[od, apair_rel[ar]])
            e1b = p.emit(VE, lambda e, o=ob, rr=ar: e.reciprocal(rec_t[rr][:, 1:2], bank[o][:, 129:130]),
                         waits=[od])
            e2 = p.emit(VE, lambda e, o=ob, rr=ar:
                        e.tensor_scalar(attn_pair[rr][:, 0:64], bank[o][:, 0:64],
                                        rec_t[rr][:, 0:1], None, ALU.mult),
                        waits=[e1])
            e3 = p.emit(VE, lambda e, o=ob, rr=ar:
                        e.tensor_scalar(attn_pair[rr][:, 64:128], bank[o][:, 65:129],
                                        rec_t[rr][:, 1:2], None, ALU.mult),
                        waits=[e1b])
            brd[ob] = e3
            if pend_tr is not None:
                emit_tr(*pend_tr)
            pend_tr = (i, e3)
        emit_tr(*pend_tr)
        qf_rel[0] = ev_attn
        qf_rel[1] = ev_attn
        flat_rel[0] = ev_attn
        flat_rel[1] = ev_attn
    DBG("attn_fm", attn_fm[:, :], [ev_attn])

    if upto == "B":
        _finish(p, nc, es, sems, inner_scopes)
        return nc

    scope_B.close()
    scope_Q.close()

    cc_w = p.emit(G, lambda e: e.collective_compute(
        "AllGather", ALU.bypass,
        replica_groups=[[0, 1, 2, 3, 4, 5, 6, 7]],
        ins=[h_wqi.ap()], outs=[h_wqo.ap()]),
        waits=stage_dmas + [sdc], sem="CC")
    cc_s = p.emit(G, lambda e: e.collective_compute(
        "AllGather", ALU.bypass,
        replica_groups=[[0, 1, 2, 3, 4, 5, 6, 7]],
        ins=[h_sci.ap()], outs=[h_sco.ap()]),
        waits=[sdc], sem="CC")
    sc_evs = []
    for s3 in range(3):
        cnt = 8 if s3 < 2 else 6
        sc_evs.append(p.emit(G, lambda e, s4=s3, c4=cnt: e.dma_start(
            out=swg_all[:, 8 * s4:8 * s4 + c4],
            in_=d_sco[0:128 * c4, s4:s4 + 1].rearrange("(c p) one -> p (c one)", p=128),
            allow_slow_non_contiguous=True),
            waits=[cc_s], dma=True, sem="DG"))
        sc_evs.append(p.emit(G, lambda e, s4=s3, c4=cnt: e.dma_start(
            out=swu_all[:, 8 * s4:8 * s4 + c4],
            in_=d_sco[0:128 * c4, 3 + s4:4 + s4].rearrange("(c p) one -> p (c one)", p=128),
            allow_slow_non_contiguous=True),
            waits=[cc_s], dma=True, sem="DG"))
    sc_evs.append(p.emit(G, lambda e: e.dma_start(
        out=swd_all[:, :],
        in_=d_sco[:, 6:7].rearrange("(c p) one -> p (c one)", p=128),
        allow_slow_non_contiguous=True),
        waits=[cc_s], dma=True, sem="DG"))
    ev_scales = sc_evs[-1]
    ev_swd = ev_scales

    scope_L = track(ExitStack())
    xs_t = [scope_L.enter_context(nc.sbuf_tensor(f"xs{r}", [128, 1024], F32, side="right"))
            for r in (0, 1)]
    x2_t = [scope_L.enter_context(nc.sbuf_tensor(f"x2t{r}", [128, 1024], F32, side="right"))
            for r in (0, 1)]
    xs_rel = [None, None]
    r_rel = [None, None]
    x2_rel = [(), ()]
    st_last2 = [None, None]
    last_stat_ev = [None, None]
    ev_ln1_dma = None
    ev_ln1 = None
    y1b_cycle = ((4, 5), (0, 1))

    for i in range(NT):
        rr = i % 2
        ldx = dma(SY, xs_t[rr][:, :], d_xtm[128 * i:128 * (i + 1), :],
                  waits=[xs_rel[rr], ev_attn if i < 2 else None],
                  stream=("D1" if rr == 0 else "D2"))
        y_banks = y1b_cycle[i % 2]
        add_evs = []
        for g2 in range(2):
            bidx = y_banks[g2]
            mm = None
            for cc in range(NC):
                st = cc == 0
                w = [ld_wo, ev_attn]
                if st:
                    w.append(brd.get(bidx))
                mm = p.emit(TE, lambda e, b3=bidx, c2=cc, g=g2, ii=i, s=st:
                            e.matmul(bank[b3][:, :],
                                     attn_fm[:, 2048 * c2 + 128 * ii:2048 * c2 + 128 * (ii + 1)],
                                     wo_sb[:, 1024 * c2 + 512 * g:1024 * c2 + 512 * (g + 1)],
                                     start=s, stop=False),
                            waits=w, inc=False)
            mm = p.emit(TE, lambda e, b3=bidx, g=g2:
                        e.matmul(bank[b3][:, :], ones_row[0:1, :],
                                 bo_adj[0:1, 512 * g:512 * (g + 1)],
                                 start=False, stop=True),
                        waits=[ev_boadj])
            bwr[bidx] = mm
            e1 = p.emit(VE, lambda e, b3=bidx, g=g2, r2=rr:
                        e.tensor_tensor(r_t[r2][:, 512 * g:512 * (g + 1)],
                                        bank[b3][:, :],
                                        xs_t[r2][:, 512 * g:512 * (g + 1)], ALU.add),
                        waits=[mm, ldx, r_rel[rr]])
            brd[bidx] = e1
            add_evs.append(e1)
        xs_rel[rr] = add_evs[-1]
        sb = 8 * rr
        s1 = p.emit(VE, lambda e, r2=rr, c0=sb: e.reduce_sum(st_t[:, c0:c0 + 1],
                                                             r_t[r2][:, :], axis=AX),
                    waits=add_evs + [last_stat_ev[rr]])
        sq = p.emit(SE, lambda e, r2=rr: e.activation(rsq_t[r2][:, :], r_t[r2][:, :],
                                                      AF.Square, bias=0.0, scale=1.0),
                    waits=add_evs)
        s2 = p.emit(VE, lambda e, r2=rr, c0=sb: e.reduce_sum(st_t[:, c0 + 1:c0 + 2],
                                                             rsq_t[r2][:, :], axis=AX),
                    waits=[sq])
        m1 = p.emit(VE, lambda e, c0=sb: e.tensor_scalar(st_t[:, c0 + 2:c0 + 3],
                                                         st_t[:, c0:c0 + 1],
                                                         1.0 / 1024.0, None, ALU.mult),
                    waits=[s1])
        m2 = p.emit(VE, lambda e, c0=sb: e.tensor_scalar(st_t[:, c0 + 3:c0 + 4],
                                                         st_t[:, c0 + 1:c0 + 2],
                                                         1.0 / 1024.0, None, ALU.mult),
                    waits=[s2])
        m3 = p.emit(VE, lambda e, c0=sb: e.tensor_tensor(st_t[:, c0 + 4:c0 + 5],
                                                         st_t[:, c0 + 2:c0 + 3],
                                                         st_t[:, c0 + 2:c0 + 3], ALU.mult),
                    waits=[m1])
        m4 = p.emit(VE, lambda e, c0=sb: e.tensor_tensor(st_t[:, c0 + 5:c0 + 6],
                                                         st_t[:, c0 + 3:c0 + 4],
                                                         st_t[:, c0 + 4:c0 + 5], ALU.subtract),
                    waits=[m2, m3])
        l1 = p.emit(SE, lambda e, c0=sb: e.activation(st_t[:, c0 + 6:c0 + 7],
                                                      st_t[:, c0 + 5:c0 + 6], AF.Ln,
                                                      bias=cst[:, 2:3], scale=1.0),
                    waits=[m4, ev_cst])
        l2 = p.emit(SE, lambda e, c0=sb: e.activation(st_t[:, c0 + 7:c0 + 8],
                                                      st_t[:, c0 + 6:c0 + 7], AF.Exp,
                                                      bias=0.0, scale=-0.5),
                    waits=[l1])
        mrs = p.emit(VE, lambda e, c0=sb: e.scalar_tensor_tensor(st_t[:, c0 + 4:c0 + 5],
                                                                 st_t[:, c0 + 2:c0 + 3],
                                                                 -1.0, st_t[:, c0 + 7:c0 + 8],
                                                                 ALU.mult, ALU.mult),
                     waits=[l2])
        ap1 = p.emit(SE, lambda e, r2=rr, c0=sb: e.activation(x2_t[r2][:, :], r_t[r2][:, :],
                                                              AF.Identity,
                                                              bias=st_t[:, c0 + 4:c0 + 5],
                                                              scale=st_t[:, c0 + 7:c0 + 8]),
                     waits=[mrs, s2] + list(x2_rel[rr]))
        last_stat_ev[rr] = ap1
        r_rel[rr] = ap1
        stx = dma(SY, d_x2[128 * i:128 * (i + 1), :], x2_t[rr][:, :], waits=[ap1],
                  stream=("D3" if rr == 0 else "D4"))
        mx = p.emit(VE, lambda e, r2=rr, ii=i:
                    e.reduce_max(runmax[:, ii:ii + 1], x2_t[r2][:, :], axis=AX,
                                 apply_absolute_value=True),
                    waits=[ap1])
        x2_rel[rr] = (stx, mx)
        st_last2[rr] = stx
        ev_ln1 = mx
    ev_ln1_dma = st_last2[1]
    ev_ln1_dma2 = st_last2[0]
    if "x2" in dbg:
        dx2o = nc.dram_tensor("dbg_x2", [T, 1024], F32, kind="ExternalOutput").ap()
        dbg_evs.append(dma(G, dx2o[:, :], d_x2[:, :], waits=[ev_ln1_dma]))

    rm1 = p.emit(VE, lambda e: e.reduce_max(rmx1[:, 0:1], runmax[:, :], axis=AX),
                 waits=[ev_ln1])
    sm1 = dma(SY, d_mx[:], rmx1[:, 0], waits=[rm1])
    lm1 = dma(SY, mrow[0:1, :], d_mx[None, :], waits=[sm1])
    rm2 = p.emit(VE, lambda e: e.reduce_max(scal1[0:1, 4:5], mrow[0:1, :], axis=AX),
                 waits=[lm1])
    sm2 = dma(SY, h_ccmi.ap()[None, :], scal1[0:1, 4:5], waits=[rm2])
    cc2 = p.emit(G, lambda e: e.collective_compute(
        "AllReduce", ALU.max,
        replica_groups=[[0, 1, 2, 3, 4, 5, 6, 7]],
        ins=[h_ccmi.ap()], outs=[h_ccmo.ap()]),
        waits=[sm2], sem="CC")
    lm2 = dma(SY, scal1[0:1, 5:6], h_ccmo.ap()[None, :], waits=[cc2])
    t1 = p.emit(VE, lambda e: e.tensor_tensor(scal1[0:1, 6:7], scal1[0:1, 5:6],
                                              sig_s[0:1, 0:1], ALU.mult),
                waits=[lm2, ev_sig])
    t2 = p.emit(VE, lambda e: e.tensor_scalar(scal1[0:1, 0:1], scal1[0:1, 6:7],
                                              1.0 / 127.0, None, ALU.mult),
                waits=[t1])
    t3 = p.emit(VE, lambda e: e.reciprocal(scal1[0:1, 1:2], scal1[0:1, 0:1]),
                waits=[t2])
    sm3 = dma(SY, d_scal[None, 0:2], scal1[0:1, 0:2], waits=[t3])
    bc1 = dma(SY, scal[:, 1:2], d_scal[None, 0:1].partition_broadcast(128), waits=[sm3])
    bc2 = dma(SY, scal[:, 0:1], d_scal[None, 1:2].partition_broadcast(128), waits=[sm3])
    ev_sx = bc1
    ev_sxinv = bc2
    if "scals" in dbg:
        dsc = nc.dram_tensor("dbg_scals", [1, 8], F32, kind="ExternalOutput").ap()
        dbg_evs.append(dma(SY, dsc[:, :], scal1[0:1, :], waits=[t3, rm2, lm2]))

    if upto == "LN1":
        _finish(p, nc, es, sems, inner_scopes)
        return nc

    scope_L.close()
    scope_B0.close()
    scope_R0.close()

    scope_F = track(ExitStack())
    sbf = lambda name, shape, dt: scope_F.enter_context(
        nc.sbuf_tensor(name, shape, dt, side="right"))
    xint_fm = sbf("xint_fm", [128, 8 * T], BF16)
    NR = 4
    ru_t = [sbf(f"ru{r}", [128, 1024], F32) for r in range(NR)]
    ri_t = [sbf(f"ri{r}", [128, 1024], I32) for r in range(NR)]
    rf_t = [sbf(f"rf{r}", [128, 1024], BF16) for r in range(NR)]
    rth_t = [sbf(f"rth{r}", [128, 1024], BF16) for r in range(NR)]
    rr1_t = [sbf(f"rr1{r}", [128, 1024], F32) for r in range(NR)]
    xq16 = [sbf(f"xq16{r}", [128, 1024], F32) for r in range(4)]
    xib = [sbf(f"xib{r}", [128, 1024], BF16) for r in (0, 1)]
    xb16a = [sbf(f"xb16a{r}", [128, 1024], BF16) for r in range(NR)]
    wgT = [sbf(f"wgT{r}", [128, 1024], BF16) for r in (0, 1, 2)]
    wuT = [sbf(f"wuT{r}", [128, 1024], BF16) for r in (0, 1, 2)]
    silu_t = [sbf(f"silu{r}", [128, 512], F32) for r in (0, 1)]
    hstg = [sbf(f"hstg{r}", [128, 2048], F32) for r in (0, 1)]
    NXR = NT
    xrev = [[None] * 9 for _ in range(NXR)]
    xld_ev = [None] * NXR
    xint_jev = [None] * NJ

    def x_ld(r):
        w = [ev_ln1_dma, ev_ln1_dma2]
        if r >= NR:
            w.append(xrev[r - NR][0])
        xld_ev[r] = dma(SY, xq16[r % NR][:, :], d_x2[128 * r:128 * (r + 1), :],
                        waits=w, stream=("D1" if r % 2 == 0 else "D4"))

    def x_st(r, s):
        sl = r % NR
        tb2 = r % 2
        u, i_, nf, th, r1 = ru_t[sl], ri_t[sl], rf_t[sl], rth_t[sl], rr1_t[sl]
        rel = lambda st2: (xrev[r - NR][st2] if r >= NR else None)
        if s == 0:
            return p.emit(SE, lambda e: e.activation(u[:, :], xq16[sl][:, :], AF.Identity,
                                                     bias=cst[:, 3:4], scale=scal[:, 0:1]),
                          waits=[xld_ev[r], ev_sx, ev_sxinv, ev_cst, rel(2)])
        if s == 1:
            return p.emit(G, lambda e: e.tensor_copy(i_[:, :], u[:, :]),
                          waits=[xrev[r][0], rel(4)])
        if s == 2:
            return p.emit(VE, lambda e: e.scalar_tensor_tensor(nf[:, :], i_[:, :], 1.0,
                                                               u[:, :], ALU.mult, ALU.subtract),
                          waits=[xrev[r][1], rel(3)])
        if s == 3:
            return p.emit(SE, lambda e: e.activation(th[:, :], nf[:, :], AF.Tanh,
                                                     bias=0.0, scale=TBIG),
                          waits=[xrev[r][2]])
        if s == 4:
            return p.emit(VE, lambda e: e.scalar_tensor_tensor(r1[:, :], th[:, :], -0.5,
                                                               i_[:, :], ALU.mult, ALU.add),
                          waits=[xrev[r][3], rel(5)])
        if s == 5:
            return p.emit(SE, lambda e: e.activation(xb16a[sl][:, :], r1[:, :], AF.Identity,
                                                     bias=cst[:, 4:5], scale=1.0),
                          waits=[xrev[r][4], rel(6)])
        if s == 6:
            w = [xrev[r][5]]
            if r >= 2:
                w.append(xrev[r - 2][7])
            return p.emit(VE, lambda e: e.tensor_scalar(xib[tb2][:, :], xb16a[sl][:, :],
                                                        127.0, None, ALU.min),
                          waits=w)
        if s == 7:
            trs = None
            for kc in range(NC):
                trs = p.emit(TE, lambda e, kk=kc:
                             e.transpose(tbank[tb2][:, 128 * kk:128 * (kk + 1)],
                                         xib[tb2][:, 128 * kk:128 * (kk + 1)], idb[:, :]),
                             waits=[xrev[r][6], ld_id,
                                    brd.get(6 + tb2) if kc == 0 else None],
                             inc=(kc == NC - 1))
            bwr[6 + tb2] = trs
            return trs
        ev = p.emit(VE, lambda e:
                    e.tensor_scalar(
                        xint_fm[:, :].rearrange("p (kc t) -> p kc t", t=2048)[:, :, 128 * r:128 * (r + 1)],
                        tbank[tb2][:, :].rearrange("p (kc t) -> p kc t", t=128),
                        -128.0, None, ALU.max),
                    waits=[xrev[r][7]])
        brd[6 + tb2] = ev
        return ev

    for r0 in range(min(NR, NXR)):
        x_ld(r0)
    for t in range(NXR + 9):
        for s in range(8, -1, -1):
            r = t - s
            if 0 <= r < NXR:
                xrev[r][s] = x_st(r, s)
        if t + NR < NXR:
            x_ld(t + NR)
    for j in range(NJ):
        xint_jev[j] = xrev[4 * j + 3][8]
    ev_xint = xrev[NXR - 1][8]
    DBG("xint_fm", xint_fm[:, :], [ev_xint])

    wgT_rel = [None, None, None]
    wuT_rel = [None, None, None]
    silu_rel = [None, None]
    hstg_rel = [None, None]
    ev_h = None
    ev_hstore = None

    e_sg = p.emit(VE, lambda e: e.tensor_scalar(sxswg[:, :], swg_all[:, :],
                                                scal[:, 1:2], None, ALU.mult),
                  waits=[ev_scales, ev_sx])
    e_su = p.emit(VE, lambda e: e.tensor_scalar(sxswu[:, :], swu_all[:, :],
                                                scal[:, 1:2], None, ALU.mult),
                  waits=[ev_scales, ev_sx])

    for ic in range(NI):
        slot = ic % 3
        cr = ic % 8
        sl3 = ic // 8
        ldg = dma(SY, wgT[slot][:, :],
                  d_wqo[128 * cr:128 * (cr + 1), 1024 * sl3:1024 * (sl3 + 1)],
                  waits=[cc_w, wgT_rel[slot]], stream="D2")
        ldu = dma(SY, wuT[slot][:, :],
                  d_wqo[128 * cr:128 * (cr + 1), 3072 + 1024 * sl3:3072 + 1024 * (sl3 + 1)],
                  waits=[cc_w, wuT_rel[slot]], stream="D3")
        h_tile_evs = []
        gmm = None
        umm = None
        for j in range(NJ):
            gb = (2 * (ic * NJ + j)) % 4
            ub = gb + 1
            for kc in range(NC):
                st = kc == 0
                w = [ldg, xint_jev[j]]
                if st:
                    w.append(brd.get(gb))
                gmm = p.emit(TE, lambda e, b3=gb, kk=kc, jj=j, sl=slot, s=st:
                             e.matmul(bank[b3][:, :],
                                      wgT[sl][:, 128 * kk:128 * (kk + 1)],
                                      xint_fm[:, 2048 * kk + 512 * jj:2048 * kk + 512 * (jj + 1)],
                                      start=s, stop=(kk == NC - 1)),
                             waits=w, inc=(kc == NC - 1))
            bwr[gb] = gmm
            for kc in range(NC):
                st = kc == 0
                w = [ldu, xint_jev[j]]
                if st:
                    w.append(brd.get(ub))
                umm = p.emit(TE, lambda e, b3=ub, kk=kc, jj=j, sl=slot, s=st:
                             e.matmul(bank[b3][:, :],
                                      wuT[sl][:, 128 * kk:128 * (kk + 1)],
                                      xint_fm[:, 2048 * kk + 512 * jj:2048 * kk + 512 * (jj + 1)],
                                      start=s, stop=(kk == NC - 1)),
                             waits=w, inc=(kc == NC - 1))
            bwr[ub] = umm
            sr = j % 2
            es1 = p.emit(SE, lambda e, b3=gb, c2=ic, sl2=sr:
                         e.activation(silu_t[sl2][:, :], bank[b3][:, :], AF.Silu,
                                      bias=0.0, scale=sxswg[:, c2:c2 + 1]),
                         waits=[gmm, e_sg, silu_rel[sr]])
            brd[gb] = es1
            hsl = ic % 2
            eh = p.emit(VE, lambda e, b3=ub, c2=ic, jj=j, sl2=sr, hs=hsl:
                        e.scalar_tensor_tensor(hstg[hs][:, 512 * jj:512 * (jj + 1)],
                                               bank[b3][:, :], sxswu[:, c2:c2 + 1],
                                               silu_t[sl2][:, :], ALU.mult, ALU.mult),
                        waits=[umm, es1, e_su] + ([hstg_rel[hsl]] if j == 0 else []))
            brd[ub] = eh
            silu_rel[sr] = eh
            h_tile_evs.append(eh)
            emx = p.emit(VE, lambda e, c2=ic, jj=j, hs=hsl:
                         e.reduce_max(runmax2[:, 4 * c2 + jj:4 * c2 + jj + 1],
                                      hstg[hs][:, 512 * jj:512 * (jj + 1)],
                                      axis=AX, apply_absolute_value=True),
                         waits=[eh])
            ev_h = emx
        ev_hstore = dma(SY, d_h[128 * ic:128 * (ic + 1), :], hstg[ic % 2][:, :],
                        waits=h_tile_evs + [ev_h], stream="D3")
        hstg_rel[ic % 2] = ev_hstore
        wgT_rel[slot] = gmm
        wuT_rel[slot] = umm
    if "hd" in dbg:
        dho = nc.dram_tensor("dbg_hd", [NI * 128, T], F32, kind="ExternalOutput").ap()
        dbg_evs.append(dma(G, dho[:, :], d_h[:, :], waits=[ev_hstore]))

    if upto == "GATE":
        _finish(p, nc, es, sems, inner_scopes)
        return nc

    rm1b = p.emit(VE, lambda e: e.reduce_max(rmx1[:, 1:2], runmax2[:, :], axis=AX),
                  waits=[ev_h])
    sm1b = dma(SY, d_mx2[:], rmx1[:, 1], waits=[rm1b])
    lm1b = dma(SY, mrow[0:1, :], d_mx2[None, :], waits=[sm1b, rm2])
    rm2b = p.emit(VE, lambda e: e.reduce_max(scal1[0:1, 4:5], mrow[0:1, :], axis=AX),
                  waits=[lm1b, t1, sm2])
    sm2b = dma(SY, h_ccmi2.ap()[None, :], scal1[0:1, 4:5], waits=[rm2b])
    cc3 = p.emit(G, lambda e: e.collective_compute(
        "AllReduce", ALU.max,
        replica_groups=[[0, 1, 2, 3, 4, 5, 6, 7]],
        ins=[h_ccmi2.ap()], outs=[h_ccmo2.ap()]),
        waits=[sm2b], sem="CC")
    lm2b = dma(SY, scal1[0:1, 5:6], h_ccmo2.ap()[None, :], waits=[cc3, t1])
    t1b = p.emit(VE, lambda e: e.tensor_tensor(scal1[0:1, 6:7], scal1[0:1, 5:6],
                                               sig_s[0:1, 1:2], ALU.mult),
                 waits=[lm2b, ev_sig, t2])
    t2b = p.emit(VE, lambda e: e.tensor_scalar(scal1[0:1, 2:3], scal1[0:1, 6:7],
                                               1.0 / 127.0, None, ALU.mult),
                 waits=[t1b])
    t3b = p.emit(VE, lambda e: e.reciprocal(scal1[0:1, 3:4], scal1[0:1, 2:3]),
                 waits=[t2b])
    sm3b = dma(SY, d_scal[None, 2:4], scal1[0:1, 2:4], waits=[t3b])
    bc3 = dma(SY, scal[:, 3:4], d_scal[None, 2:3].partition_broadcast(128), waits=[sm3b])
    bc4 = dma(SY, scal[:, 2:3], d_scal[None, 3:4].partition_broadcast(128), waits=[sm3b])
    ev_sh = bc3
    ev_shinv = bc4

    scope_F.close()

    scope_D = track(ExitStack())
    sbd = lambda name, shape, dt: scope_D.enter_context(
        nc.sbuf_tensor(name, shape, dt, side="right"))
    NDR = 4
    wdT_all = sbd("wdT_all", [128, NI * 1024], BF16)
    swd_row = sbd("swd_row", [128, 1024], F32)
    hint_tb = [sbd(f"hint_tb{r}", [128, NI * 512], BF16) for r in (0, 1)]
    du_t = [sbd(f"du{r}", [128, 512], F32) for r in range(NDR)]
    di_t = [sbd(f"di{r}", [128, 512], I32) for r in range(NDR)]
    df_t = [sbd(f"dfr{r}", [128, 512], BF16) for r in range(NDR)]
    dth_t = [sbd(f"dth{r}", [128, 512], BF16) for r in range(NDR)]
    dr1_t = [sbd(f"dr1{r}", [128, 512], F32) for r in range(NDR)]
    hsl_t = [sbd(f"hsl{r}", [128, 512], F32) for r in range(NDR)]
    hb16a = [sbd(f"hb16a{r}", [128, 512], BF16) for r in range(NDR)]
    hb16b = [sbd(f"hb16b{r}", [128, 512], BF16) for r in range(NDR)]
    y2s4 = [sbd(f"y2s{r}", [128, 1024], F32) for r in range(4)]
    x216 = [sbd(f"x216{r}", [128, 1024], F32) for r in range(4)]
    out_t = [sbd(f"outt{r}", [128, 1024], F32) for r in (0, 1)]

    smd = p.emit(SY, lambda e: e.dma_start(out=d_swd[:].rearrange("(c q) -> q c", q=128),
                                           in_=swd_all[:, :],
                                           allow_slow_non_contiguous=True),
                 waits=[ev_swd], dma=True, sem="D2")
    ev_wdT = None
    for cr in range(8):
        ev_wdT = dma(SY, wdT_all[:, :].rearrange("p (ic d) -> p ic d", d=1024)[:, :, 128 * cr:128 * (cr + 1)],
                     d_wqo[128 * cr:128 * (cr + 1), 6144:8960].rearrange("p (ic d) -> p ic d", d=128),
                     waits=[cc_w, ev_h, ev_hstore], stream="D2")
    DBG("wdT_all", wdT_all[:, :], [ev_wdT])
    bcd = dma(SY, swd_row[:, :], d_swd[None, :].partition_broadcast(128),
              waits=[smd, ev_hstore], stream="D2")
    ev_swdrow = bcd

    NRND = 4 * NI
    rev = [[None] * 8 for _ in range(NRND)]
    ld_ev = [None] * NRND
    dn_last = [None] * 4
    dmm_ev = [[None] * 8 for _ in range(4)]
    ey_ev = [[None] * 8 for _ in range(4)]
    x2_ld = [[None] * 4 for _ in range(4)]
    er_ev = [[None] * 4 for _ in range(4)]
    out_rel = [None, None]
    r2_rel = r_rel
    last_stat2 = [None]
    BPAIRS = ((0, 1), (2, 3), (4, 5))

    def emit_ld(r):
        tb, ic = r // NI, r % NI
        w = [ev_hstore]
        if r >= NDR:
            w.append(rev[r - NDR][0])
        ld_ev[r] = dma(SY, hsl_t[r % NDR][:, :],
                       d_h[128 * ic:128 * (ic + 1), 512 * tb:512 * (tb + 1)],
                       waits=w, stream=("D1", "D2", "D3", "D4")[r % 4])

    def st_h(r, s):
        tb, ic = r // NI, r % NI
        sl = r % NDR
        u, i_, nf, th, r1 = du_t[sl], di_t[sl], df_t[sl], dth_t[sl], dr1_t[sl]
        rel = lambda st2: (rev[r - NDR][st2] if r >= NDR else None)
        if s == 0:
            return p.emit(SE, lambda e: e.activation(u[:, :], hsl_t[sl][:, :], AF.Identity,
                                                     bias=cst[:, 3:4], scale=scal[:, 2:3]),
                          waits=[ld_ev[r], ev_sh, ev_shinv, rel(2)])
        if s == 1:
            return p.emit(G, lambda e: e.tensor_copy(i_[:, :], u[:, :]),
                          waits=[rev[r][0], rel(4)])
        if s == 2:
            return p.emit(VE, lambda e: e.scalar_tensor_tensor(nf[:, :], i_[:, :], 1.0,
                                                               u[:, :], ALU.mult, ALU.subtract),
                          waits=[rev[r][1], rel(3)])
        if s == 3:
            return p.emit(SE, lambda e: e.activation(th[:, :], nf[:, :], AF.Tanh,
                                                     bias=0.0, scale=TBIG),
                          waits=[rev[r][2]])
        if s == 4:
            return p.emit(VE, lambda e: e.scalar_tensor_tensor(r1[:, :], th[:, :], -0.5,
                                                               i_[:, :], ALU.mult, ALU.add),
                          waits=[rev[r][3], rel(5)])
        if s == 5:
            return p.emit(SE, lambda e: e.activation(hb16a[sl][:, :], r1[:, :], AF.Identity,
                                                     bias=cst[:, 4:5], scale=1.0),
                          waits=[rev[r][4], rel(6)])
        if s == 6:
            return p.emit(VE, lambda e: e.tensor_scalar(hb16b[sl][:, :], hb16a[sl][:, :],
                                                        127.0, None, ALU.min),
                          waits=[rev[r][5], rel(7)])
        w7 = [rev[r][6]]
        if tb >= 2:
            w7.append(dn_last[tb - 2])
        return p.emit(VE, lambda e: e.tensor_scalar(hint_tb[tb % 2][:, 512 * ic:512 * (ic + 1)],
                                                    hb16b[sl][:, :], -128.0, None, ALU.max),
                      waits=w7)

    def mk_down_group(tb, k):
        def go():
            j, dg = k // 2, k % 2
            i = 4 * tb + j
            bidx = BPAIRS[i % 3][dg]
            mm = None
            for ic in range(NI):
                st = ic == 0
                w = [rev[NI * tb + ic][7], ev_wdT]
                if st:
                    w.append(brd.get(bidx))
                mm = p.emit(TE, lambda e, b3=bidx, c2=ic, g=dg, tb2=tb, j2=j, s=st:
                            e.matmul(bank[b3][:, :],
                                     hint_tb[tb2 % 2][:, 512 * c2 + 128 * j2:512 * c2 + 128 * (j2 + 1)],
                                     wdT_all[:, 1024 * c2 + 512 * g:1024 * c2 + 512 * (g + 1)],
                                     start=s, stop=(c2 == NI - 1)),
                            waits=w, inc=(ic == NI - 1))
            bwr[bidx] = mm
            dmm_ev[tb][k] = mm
            if k == 7:
                dn_last[tb] = mm
        return go

    def mk_ey(tb, k):
        def go():
            j, dg = k // 2, k % 2
            i = 4 * tb + j
            bidx = BPAIRS[i % 3][dg]
            w = [dmm_ev[tb][k], ev_sh, ev_swdrow]
            if tb >= 1:
                w.append(er_ev[tb - 1][j])
            ey = p.emit(VE, lambda e: e.scalar_tensor_tensor(y2s4[j][:, 512 * dg:512 * (dg + 1)],
                                                             bank[bidx][:, :], scal[:, 3:4],
                                                             swd_row[:, 512 * dg:512 * (dg + 1)],
                                                             ALU.mult, ALU.mult),
                        waits=w)
            brd[bidx] = ey
            ey_ev[tb][k] = ey
        return go

    def mk_ldx2(tb, j):
        def go():
            i = 4 * tb + j
            w = [ev_ln1_dma, ev_ln1_dma2, ev_hstore]
            if tb >= 1:
                w.append(er_ev[tb - 1][j])
            x2_ld[tb][j] = dma(SY, x216[j][:, :], d_x2[128 * i:128 * (i + 1), :],
                               waits=w, stream="D1")
        return go

    def mk_ln2(tb, j):
        def go():
            i = 4 * tb + j
            rr = i % 2
            er = p.emit(G, lambda e: e.tensor_tensor(r_t[rr][:, :], y2s4[j][:, :],
                                                     x216[j][:, :], ALU.add),
                        waits=[ey_ev[tb][2 * j], ey_ev[tb][2 * j + 1], x2_ld[tb][j],
                               r2_rel[rr]])
            er_ev[tb][j] = er
            s1 = p.emit(VE, lambda e: e.reduce_sum(st_t[:, 0:1], r_t[rr][:, :], axis=AX),
                        waits=[er, last_stat2[0]])
            sq = p.emit(SE, lambda e: e.activation(rsq_t[rr][:, :], r_t[rr][:, :],
                                                   AF.Square, bias=0.0, scale=1.0),
                        waits=[er])
            s2 = p.emit(VE, lambda e: e.reduce_sum(st_t[:, 1:2], rsq_t[rr][:, :], axis=AX),
                        waits=[sq])
            m1 = p.emit(VE, lambda e: e.tensor_scalar(st_t[:, 2:3], st_t[:, 0:1],
                                                      1.0 / 1024.0, None, ALU.mult),
                        waits=[s1])
            m2 = p.emit(VE, lambda e: e.tensor_scalar(st_t[:, 3:4], st_t[:, 1:2],
                                                      1.0 / 1024.0, None, ALU.mult),
                        waits=[s2])
            m3 = p.emit(VE, lambda e: e.tensor_tensor(st_t[:, 4:5], st_t[:, 2:3],
                                                      st_t[:, 2:3], ALU.mult),
                        waits=[m1])
            m4 = p.emit(VE, lambda e: e.tensor_tensor(st_t[:, 5:6], st_t[:, 3:4],
                                                      st_t[:, 4:5], ALU.subtract),
                        waits=[m2, m3])
            l1 = p.emit(SE, lambda e: e.activation(st_t[:, 6:7], st_t[:, 5:6], AF.Ln,
                                                   bias=cst[:, 2:3], scale=1.0),
                        waits=[m4])
            l2 = p.emit(SE, lambda e: e.activation(st_t[:, 7:8], st_t[:, 6:7], AF.Exp,
                                                   bias=0.0, scale=-0.5),
                        waits=[l1])
            mrs = p.emit(VE, lambda e: e.scalar_tensor_tensor(st_t[:, 4:5], st_t[:, 2:3],
                                                              -1.0, st_t[:, 7:8],
                                                              ALU.mult, ALU.mult),
                         waits=[l2])
            ap = p.emit(SE, lambda e: e.activation(out_t[i % 2][:, :], r_t[rr][:, :],
                                                   AF.Identity, bias=st_t[:, 4:5],
                                                   scale=st_t[:, 7:8]),
                        waits=[mrs, out_rel[i % 2]])
            last_stat2[0] = ap
            r2_rel[rr] = ap
            fs = dma(SY, d_out[128 * i:128 * (i + 1), :], out_t[i % 2][:, :],
                     waits=[ap], stream="D4")
            out_rel[i % 2] = fs
        return go

    trig = [NI * tb + NI + 7 for tb in range(4)]
    sched = {}

    def at(t, fn):
        sched.setdefault(t, []).append(fn)

    for tb in range(4):
        for k in range(8):
            at(trig[tb] + 2 * k, mk_down_group(tb, k))
            at(trig[tb] + 2 * k + 3, mk_ey(tb, k))
        for j in range(4):
            at(trig[tb] + 5 * j + 1, mk_ldx2(tb, j))
        for j in range(4):
            at(trig[tb] + NI + 5 * j, mk_ln2(tb, j))
    maxt = max(sched)
    for r0 in range(min(NDR, NRND)):
        emit_ld(r0)
    for t in range(max(NRND + 8, maxt + 1)):
        for fn in sched.get(t, ()):
            fn()
        for s in range(7, -1, -1):
            r = t - s
            if 0 <= r < NRND:
                rev[r][s] = st_h(r, s)
        if t + NDR < NRND:
            emit_ld(t + NDR)


    _finish(p, nc, es, sems, inner_scopes)
    return nc


def _finish(p, nc, es, sems, inner_scopes=()):
    for sc in reversed(list(inner_scopes)):
        sc.close()
    with nc.Block() as block:
        def mk_runner(engine_name):
            def _run(eng):
                for need, fn, sem, by in p.ops[engine_name]:
                    for s, v in need:
                        eng.wait_ge(sems[s], v)
                    ins = fn(eng)
                    if sem is not None:
                        ins.then_inc(sems[sem], by)
                if engine_name == "sync":
                    for s in ("DS", "DG", "D1", "D2", "D3", "D4"):
                        if p.counts.get(s):
                            eng.wait_ge(sems[s], p.counts[s])
            return _run
        block.tensor(mk_runner("tensor"))
        block.vector(mk_runner("vector"))
        block.scalar(mk_runner("scalar"))
        block.gpsimd(mk_runner("gpsimd"))
        block.sync(mk_runner("sync"))
    es.close()


_NC_CACHE = {}


def _get_nc(upto="ALL", dbg=()):
    key = (upto, tuple(sorted(dbg)))
    if key not in _NC_CACHE:
        _NC_CACHE[key] = build_nc(upto, dbg)
    return _NC_CACHE[key]


def _host_prep(inputs):
    x = np.asarray(inputs["L0_attn_x"], np.float32)
    Wq = np.asarray(inputs["L0_attn_Wq"], np.float32)
    Wk = np.asarray(inputs["L0_attn_Wk"], np.float32)
    Wv = np.asarray(inputs["L0_attn_Wv"], np.float32)
    Wo = np.asarray(inputs["L0_attn_Wo"], np.float32)
    orf = np.asarray(inputs["L0_attn_orf"], np.float32)

    def prep_wqk(W):
        a = W.T.reshape(8, 128, 8, 128)
        return np.ascontiguousarray(a.transpose(1, 2, 0, 3).reshape(128, 8192))

    def prep_wvo(W):
        a = W.T.reshape(8, 128, 1024)
        return np.ascontiguousarray(a.transpose(1, 0, 2).reshape(128, 8192))

    indh = np.zeros((128, 128), np.float32)
    for p_ in range(128):
        for c in range(8):
            indh[p_, 16 * c + 2 * c + (1 if p_ >= 64 else 0)] = 1.0

    shared = {
        "wqt": prep_wqk(Wq),
        "wkt": prep_wqk(Wk),
        "wvt": prep_wvo(Wv),
        "wot": prep_wvo(Wo),
        "orft": np.ascontiguousarray(orf.T),
        "identh": np.eye(128, dtype=np.float32),
        "indh": indh,
        "bqh": np.ascontiguousarray(np.asarray(inputs["L0_attn_bq"], np.float32).reshape(8, 128).T),
        "bkh": np.ascontiguousarray(np.asarray(inputs["L0_attn_bk"], np.float32).reshape(8, 128).T),
        "bvh": np.ascontiguousarray(np.asarray(inputs["L0_attn_bv"], np.float32).reshape(8, 128).T),
        "boh": np.asarray(inputs["L0_attn_bo"], np.float32).reshape(1, 1024),
        "clip_x": np.asarray(inputs["L0_ffn_clip_x"], np.float32).reshape(1, 1),
        "clip_down": np.asarray(inputs["L0_ffn_clip_down"], np.float32).reshape(1, 1),
    }
    wg = np.asarray(inputs["L0_ffn_w_gate"], np.float32)
    wu = np.asarray(inputs["L0_ffn_w_up"], np.float32)
    wd = np.asarray(inputs["L0_ffn_w_down"], np.float32)

    in_maps = []
    for core in range(8):
        b = core // 2
        hf = core % 2
        xs = np.ascontiguousarray(x[b, T * hf:T * (hf + 1), :])
        xfm = np.ascontiguousarray(xs.T.reshape(8, 128, T).transpose(1, 0, 2).reshape(128, 8 * T))
        m = dict(shared)
        m["x_tm"] = xs
        m["x_fm"] = xfm
        wgm = np.ones((384, 1024), np.float32)
        wum = np.ones((384, 1024), np.float32)
        for s in range(3):
            ic = core + 8 * s
            if ic < NI:
                wgm[128 * s:128 * (s + 1)] = wg[128 * ic:128 * (ic + 1)]
                wum[128 * s:128 * (s + 1)] = wu[128 * ic:128 * (ic + 1)]
        m["wgq_my"] = wgm
        m["wuq_my"] = wum
        m["wdq_my"] = np.ascontiguousarray(wd[128 * core:128 * (core + 1)])
        in_maps.append(m)
    return in_maps


def kernel(**inputs):
    in_maps = _host_prep(inputs)
    nc = _get_nc("ALL", ())
    res = run_bass_kernel_spmd(nc, in_maps, core_ids=list(range(8)))
    outs = [res.results[c]["out"] for c in range(8)]
    full = np.stack(outs, axis=0).reshape(4, 2, T, 1024).reshape(4, 4096, 1024)
    return np.ascontiguousarray(full.astype(np.float32))


# version 19
# speedup vs baseline: 1.0314x; 1.0170x over previous
from contextlib import ExitStack

import numpy as np

import concourse.bass as bass
import concourse.mybir as mybir
from concourse.bass_utils import run_bass_kernel_spmd

F32 = mybir.dt.float32
BF16 = mybir.dt.bfloat16
FP16 = mybir.dt.float16
I32 = mybir.dt.int32
AF = mybir.ActivationFunctionType
ALU = mybir.AluOpType
AX = mybir.AxisListType.X

D4 = 64.0 ** 0.25
LN16 = float(np.log(16.0))
EPS16 = 1e-6 / 16.0
LN_EPS = 1e-5
RBIAS = 1023.5
TBIG = 1e9

T = 2048
NT = 16
NJ = 4
NC = 8
H = 16
I_FF = 2816
NI = 22

ENGINES = ("tensor", "vector", "scalar", "gpsimd", "sync")
ESEM = {"tensor": "T", "vector": "V", "scalar": "S", "gpsimd": "G", "sync": "SY"}


class Prog:
    def __init__(self):
        self.ops = {e: [] for e in ENGINES}
        self.counts = {}
        self.hw = {}

    def emit(self, engine, fn, waits=(), dma=False, inc=True, sem=None):
        by = 1
        if sem is None:
            if dma:
                sem = "DG" if engine == "gpsimd" else "DS"
                by = 16
            elif inc:
                sem = ESEM[engine]
        elif dma:
            by = 16
        waits = list(waits)
        if dma and self.counts.get(sem, 0) > 0:
            waits.append((sem, self.counts[sem]))
        need = []
        for ev in waits:
            if ev is None:
                continue
            s, v = ev
            if self.hw.get((engine, s), 0) < v:
                need.append((s, v))
                self.hw[(engine, s)] = v
        ev_out = None
        if sem is not None:
            self.counts[sem] = self.counts.get(sem, 0) + by
            ev_out = (sem, self.counts[sem])
        self.ops[engine].append((need, fn, sem, by))
        return ev_out


def build_nc(upto="ALL", dbg=()):
    nc = bass.Bass("TRN2", target_bir_lowering=False, debug=False)
    p = Prog()

    def din(name, shape):
        return nc.dram_tensor(name, shape, F32, kind="ExternalInput").ap()

    d_xfm = din("x_fm", [128, 8 * T])
    d_xtm = din("x_tm", [T, 1024])
    d_wqt = din("wqt", [128, 8192])
    d_wkt = din("wkt", [128, 8192])
    d_wvt = din("wvt", [128, 8192])
    d_wot = din("wot", [128, 8192])
    d_orft = din("orft", [64, 256])
    d_ind = din("indh", [128, 128])
    d_ident = din("identh", [128, 128])
    d_bq = din("bqh", [128, 8])
    d_bk = din("bkh", [128, 8])
    d_bv = din("bvh", [128, 8])
    d_bo = din("boh", [1, 1024])
    d_clipx = din("clip_x", [1, 1])
    d_clipd = din("clip_down", [1, 1])
    d_wgm = din("wgq_my", [384, 1024])
    d_wum = din("wuq_my", [384, 1024])
    d_wdm = din("wdq_my", [128, I_FF])
    d_out = nc.dram_tensor("out", [T, 1024], F32, kind="ExternalOutput").ap()

    h_cc1i = nc.dram_tensor("cc1i", [128, 2080], F32)
    h_cc1o = nc.dram_tensor("cc1o", [128, 2080], F32)
    d_cc1i, d_cc1o = h_cc1i.ap(), h_cc1o.ap()
    d_mx = nc.dram_tensor("mxd", [128], F32).ap()
    h_ccmi = nc.dram_tensor("ccmi", [1], F32)
    h_ccmo = nc.dram_tensor("ccmo", [1], F32)
    d_mx2 = nc.dram_tensor("mxd2", [128], F32).ap()
    h_ccmi2 = nc.dram_tensor("ccmi2", [1], F32)
    h_ccmo2 = nc.dram_tensor("ccmo2", [1], F32)
    d_scal = nc.dram_tensor("scald", [8], F32).ap()
    d_x2 = nc.dram_tensor("x2d", [T, 1024], F32).ap()
    d_h = nc.dram_tensor("hd", [NI * 128, T], F32).ap()
    d_swd = nc.dram_tensor("swdd", [1024], F32).ap()
    h_wqi = nc.dram_tensor("wqi", [128, 8960], BF16)
    h_wqo = nc.dram_tensor("wqo", [1024, 8960], BF16)
    h_sci = nc.dram_tensor("sci", [128, 8], F32)
    h_sco = nc.dram_tensor("sco", [1024, 8], F32)
    d_wqi, d_wqo = h_wqi.ap(), h_wqo.ap()
    d_sci, d_sco = h_sci.ap(), h_sco.ap()

    es = ExitStack()
    inner_scopes = []

    def track(sc):
        inner_scopes.append(sc)
        return sc

    def SBL(name, shape, dt):
        return es.enter_context(nc.sbuf_tensor(name, shape, dt, side="left"))

    sems = {}
    for s in ("T", "V", "S", "G", "SY", "DG", "DS", "CC", "D1", "D2", "D3", "D4"):
        sems[s] = es.enter_context(nc.semaphore(s))

    bank = [es.enter_context(nc.psum_tensor(f"pb{i}", [128, 512], F32)) for i in range(6)]
    tbank = [es.enter_context(nc.psum_tensor(f"tb{i}", [128, 1024], BF16)) for i in range(2)]
    bwr = {}
    brd = {}

    cst = SBL("cst", [128, 8], F32)
    ones_col = SBL("ones_col", [128, 1], BF16)
    ones_row = SBL("ones_row", [1, 128], BF16)
    idb = SBL("idb", [128, 128], BF16)
    ind = SBL("ind", [128, 128], BF16)
    orf2x = SBL("orf2x", [128, 256], BF16)
    bqf = SBL("bqf", [128, 8], F32)
    bkf = SBL("bkf", [128, 8], F32)
    bvb = SBL("bvb", [128, 8], BF16)
    bo_sb = SBL("bo_sb", [1, 1024], BF16)
    bo_adj = SBL("bo_adj", [1, 1024], BF16)
    sig_s = SBL("sig_s", [1, 2], F32)
    clip_sb = SBL("clip_sb", [1, 2], F32)
    scal = SBL("scal", [128, 4], F32)
    scal1 = SBL("scal1", [1, 8], F32)
    mrow = SBL("mrow", [1, 128], F32)
    runmax = SBL("runmax", [128, NT], F32)
    runmax2 = SBL("runmax2", [128, NI * NJ], F32)
    rmx1 = SBL("rmx1", [128, 2], F32)
    swg_all = SBL("swg_all", [128, NI], F32)
    swu_all = SBL("swu_all", [128, NI], F32)
    sxswg = SBL("sxswg", [128, NI], F32)
    sxswu = SBL("sxswu", [128, NI], F32)
    swd_all = SBL("swd_all", [128, 8], F32)
    st_t = SBL("st_t", [128, 16], F32)
    r_t = [SBL(f"rt{r}", [128, 1024], F32) for r in (0, 1)]
    rsq_t = [SBL(f"rsq{r}", [128, 1024], F32) for r in (0, 1)]
    wst = SBL("wst", [128, 4], F32)

    G, SY, TE, VE, SE = "gpsimd", "sync", "tensor", "vector", "scalar"
    dbg_evs = []

    def DBG(name, ap, after_list):
        if name in dbg:
            d = nc.dram_tensor("dbg_" + name, list(ap.shape), F32,
                               kind="ExternalOutput").ap()
            q = SY if ap.dtype == F32 else G
            dbg_evs.append(p.emit(q, lambda e, dd=d, aa=ap: e.dma_start(out=dd, in_=aa),
                                  waits=after_list, dma=True))

    def dma(q, out, in_, waits=(), stream=None):
        return p.emit(q, lambda e, o=out, i=in_: e.dma_start(out=o, in_=i),
                      waits=waits, dma=True, sem=stream)

    scope_Q = track(ExitStack())
    q_sb = scope_Q.enter_context(nc.sbuf_tensor("q_sb", [128, 8 * T], BF16, side="left"))
    scope_K = track(ExitStack())
    k_sb = scope_K.enter_context(nc.sbuf_tensor("k_sb", [128, 8 * T], BF16, side="left"))
    scope_R0 = track(ExitStack())
    sbr0 = lambda name, shape, dt: scope_R0.enter_context(
        nc.sbuf_tensor(name, shape, dt, side="right"))
    expnq = sbr0("expnq", [16, 2048], BF16)
    negnk = sbr0("negnk", [128, 256], F32)
    vs_eps = sbr0("vs_eps", [1, 1040], BF16)
    bufs_eps = sbr0("bufs_eps", [1, 1040], BF16)
    buf_sb = sbr0("buf_sb", [128, 2080], F32)
    buf_bf = sbr0("buf_bf", [128, 2080], BF16)
    scope_RV = track(ExitStack())
    v_ext = scope_RV.enter_context(nc.sbuf_tensor("v_ext", [128, NT * 1040], BF16, side="right"))
    scope_P1 = track(ExitStack())
    x_fm = scope_P1.enter_context(nc.sbuf_tensor("x_fm_s", [128, 8 * T], BF16, side="left"))
    orf_f = scope_P1.enter_context(nc.sbuf_tensor("orf_f", [128, 256], F32, side="left"))
    sqq = [scope_P1.enter_context(nc.sbuf_tensor(f"sqq{r}", [128, 512], BF16, side="left"))
           for r in range(8)]
    sqk = sqq
    scope_P2 = track(ExitStack())
    wqk_sb = scope_P2.enter_context(nc.sbuf_tensor("wqk_sb", [128, 8192], BF16, side="left"))

    ld_xfm = dma(G, x_fm[:, :], d_xfm[:, :])
    ld_wq = dma(G, wqk_sb[:, :], d_wqt[:, :])
    ld_orf0 = dma(G, orf_f[0:64, :], d_orft[:, :])
    ld_orf1 = dma(G, orf_f[64:128, :], d_orft[:, :])
    ld_ind = dma(G, ind[:, :], d_ind[:, :])
    ld_id = dma(G, idb[:, :], d_ident[:, :])
    ld_bq = dma(G, bqf[:, :], d_bq[:, :])
    ld_bk = dma(G, bkf[:, :], d_bk[:, :])
    ld_bv = dma(G, bvb[:, :], d_bv[:, :])
    ld_bo = dma(G, bo_sb[:, :], d_bo[:, :])
    ld_cl = dma(G, clip_sb[0:1, 0:1], d_clipx[:, :])
    ld_cl2 = dma(G, clip_sb[0:1, 1:2], d_clipd[:, :])

    ev_c = []
    for col, val in ((0, LN16), (1, -LN16), (2, LN_EPS), (3, RBIAS), (4, -RBIAS)):
        ev_c.append(p.emit(VE, lambda e, c=col, v=val: e.memset(cst[:, c:c + 1], v)))
    ev_c.append(p.emit(VE, lambda e: e.memset(ones_col[:, :], 1.0)))
    ev_c.append(p.emit(VE, lambda e: e.memset(ones_row[:, :], 1.0)))
    ev_cst = ev_c[-1]

    ev_wmb = p.emit(VE, lambda e: e.memset(buf_bf[:, 0:512], 0.0))
    warm_last = None
    for wi in range(120):
        warm_last = p.emit(TE, lambda e: e.matmul(bank[5][:, :], idb[:, 0:128],
                                                  buf_bf[:, 0:512],
                                                  start=True, stop=True),
                           waits=([ld_id, ev_wmb] if wi == 0 else []),
                           inc=(wi == 119))
    brd[5] = warm_last

    ev_orf = p.emit(SE, lambda e: e.activation(orf2x[:, :], orf_f[:, :], AF.Identity,
                                               bias=0.0, scale=1.0 / D4),
                    waits=[ld_orf0, ld_orf1])
    ev_sig = p.emit(SE, lambda e: e.activation(sig_s[0:1, 0:2], clip_sb[0:1, 0:2],
                                               AF.Sigmoid, bias=0.0, scale=1.0),
                    waits=[ld_cl, ld_cl2])

    sq_rel = [None] * 8
    last_q_ev = None
    last_k_ev = None
    last_qk_mm = None
    ev_expnq = None
    ev_negnk = None
    ld_wk = None
    for wpass in ("q", "k"):
        if wpass == "k":
            ld_wk = dma(G, wqk_sb[:, :], d_wkt[:, :], waits=[last_qk_mm])
        for j in range(NJ):
            sq_evs = []
            for c in range(NC):
                bidx = (j * NC + c) % 2 + (0 if wpass == "q" else 2)
                b_ap = bqf if wpass == "q" else bkf
                dst = q_sb if wpass == "q" else k_sb
                sqt = sqq[c]
                ldw = ld_wq if wpass == "q" else ld_wk
                mm = None
                for kc in range(NC):
                    st = kc == 0
                    w = [ld_xfm, ldw]
                    if st:
                        w.append(brd.get(bidx))
                    mm = p.emit(TE, lambda e, b=bidx, kk=kc, cc=c, jj=j,
                                s=st, sp=(kc == NC - 1):
                                e.matmul(bank[b][:, :],
                                         wqk_sb[:, 1024 * cc + 128 * kk:1024 * cc + 128 * (kk + 1)],
                                         x_fm[:, 2048 * kk + 512 * jj:2048 * kk + 512 * (jj + 1)],
                                         start=s, stop=sp),
                                waits=w, inc=(kc == NC - 1))
                bwr[bidx] = mm
                last_qk_mm = mm
                e1 = p.emit(SE, lambda e, b=bidx, d=dst, ba=b_ap, cc=c, jj=j:
                            e.activation(d[:, 2048 * cc + 512 * jj:2048 * cc + 512 * (jj + 1)],
                                         bank[b][:, :], AF.Identity,
                                         bias=ba[:, cc:cc + 1], scale=1.0),
                            waits=[mm, ld_bq, ld_bk])
                e2 = p.emit(VE, lambda e, d=dst, cc=c, jj=j, sq=sqt:
                            e.tensor_tensor(sq[:, :],
                                            d[:, 2048 * cc + 512 * jj:2048 * cc + 512 * (jj + 1)],
                                            d[:, 2048 * cc + 512 * jj:2048 * cc + 512 * (jj + 1)],
                                            ALU.mult),
                            waits=[e1, sq_rel[c]])
                brd[bidx] = e1
                sq_evs.append(e2)
                if wpass == "q":
                    last_q_ev = e1
                else:
                    last_k_ev = e1
            if wpass == "q":
                n_last = None
                for c in range(NC):
                    n_last = p.emit(TE, lambda e, cc=c, sq=sqq[c], s=(c == 0):
                                    e.matmul(bank[4][0:16, :], ind[:, 16 * cc:16 * (cc + 1)],
                                             sq[:, :], start=s, stop=(cc == NC - 1)),
                                    waits=[sq_evs[c], ld_ind, brd.get(4) if c == 0 else None],
                                    inc=(c == NC - 1))
                for c in range(NC):
                    sq_rel[c] = n_last
                bwr[4] = n_last
                ev_expnq = p.emit(SE, lambda e, jj=j:
                                  e.activation(expnq[:, 512 * jj:512 * (jj + 1)],
                                               bank[4][0:16, :], AF.Exp,
                                               bias=cst[0:16, 0:1], scale=1.0 / 16.0),
                                  waits=[n_last, ev_cst])
                brd[4] = ev_expnq
            else:
                n_lasts = [None] * 4
                for c in range(NC):
                    for ts_ in range(4):
                        nb = (4, 5, 0, 1)[ts_]
                        n_lasts[ts_] = p.emit(TE, lambda e, cc=c, sq=sqq[c], ts=ts_, b4=nb,
                                              s=(c == 0):
                                              e.matmul(bank[b4][:, 0:16],
                                                       sq[:, 128 * ts:128 * (ts + 1)],
                                                       ind[:, 16 * cc:16 * (cc + 1)],
                                                       start=s, stop=(cc == NC - 1)),
                                              waits=[sq_evs[c], ld_ind,
                                                     brd.get(nb) if c == 0 else None],
                                              inc=(c == NC - 1))
                    sq_rel[c] = n_lasts[3] if c == NC - 1 else sq_rel[c]
                for c in range(NC):
                    sq_rel[c] = n_lasts[3]
                for ts_ in range(4):
                    nb = (4, 5, 0, 1)[ts_]
                    bwr[nb] = n_lasts[ts_]
                    ev_negnk = p.emit(SE, lambda e, jj=j, ts=ts_, b4=nb:
                                      e.activation(negnk[:, 64 * jj + 16 * ts:64 * jj + 16 * (ts + 1)],
                                                   bank[b4][:, 0:16], AF.Identity,
                                                   bias=cst[:, 1:2], scale=-1.0 / 16.0),
                                      waits=[n_lasts[ts_], ev_cst])
                    brd[nb] = ev_negnk
    DBG("q_sb", q_sb[:, :], [last_q_ev])
    DBG("k_sb", k_sb[:, :], [last_k_ev])
    DBG("expnq", expnq[:, :], [ev_expnq])
    DBG("negnk", negnk[:, :], [ev_negnk])

    scope_P2.close()
    scope_P3 = track(ExitStack())
    wv_sb = scope_P3.enter_context(nc.sbuf_tensor("wv_sb", [128, 8192], BF16, side="left"))
    ld_wv = dma(G, wv_sb[:, :], d_wvt[:, :], waits=[last_qk_mm])

    last_vevac = None
    ev_vdone = None
    vq = []
    vs_state = [None]

    def emit_vsum(ii0, vev):
        for seg, (lo, n, bk2) in enumerate(((0, 512, 4), (512, 512, 5), (1024, 16, 3))):
            w = [vev, ev_cst]
            if ii0 == 0:
                w.append(brd.get(bk2))
            vs_state[0] = p.emit(TE, lambda e, ii=ii0, l=lo, nn=n, b2=bk2, s=(ii0 == 0):
                                 e.matmul(bank[b2][0:1, 0:nn], ones_col[:, :],
                                          v_ext[:, 1040 * ii + l:1040 * ii + l + nn],
                                          start=s, stop=(ii == NT - 1)),
                                 waits=w, inc=(ii0 == NT - 1 and seg == 2))

    for i in range(NT):
        evs2 = []
        for dg in range(2):
            bidx = (2 * i + dg) % 2
            mm = None
            for kc in range(NC):
                st = kc == 0
                w = [ld_xfm, ld_wv]
                if st:
                    w.append(brd.get(bidx))
                mm = p.emit(TE, lambda e, b=bidx, kk=kc, ii=i, d=dg, s=st,
                            sp=(kc == NC - 1):
                            e.matmul(bank[b][:, :],
                                     x_fm[:, 2048 * kk + 128 * ii:2048 * kk + 128 * (ii + 1)],
                                     wv_sb[:, 1024 * kk + 512 * d:1024 * kk + 512 * (d + 1)],
                                     start=s, stop=sp),
                            waits=w, inc=(kc == NC - 1))
            bwr[bidx] = mm
            base = 1040 * i + 520 * dg
            e1 = p.emit(VE, lambda e, b=bidx, bs=base:
                        e.tensor_copy(
                            v_ext[:, bs:bs + 520].rearrange("p (h d) -> p h d", d=65)[:, :, 0:64],
                            bank[b][:, :].rearrange("p (h d) -> p h d", d=64)),
                        waits=[mm])
            e2 = p.emit(VE, lambda e, bs=base:
                        e.memset(v_ext[:, bs:bs + 520].rearrange("p (h d) -> p h d", d=65)[:, :, 64:65], 1.0),
                        waits=[e1])
            brd[bidx] = e2
            evs2.append(e2)
        last_vevac = evs2[-1]
        vq.append(evs2[-1])
        if i >= 1:
            emit_vsum(i - 1, vq[i - 1])
    emit_vsum(NT - 1, vq[NT - 1])
    ev_vdone = vs_state[0]
    for bk2 in (4, 5, 3):
        bwr[bk2] = ev_vdone
    p.emit(SE, lambda e: e.activation(vs_eps[0:1, 0:512], bank[4][0:1, 0:512],
                                      AF.Identity, bias=0.0, scale=EPS16),
           waits=[ev_vdone])
    p.emit(SE, lambda e: e.activation(vs_eps[0:1, 512:1024], bank[5][0:1, 0:512],
                                      AF.Identity, bias=0.0, scale=EPS16),
           waits=[ev_vdone])
    ev_vseps = p.emit(SE, lambda e: e.activation(vs_eps[0:1, 1024:1040], bank[3][0:1, 0:16],
                                                 AF.Identity, bias=0.0, scale=EPS16),
                      waits=[ev_vdone])
    brd[4] = ev_vseps
    brd[5] = ev_vseps
    brd[3] = ev_vseps
    DBG("v_ext", v_ext[:, :], [last_vevac])
    DBG("vs_eps", vs_eps[:, :], [ev_vseps])

    if upto == "P":
        _finish(p, nc, es, sems, inner_scopes)
        return nc

    scope_P3.close()
    scope_P1.close()

    scope_A = track(ExitStack())
    kf_t = [scope_A.enter_context(nc.sbuf_tensor(f"kf{r}", [128, 256], BF16, side="right"))
            for r in (0, 1, 2)]
    kf_rel = [None, None, None]
    ev_bufs = None
    buf_done = None
    for h in range(H):
        b = h % 2
        c = h // 2
        pend = None
        for i in range(NT):
            r = i % 3
            pb_i = i % 2
            mm = p.emit(TE, lambda e, bb=b, cc=c, ii=i, pbi=pb_i:
                        e.matmul(bank[pbi][:, 0:256],
                                 k_sb[64 * bb:64 * (bb + 1),
                                      2048 * cc + 128 * ii:2048 * cc + 128 * (ii + 1)],
                                 orf2x[64 * bb:64 * (bb + 1), :],
                                 start=True, stop=True),
                        waits=[ev_orf, last_k_ev, ev_negnk, brd.get(pb_i), kf_rel[r]])
            bwr[pb_i] = mm
            ef = p.emit(SE, lambda e, rr=r, pbi=pb_i, ii=i, hh=h:
                        e.activation(kf_t[rr][:, :], bank[pbi][:, 0:256], AF.Exp,
                                     bias=negnk[:, 16 * ii + hh:16 * ii + hh + 1],
                                     scale=1.0),
                        waits=[mm])
            brd[pb_i] = ef
            if pend is not None:
                pi, pr, pef = pend
                for mc in range(2):
                    w = [pef, last_vevac]
                    if pi == 0:
                        w.append(brd.get(2 + mc))
                    buf_done = p.emit(TE, lambda e, rr=pr, m=mc, ii=pi, hh=h,
                                      s=(pi == 0):
                                      e.matmul(bank[2 + m][:, 0:65],
                                               kf_t[rr][:, 128 * m:128 * (m + 1)],
                                               v_ext[:, 1040 * ii + 65 * hh:1040 * ii + 65 * (hh + 1)],
                                               start=s, stop=False),
                                      waits=w, inc=(mc == 1))
                kf_rel[pr] = buf_done
            pend = (i, r, ef)
        pi, pr, pef = pend
        for mc in range(2):
            w = [pef, last_vevac]
            buf_done = p.emit(TE, lambda e, rr=pr, m=mc, ii=pi, hh=h:
                              e.matmul(bank[2 + m][:, 0:65],
                                       kf_t[rr][:, 128 * m:128 * (m + 1)],
                                       v_ext[:, 1040 * ii + 65 * hh:1040 * ii + 65 * (hh + 1)],
                                       start=False, stop=False),
                              waits=w, inc=(mc == 1))
            kf_rel[pr] = buf_done
        for mc in range(2):
            buf_done = p.emit(TE, lambda e, m=mc, hh=h:
                              e.matmul(bank[2 + m][:, 0:65],
                                       ones_row[0:1, :],
                                       vs_eps[0:1, 65 * hh:65 * (hh + 1)],
                                       start=False, stop=True),
                              waits=[ev_vseps], inc=True)
            bwr[2 + mc] = buf_done
        ev_bufs = None
        for mc in range(2):
            ev_bufs = p.emit(VE, lambda e, hh=h, m=mc:
                             e.tensor_copy(buf_sb[:, 130 * hh + 65 * m:130 * hh + 65 * (m + 1)],
                                           bank[2 + m][:, 0:65]),
                             waits=[bwr[2 + mc]])
            brd[2 + mc] = ev_bufs
    last_bufmm = buf_done
    DBG("buf_loc", buf_sb[:, :], [ev_bufs])

    if upto == "A":
        _finish(p, nc, es, sems, inner_scopes)
        return nc

    scope_A.close()
    scope_RV.close()
    scope_K.close()

    st1 = dma(SY, d_cc1i[:, :], buf_sb[:, :], waits=[ev_bufs], stream="D2")
    cc1 = p.emit(G, lambda e: e.collective_compute(
        "AllReduce", ALU.add,
        replica_groups=[[0, 1], [2, 3], [4, 5], [6, 7]],
        ins=[h_cc1i.ap()], outs=[h_cc1o.ap()]),
        waits=[st1], sem="CC")
    ld1 = dma(SY, buf_sb[:, :], d_cc1o[:, :], waits=[cc1, ev_bufs], stream="D2")
    ev_bufbf = p.emit(VE, lambda e: e.tensor_copy(buf_bf[:, :], buf_sb[:, :]),
                      waits=[ld1, warm_last])
    DBG("buf_bf", buf_bf[:, :], [ev_bufbf])

    scope_QW = track(ExitStack())
    sbq = lambda name, shape, dt: scope_QW.enter_context(
        nc.sbuf_tensor(name, shape, dt, side="right"))
    S3 = 3
    qw_ld = [sbq(f"qwld{r}", [128, 1024], F32) for r in range(S3)]
    qru = [sbq(f"qru{r}", [128, 1024], F32) for r in range(S3)]
    qri = [sbq(f"qri{r}", [128, 1024], I32) for r in range(S3)]
    qrf = [sbq(f"qrf{r}", [128, 1024], BF16) for r in range(S3)]
    qrth = [sbq(f"qrth{r}", [128, 1024], BF16) for r in range(S3)]
    qrr = [sbq(f"qrr{r}", [128, 1024], F32) for r in range(S3)]
    qwib = sbq("qwib", [128, I_FF], BF16)
    qstage = [sbq(f"qstage{r}", [128, 1024], BF16) for r in (0, 1)]
    qsc = sbq("qsc", [128, 8], F32)
    qst = sbq("qst", [128, 20], F32)
    qwld_d = sbq("qwld_d", [128, I_FF], F32)
    qwdT = sbq("qwdT", [128, 22 * 128], BF16)

    QR = 9
    qrev = [[None] * 6 for _ in range(QR)]
    qld_ev = [None] * QR
    qch_ev = [None] * QR
    qest_ev = [None] * QR
    qsc_evs = []
    stage_dmas = []
    qstage_rel = [None, None]

    def q_meta(r):
        if r < 6:
            kind = "g" if r < 3 else "u"
            s3 = r % 3
            return kind, s3, 0, 1024
        lo, n = ((0, 1024), (1024, 1024), (2048, 768))[r - 6]
        return "d", r - 6, lo, n

    def q_ld(r):
        w = [last_bufmm, ev_bufs]
        if r >= S3:
            w.append(qrev[r - S3][0])
        kind, s3, _, _ = q_meta(r)
        srcd = d_wgm if kind == "g" else d_wum
        qld_ev[r] = dma(SY, qw_ld[r % S3][:, :], srcd[128 * s3:128 * (s3 + 1), :],
                        waits=w, stream="D2")

    def q_chain(r):
        kind, s3, lo, n = q_meta(r)
        if kind == "d":
            if r != 6:
                qch_ev[r] = qch_ev[6]
                return
            e1 = p.emit(VE, lambda e: e.reduce_max(qst[:, 10:11], qwld_d[:, :], axis=AX,
                                                   apply_absolute_value=True),
                        waits=[qld_ev[6]])
            qsc_evs.append(p.emit(VE, lambda e: e.tensor_scalar(qsc[:, 6:7], qst[:, 10:11],
                                                                1.0 / 127.0, None, ALU.mult),
                                  waits=[e1]))
            e3 = p.emit(VE, lambda e: e.reciprocal(qst[:, 11:12], qst[:, 10:11]), waits=[e1])
            qch_ev[r] = p.emit(VE, lambda e: e.tensor_scalar(qst[:, 11:12], qst[:, 11:12],
                                                             127.0, None, ALU.mult),
                               waits=[e3])
            return
        sc = (0 if kind == "g" else 3) + s3
        e1 = p.emit(VE, lambda e, sl=r % S3: e.reduce_max(qst[:, r:r + 1], qw_ld[sl][:, :],
                                                          axis=AX, apply_absolute_value=True),
                    waits=[qld_ev[r]])
        qsc_evs.append(p.emit(VE, lambda e, c=sc: e.tensor_scalar(qsc[:, c:c + 1],
                                                                  qst[:, r:r + 1],
                                                                  1.0 / 127.0, None, ALU.mult),
                              waits=[e1]))
        e3 = p.emit(VE, lambda e: e.reciprocal(qst[:, 12 + r:13 + r], qst[:, r:r + 1]),
                    waits=[e1])
        qch_ev[r] = p.emit(VE, lambda e: e.tensor_scalar(qst[:, 12 + r:13 + r],
                                                         qst[:, 12 + r:13 + r], 127.0,
                                                         None, ALU.mult),
                           waits=[e3])

    qwinv = [None] * QR

    def q_st(r, s):
        kind, s3, lo, n = q_meta(r)
        sl = r % S3
        u = qru[sl][:, 0:n]
        i_ = qri[sl][:, 0:n]
        nf = qrf[sl][:, 0:n]
        th = qrth[sl][:, 0:n]
        r1 = qrr[sl][:, 0:n]
        rel = lambda st2: (qrev[r - S3][st2] if r >= S3 else None)
        if s == 0:
            src = qw_ld[sl][:, :] if kind != "d" else qwld_d[:, lo:lo + n]
            scol = qst[:, 12 + r:13 + r] if kind != "d" else qst[:, 11:12]
            return p.emit(SE, lambda e: e.activation(u, src, AF.Identity,
                                                     bias=cst[:, 3:4], scale=scol),
                          waits=[qch_ev[r], ev_cst, rel(2)])
        if s == 1:
            return p.emit(G, lambda e: e.tensor_copy(i_, u), waits=[qrev[r][0], rel(4)])
        if s == 2:
            return p.emit(VE, lambda e: e.scalar_tensor_tensor(nf, i_, 1.0, u,
                                                               ALU.mult, ALU.subtract),
                          waits=[qrev[r][1], rel(3)])
        if s == 3:
            return p.emit(SE, lambda e: e.activation(th, nf, AF.Tanh, bias=0.0, scale=TBIG),
                          waits=[qrev[r][2]])
        if s == 4:
            return p.emit(VE, lambda e: e.scalar_tensor_tensor(r1, th, -0.5, i_,
                                                               ALU.mult, ALU.add),
                          waits=[qrev[r][3], rel(5)])
        w5 = [qrev[r][4]]
        if r >= 1 and (r < 6 or r == 6):
            w5.append(qest_ev[r - 1] if r < 6 else qest_ev[5])
        return p.emit(SE, lambda e: e.activation(qwib[:, lo:lo + n], r1, AF.Identity,
                                                 bias=cst[:, 4:5], scale=1.0),
                      waits=w5)

    def q_trs(r):
        kind, s3, _, _ = q_meta(r)
        trs = None
        for kc in range(NC):
            trs = p.emit(TE, lambda e, kk=kc:
                         e.transpose(tbank[0][:, 128 * kk:128 * (kk + 1)],
                                     qwib[:, 128 * kk:128 * (kk + 1)], idb[:, :]),
                         waits=[qrev[r][5], ld_id, brd.get(6) if kc == 0 else None],
                         inc=(kc == NC - 1))
        bwr[6] = trs
        est = p.emit(VE, lambda e, sl=r % 2: e.tensor_copy(qstage[sl][:, :], tbank[0][:, :]),
                     waits=[trs, qstage_rel[r % 2]])
        brd[6] = est
        qest_ev[r] = est
        col0 = 1024 * s3 if kind == "g" else 3072 + 1024 * s3
        sd = dma(SY, d_wqi[:, col0:col0 + 1024], qstage[r % 2][:, :],
                 waits=[est], stream="D3")
        stage_dmas.append(sd)
        qstage_rel[r % 2] = sd

    for r0 in range(min(S3, QR)):
        q_ld(r0)
    ldwd_ev = dma(SY, qwld_d[:, :], d_wdm[:, :],
                  waits=[last_bufmm, ev_bufs], stream="D1")
    qld_ev[6] = ldwd_ev
    qld_ev[7] = ldwd_ev
    qld_ev[8] = ldwd_ev
    for t in range(QR + 6):
        if 0 <= t - 6 < 6:
            q_trs(t - 6)
        if t < QR:
            q_chain(t)
        for s in range(5, -1, -1):
            r = t - s
            if 0 <= r < QR:
                qrev[r][s] = q_st(r, s)
        if t + S3 < 6:
            q_ld(t + S3)
    wq_done = qrev[8][5]
    for bt in range(3):
        ics = list(range(8 * bt, min(8 * (bt + 1), NI)))
        trs = None
        for idx, ic in enumerate(ics):
            trs = p.emit(TE, lambda e, c2=ic, ix=idx:
                         e.transpose(tbank[0][:, 128 * ix:128 * (ix + 1)],
                                     qwib[:, 128 * c2:128 * (c2 + 1)], idb[:, :]),
                         waits=[wq_done, ld_id, brd.get(6) if idx == 0 else None],
                         inc=(idx == len(ics) - 1))
        bwr[6] = trs
        nb = len(ics)
        ewt = p.emit(VE, lambda e, b0=8 * bt, nn=nb:
                     e.tensor_copy(qwdT[:, 128 * b0:128 * (b0 + nn)],
                                   tbank[0][:, 0:128 * nn]),
                     waits=[trs])
        brd[6] = ewt
    sd = dma(SY, d_wqi[:, 6144:8960], qwdT[:, :], waits=[ewt], stream="D3")
    stage_dmas.append(sd)
    ev_q7 = p.emit(VE, lambda e: e.memset(qsc[:, 7:8], 0.0), waits=[ev_bufs])
    sdc = dma(SY, d_sci[:, :], qsc[:, :], waits=qsc_evs + [ev_q7], stream="D3")
    qw_mem_rel = [stage_dmas[-1], sdc]
    scope_QW.close()

    scope_B0 = track(ExitStack())
    wo_sb = scope_B0.enter_context(nc.sbuf_tensor("wo_sb", [128, 8192], BF16, side="right"))
    attn_fm = scope_B0.enter_context(nc.sbuf_tensor("attn_fm", [128, 8 * T], BF16, side="right"))
    ld_wo = dma(G, wo_sb[:, :], d_wot[:, :],
                waits=[last_bufmm, ev_bufs] + qw_mem_rel)

    ev_boadj = None
    for g2 in range(2):
        mm = None
        for cc in range(NC):
            w = [ld_bv, ld_wo]
            if cc == 0:
                w.append(brd.get(4))
            mm = p.emit(TE, lambda e, c2=cc, g=g2, s=(cc == 0):
                        e.matmul(bank[4][0:1, 0:512], bvb[:, c2:c2 + 1],
                                 wo_sb[:, 1024 * c2 + 512 * g:1024 * c2 + 512 * (g + 1)],
                                 start=s, stop=(c2 == NC - 1)),
                        waits=w, inc=(cc == NC - 1))
        bwr[4] = mm
        ev_boadj = p.emit(VE, lambda e, g=g2:
                          e.scalar_tensor_tensor(bo_adj[0:1, 512 * g:512 * (g + 1)],
                                                 bank[4][0:1, 0:512], 1.0,
                                                 bo_sb[0:1, 512 * g:512 * (g + 1)],
                                                 ALU.mult, ALU.add),
                          waits=[mm, ld_bo])
        brd[4] = ev_boadj


    scope_B = track(ExitStack())
    sbr = lambda name, shape, dt: scope_B.enter_context(
        nc.sbuf_tensor(name, shape, dt, side="right"))
    qf_t2 = [[sbr(f"qf{ps}_{b}", [128, 2 * T], BF16) for b in (0, 1)] for ps in (0, 1)]
    flat_p2 = [[sbr(f"flatp{ps}_{r}", [1, T], BF16) for r in (0, 1)] for ps in (0, 1)]
    attn_pair = [sbr(f"apair{r}", [128, 128], BF16) for r in (0, 1)]
    rec_t = [sbr(f"rec{r}", [128, 2], F32) for r in (0, 1)]
    flat_rel = [None, None]
    qf_rel = [None, None]
    apair_rel = [None, None]
    ev_attn = None
    qf_done_ps = [[None, None], [None, None]]
    fls_ps = [[None, None], [None, None]]

    def emit_qf(hp):
        ps = hp % 2
        for b2 in range(2):
            h = 2 * hp + b2
            fls_ps[ps][b2] = dma(G, flat_p2[ps][b2][0:1, :], expnq[h:h + 1, :],
                                 waits=[ev_expnq, qf_rel[ps]] + qw_mem_rel)
            for mc in range(2):
                for j in range(NJ):
                    pbi = (mc * NJ + j) % 2
                    mm = p.emit(TE, lambda e, bb=b2, m=mc, jj=j, cc=hp, pbi2=pbi:
                                e.matmul(bank[pbi2][:, :],
                                         orf2x[64 * bb:64 * (bb + 1), 128 * m:128 * (m + 1)],
                                         q_sb[64 * bb:64 * (bb + 1),
                                              2048 * cc + 512 * jj:2048 * cc + 512 * (jj + 1)],
                                         start=True, stop=True),
                                waits=[ev_orf, last_q_ev, brd.get(pbi), qf_rel[ps]])
                    bwr[pbi] = mm
                    qf_done_ps[ps][b2] = p.emit(SE, lambda e, bb=b2, m=mc, jj=j, pbi2=pbi, ps2=ps:
                                                e.activation(qf_t2[ps2][bb][:, 2048 * m + 512 * jj:2048 * m + 512 * (jj + 1)],
                                                             bank[pbi2][:, :], AF.Exp,
                                                             bias=0.0, scale=1.0),
                                                waits=[mm])
                    brd[pbi] = qf_done_ps[ps][b2]

    emit_qf(0)
    ev_bufse = None
    for h in range(H):
        mm = None
        for mc in range(2):
            w = [ev_bufbf, ev_cst]
            if mc == 0:
                w.append(brd.get(4))
            mm = p.emit(TE, lambda e, hh=h, m=mc, s=(mc == 0):
                        e.matmul(bank[4][0:1, 0:65], ones_col[:, :],
                                 buf_bf[:, 130 * hh + 65 * m:130 * hh + 65 * (m + 1)],
                                 start=s, stop=(m == 1)),
                        waits=w, inc=(mc == 1))
        bwr[4] = mm
        ev_bufse = p.emit(SE, lambda e, hh=h:
                          e.activation(bufs_eps[0:1, 65 * hh:65 * (hh + 1)],
                                       bank[4][0:1, 0:65], AF.Identity,
                                       bias=0.0, scale=EPS16),
                          waits=[mm])
        brd[4] = ev_bufse

    for hp in range(8):
        pend_tr = None

        def emit_tr(ii0, div_ev):
            nonlocal ev_attn
            ar2 = ii0 % 2
            tb_i = ii0 % 2
            tr = p.emit(TE, lambda e, rr=ar2, tbi=tb_i:
                        e.transpose(tbank[tbi][:, 0:128], attn_pair[rr][:, :], idb[:, :]),
                        waits=[div_ev, ld_id, brd.get(6 + tb_i)])
            bwr[6 + tb_i] = tr
            e4 = p.emit(VE, lambda e, tbi=tb_i, cc=hp, ii=ii0:
                        e.tensor_copy(attn_fm[:, 2048 * cc + 128 * ii:2048 * cc + 128 * (ii + 1)],
                                      tbank[tbi][:, 0:128]),
                        waits=[tr])
            brd[6 + tb_i] = e4
            apair_rel[ar2] = e4
            ev_attn = e4

        if hp + 1 < 8:
            emit_qf(hp + 1)
        ps = hp % 2
        qf_done = qf_done_ps[ps]
        fls = fls_ps[ps]
        for i in range(NT):
            ob = 2 + i % 2
            od = None
            for b2 in range(2):
                h = 2 * hp + b2
                for mc in range(2):
                    w = [qf_done[b2], ev_bufbf]
                    if b2 == 0 and mc == 0:
                        w.append(brd.get(ob))
                    od = p.emit(TE, lambda e, bb=b2, m=mc, ii=i, hh=h, o=ob, s=(mc == 0), ps2=ps:
                                e.matmul(bank[o][:, 65 * bb:65 * (bb + 1)],
                                         qf_t2[ps2][bb][:, 2048 * m + 128 * ii:2048 * m + 128 * (ii + 1)],
                                         buf_bf[:, 130 * hh + 65 * m:130 * hh + 65 * (m + 1)],
                                         start=s, stop=False),
                                waits=w, inc=False)
                od = p.emit(TE, lambda e, bb=b2, ii=i, hh=h, o=ob, ps2=ps:
                            e.matmul(bank[o][:, 65 * bb:65 * (bb + 1)],
                                     flat_p2[ps2][bb][0:1, 128 * ii:128 * (ii + 1)],
                                     bufs_eps[0:1, 65 * hh:65 * (hh + 1)],
                                     start=False, stop=True),
                            waits=[fls[b2], ev_bufse], inc=(b2 == 1))
            bwr[ob] = od
            ar = i % 2
            e1 = p.emit(VE, lambda e, o=ob, rr=ar: e.reciprocal(rec_t[rr][:, 0:1], bank[o][:, 64:65]),
                        waits=[od, apair_rel[ar]])
            e1b = p.emit(VE, lambda e, o=ob, rr=ar: e.reciprocal(rec_t[rr][:, 1:2], bank[o][:, 129:130]),
                         waits=[od])
            e2 = p.emit(VE, lambda e, o=ob, rr=ar:
                        e.tensor_scalar(attn_pair[rr][:, 0:64], bank[o][:, 0:64],
                                        rec_t[rr][:, 0:1], None, ALU.mult),
                        waits=[e1])
            e3 = p.emit(VE, lambda e, o=ob, rr=ar:
                        e.tensor_scalar(attn_pair[rr][:, 64:128], bank[o][:, 65:129],
                                        rec_t[rr][:, 1:2], None, ALU.mult),
                        waits=[e1b])
            brd[ob] = e3
            if pend_tr is not None:
                emit_tr(*pend_tr)
            pend_tr = (i, e3)
        emit_tr(*pend_tr)
        qf_rel[0] = ev_attn
        qf_rel[1] = ev_attn
        flat_rel[0] = ev_attn
        flat_rel[1] = ev_attn
    DBG("attn_fm", attn_fm[:, :], [ev_attn])

    if upto == "B":
        _finish(p, nc, es, sems, inner_scopes)
        return nc

    scope_B.close()
    scope_Q.close()

    cc_w = p.emit(G, lambda e: e.collective_compute(
        "AllGather", ALU.bypass,
        replica_groups=[[0, 1, 2, 3, 4, 5, 6, 7]],
        ins=[h_wqi.ap()], outs=[h_wqo.ap()]),
        waits=stage_dmas + [sdc], sem="CC")
    cc_s = p.emit(G, lambda e: e.collective_compute(
        "AllGather", ALU.bypass,
        replica_groups=[[0, 1, 2, 3, 4, 5, 6, 7]],
        ins=[h_sci.ap()], outs=[h_sco.ap()]),
        waits=[sdc], sem="CC")
    sc_evs = []
    for s3 in range(3):
        cnt = 8 if s3 < 2 else 6
        sc_evs.append(p.emit(G, lambda e, s4=s3, c4=cnt: e.dma_start(
            out=swg_all[:, 8 * s4:8 * s4 + c4],
            in_=d_sco[0:128 * c4, s4:s4 + 1].rearrange("(c p) one -> p (c one)", p=128),
            allow_slow_non_contiguous=True),
            waits=[cc_s], dma=True, sem="DG"))
        sc_evs.append(p.emit(G, lambda e, s4=s3, c4=cnt: e.dma_start(
            out=swu_all[:, 8 * s4:8 * s4 + c4],
            in_=d_sco[0:128 * c4, 3 + s4:4 + s4].rearrange("(c p) one -> p (c one)", p=128),
            allow_slow_non_contiguous=True),
            waits=[cc_s], dma=True, sem="DG"))
    sc_evs.append(p.emit(G, lambda e: e.dma_start(
        out=swd_all[:, :],
        in_=d_sco[:, 6:7].rearrange("(c p) one -> p (c one)", p=128),
        allow_slow_non_contiguous=True),
        waits=[cc_s], dma=True, sem="DG"))
    ev_scales = sc_evs[-1]
    ev_swd = ev_scales

    scope_L = track(ExitStack())
    xs_t = [scope_L.enter_context(nc.sbuf_tensor(f"xs{r}", [128, 1024], F32, side="right"))
            for r in (0, 1)]
    x2_t = [scope_L.enter_context(nc.sbuf_tensor(f"x2t{r}", [128, 1024], F32, side="right"))
            for r in (0, 1)]
    xs_rel = [None, None]
    r_rel = [None, None]
    x2_rel = [(), ()]
    st_last2 = [None, None]
    last_stat_ev = [None, None]
    ev_ln1_dma = None
    ev_ln1 = None
    y1b_cycle = ((4, 5), (0, 1))

    for i in range(NT):
        rr = i % 2
        ldx = dma(SY, xs_t[rr][:, :], d_xtm[128 * i:128 * (i + 1), :],
                  waits=[xs_rel[rr], ev_attn if i < 2 else None],
                  stream=("D1" if rr == 0 else "D2"))
        y_banks = y1b_cycle[i % 2]
        add_evs = []
        for g2 in range(2):
            bidx = y_banks[g2]
            mm = None
            for cc in range(NC):
                st = cc == 0
                w = [ld_wo, ev_attn]
                if st:
                    w.append(brd.get(bidx))
                mm = p.emit(TE, lambda e, b3=bidx, c2=cc, g=g2, ii=i, s=st:
                            e.matmul(bank[b3][:, :],
                                     attn_fm[:, 2048 * c2 + 128 * ii:2048 * c2 + 128 * (ii + 1)],
                                     wo_sb[:, 1024 * c2 + 512 * g:1024 * c2 + 512 * (g + 1)],
                                     start=s, stop=False),
                            waits=w, inc=False)
            mm = p.emit(TE, lambda e, b3=bidx, g=g2:
                        e.matmul(bank[b3][:, :], ones_row[0:1, :],
                                 bo_adj[0:1, 512 * g:512 * (g + 1)],
                                 start=False, stop=True),
                        waits=[ev_boadj])
            bwr[bidx] = mm
            e1 = p.emit(VE, lambda e, b3=bidx, g=g2, r2=rr:
                        e.tensor_tensor(r_t[r2][:, 512 * g:512 * (g + 1)],
                                        bank[b3][:, :],
                                        xs_t[r2][:, 512 * g:512 * (g + 1)], ALU.add),
                        waits=[mm, ldx, r_rel[rr]])
            brd[bidx] = e1
            add_evs.append(e1)
        xs_rel[rr] = add_evs[-1]
        sb = 8 * rr
        s1 = p.emit(VE, lambda e, r2=rr, c0=sb: e.reduce_sum(st_t[:, c0:c0 + 1],
                                                             r_t[r2][:, :], axis=AX),
                    waits=add_evs + [last_stat_ev[rr]])
        sq = p.emit(SE, lambda e, r2=rr: e.activation(rsq_t[r2][:, :], r_t[r2][:, :],
                                                      AF.Square, bias=0.0, scale=1.0),
                    waits=add_evs)
        s2 = p.emit(VE, lambda e, r2=rr, c0=sb: e.reduce_sum(st_t[:, c0 + 1:c0 + 2],
                                                             rsq_t[r2][:, :], axis=AX),
                    waits=[sq])
        m1 = p.emit(VE, lambda e, c0=sb: e.tensor_scalar(st_t[:, c0 + 2:c0 + 3],
                                                         st_t[:, c0:c0 + 1],
                                                         1.0 / 1024.0, None, ALU.mult),
                    waits=[s1])
        m2 = p.emit(VE, lambda e, c0=sb: e.tensor_scalar(st_t[:, c0 + 3:c0 + 4],
                                                         st_t[:, c0 + 1:c0 + 2],
                                                         1.0 / 1024.0, None, ALU.mult),
                    waits=[s2])
        m3 = p.emit(VE, lambda e, c0=sb: e.tensor_tensor(st_t[:, c0 + 4:c0 + 5],
                                                         st_t[:, c0 + 2:c0 + 3],
                                                         st_t[:, c0 + 2:c0 + 3], ALU.mult),
                    waits=[m1])
        m4 = p.emit(VE, lambda e, c0=sb: e.tensor_tensor(st_t[:, c0 + 5:c0 + 6],
                                                         st_t[:, c0 + 3:c0 + 4],
                                                         st_t[:, c0 + 4:c0 + 5], ALU.subtract),
                    waits=[m2, m3])
        l1 = p.emit(SE, lambda e, c0=sb: e.activation(st_t[:, c0 + 6:c0 + 7],
                                                      st_t[:, c0 + 5:c0 + 6], AF.Ln,
                                                      bias=cst[:, 2:3], scale=1.0),
                    waits=[m4, ev_cst])
        l2 = p.emit(SE, lambda e, c0=sb: e.activation(st_t[:, c0 + 7:c0 + 8],
                                                      st_t[:, c0 + 6:c0 + 7], AF.Exp,
                                                      bias=0.0, scale=-0.5),
                    waits=[l1])
        mrs = p.emit(VE, lambda e, c0=sb: e.scalar_tensor_tensor(st_t[:, c0 + 4:c0 + 5],
                                                                 st_t[:, c0 + 2:c0 + 3],
                                                                 -1.0, st_t[:, c0 + 7:c0 + 8],
                                                                 ALU.mult, ALU.mult),
                     waits=[l2])
        ap1 = p.emit(SE, lambda e, r2=rr, c0=sb: e.activation(x2_t[r2][:, :], r_t[r2][:, :],
                                                              AF.Identity,
                                                              bias=st_t[:, c0 + 4:c0 + 5],
                                                              scale=st_t[:, c0 + 7:c0 + 8]),
                     waits=[mrs, s2] + list(x2_rel[rr]))
        last_stat_ev[rr] = ap1
        r_rel[rr] = ap1
        stx = dma(SY, d_x2[128 * i:128 * (i + 1), :], x2_t[rr][:, :], waits=[ap1],
                  stream=("D3" if rr == 0 else "D4"))
        mx = p.emit(VE, lambda e, r2=rr, ii=i:
                    e.reduce_max(runmax[:, ii:ii + 1], x2_t[r2][:, :], axis=AX,
                                 apply_absolute_value=True),
                    waits=[ap1])
        x2_rel[rr] = (stx, mx)
        st_last2[rr] = stx
        ev_ln1 = mx
    ev_ln1_dma = st_last2[1]
    ev_ln1_dma2 = st_last2[0]
    if "x2" in dbg:
        dx2o = nc.dram_tensor("dbg_x2", [T, 1024], F32, kind="ExternalOutput").ap()
        dbg_evs.append(dma(G, dx2o[:, :], d_x2[:, :], waits=[ev_ln1_dma]))

    rm1 = p.emit(VE, lambda e: e.reduce_max(rmx1[:, 0:1], runmax[:, :], axis=AX),
                 waits=[ev_ln1])
    sm1 = dma(SY, d_mx[:], rmx1[:, 0], waits=[rm1])
    lm1 = dma(SY, mrow[0:1, :], d_mx[None, :], waits=[sm1])
    rm2 = p.emit(VE, lambda e: e.reduce_max(scal1[0:1, 4:5], mrow[0:1, :], axis=AX),
                 waits=[lm1])
    sm2 = dma(SY, h_ccmi.ap()[None, :], scal1[0:1, 4:5], waits=[rm2])
    cc2 = p.emit(G, lambda e: e.collective_compute(
        "AllReduce", ALU.max,
        replica_groups=[[0, 1, 2, 3, 4, 5, 6, 7]],
        ins=[h_ccmi.ap()], outs=[h_ccmo.ap()]),
        waits=[sm2], sem="CC")
    lm2 = dma(SY, scal1[0:1, 5:6], h_ccmo.ap()[None, :], waits=[cc2])
    t1 = p.emit(VE, lambda e: e.tensor_tensor(scal1[0:1, 6:7], scal1[0:1, 5:6],
                                              sig_s[0:1, 0:1], ALU.mult),
                waits=[lm2, ev_sig])
    t2 = p.emit(VE, lambda e: e.tensor_scalar(scal1[0:1, 0:1], scal1[0:1, 6:7],
                                              1.0 / 127.0, None, ALU.mult),
                waits=[t1])
    t3 = p.emit(VE, lambda e: e.reciprocal(scal1[0:1, 1:2], scal1[0:1, 0:1]),
                waits=[t2])
    sm3 = dma(SY, d_scal[None, 0:2], scal1[0:1, 0:2], waits=[t3])
    bc1 = dma(SY, scal[:, 1:2], d_scal[None, 0:1].partition_broadcast(128), waits=[sm3])
    bc2 = dma(SY, scal[:, 0:1], d_scal[None, 1:2].partition_broadcast(128), waits=[sm3])
    ev_sx = bc1
    ev_sxinv = bc2
    if "scals" in dbg:
        dsc = nc.dram_tensor("dbg_scals", [1, 8], F32, kind="ExternalOutput").ap()
        dbg_evs.append(dma(SY, dsc[:, :], scal1[0:1, :], waits=[t3, rm2, lm2]))

    if upto == "LN1":
        _finish(p, nc, es, sems, inner_scopes)
        return nc

    scope_L.close()
    scope_B0.close()
    scope_R0.close()

    scope_F = track(ExitStack())
    sbf = lambda name, shape, dt: scope_F.enter_context(
        nc.sbuf_tensor(name, shape, dt, side="right"))
    xint_fm = sbf("xint_fm", [128, 8 * T], BF16)
    NR = 4
    ru_t = [sbf(f"ru{r}", [128, 1024], F32) for r in range(NR)]
    ri_t = [sbf(f"ri{r}", [128, 1024], I32) for r in range(NR)]
    rf_t = [sbf(f"rf{r}", [128, 1024], BF16) for r in range(NR)]
    rth_t = [sbf(f"rth{r}", [128, 1024], BF16) for r in range(NR)]
    rr1_t = [sbf(f"rr1{r}", [128, 1024], F32) for r in range(NR)]
    xq16 = [sbf(f"xq16{r}", [128, 1024], F32) for r in range(4)]
    xib = [sbf(f"xib{r}", [128, 1024], BF16) for r in (0, 1)]
    xb16a = [sbf(f"xb16a{r}", [128, 1024], BF16) for r in range(NR)]
    wgT = [sbf(f"wgT{r}", [128, 1024], BF16) for r in (0, 1, 2)]
    wuT = [sbf(f"wuT{r}", [128, 1024], BF16) for r in (0, 1, 2)]
    silu_t = [sbf(f"silu{r}", [128, 512], F32) for r in (0, 1)]
    hstg = [sbf(f"hstg{r}", [128, 2048], F32) for r in (0, 1)]
    NXR = NT
    xrev = [[None] * 9 for _ in range(NXR)]
    xld_ev = [None] * NXR
    xint_jev = [None] * NJ

    def x_ld(r):
        w = [ev_ln1_dma, ev_ln1_dma2]
        if r >= NR:
            w.append(xrev[r - NR][0])
        xld_ev[r] = dma(SY, xq16[r % NR][:, :], d_x2[128 * r:128 * (r + 1), :],
                        waits=w, stream=("D1" if r % 2 == 0 else "D4"))

    def x_st(r, s):
        sl = r % NR
        tb2 = r % 2
        u, i_, nf, th, r1 = ru_t[sl], ri_t[sl], rf_t[sl], rth_t[sl], rr1_t[sl]
        rel = lambda st2: (xrev[r - NR][st2] if r >= NR else None)
        if s == 0:
            return p.emit(SE, lambda e: e.activation(u[:, :], xq16[sl][:, :], AF.Identity,
                                                     bias=cst[:, 3:4], scale=scal[:, 0:1]),
                          waits=[xld_ev[r], ev_sx, ev_sxinv, ev_cst, rel(2)])
        if s == 1:
            return p.emit(G, lambda e: e.tensor_copy(i_[:, :], u[:, :]),
                          waits=[xrev[r][0], rel(4)])
        if s == 2:
            return p.emit(VE, lambda e: e.scalar_tensor_tensor(nf[:, :], i_[:, :], 1.0,
                                                               u[:, :], ALU.mult, ALU.subtract),
                          waits=[xrev[r][1], rel(3)])
        if s == 3:
            return p.emit(SE, lambda e: e.activation(th[:, :], nf[:, :], AF.Tanh,
                                                     bias=0.0, scale=TBIG),
                          waits=[xrev[r][2]])
        if s == 4:
            return p.emit(VE, lambda e: e.scalar_tensor_tensor(r1[:, :], th[:, :], -0.5,
                                                               i_[:, :], ALU.mult, ALU.add),
                          waits=[xrev[r][3], rel(5)])
        if s == 5:
            return p.emit(SE, lambda e: e.activation(xb16a[sl][:, :], r1[:, :], AF.Identity,
                                                     bias=cst[:, 4:5], scale=1.0),
                          waits=[xrev[r][4], rel(6)])
        if s == 6:
            w = [xrev[r][5]]
            if r >= 2:
                w.append(xrev[r - 2][7])
            return p.emit(VE, lambda e: e.tensor_scalar(xib[tb2][:, :], xb16a[sl][:, :],
                                                        127.0, None, ALU.min),
                          waits=w)
        if s == 7:
            trs = None
            for kc in range(NC):
                trs = p.emit(TE, lambda e, kk=kc:
                             e.transpose(tbank[tb2][:, 128 * kk:128 * (kk + 1)],
                                         xib[tb2][:, 128 * kk:128 * (kk + 1)], idb[:, :]),
                             waits=[xrev[r][6], ld_id,
                                    brd.get(6 + tb2) if kc == 0 else None],
                             inc=(kc == NC - 1))
            bwr[6 + tb2] = trs
            return trs
        ev = p.emit(VE, lambda e:
                    e.tensor_scalar(
                        xint_fm[:, :].rearrange("p (kc t) -> p kc t", t=2048)[:, :, 128 * r:128 * (r + 1)],
                        tbank[tb2][:, :].rearrange("p (kc t) -> p kc t", t=128),
                        -128.0, None, ALU.max),
                    waits=[xrev[r][7]])
        brd[6 + tb2] = ev
        return ev

    for r0 in range(min(NR, NXR)):
        x_ld(r0)
    for t in range(NXR + 9):
        for s in range(8, -1, -1):
            r = t - s
            if 0 <= r < NXR:
                xrev[r][s] = x_st(r, s)
        if t + NR < NXR:
            x_ld(t + NR)
    for j in range(NJ):
        xint_jev[j] = xrev[4 * j + 3][8]
    ev_xint = xrev[NXR - 1][8]
    DBG("xint_fm", xint_fm[:, :], [ev_xint])

    wgT_rel = [None, None, None]
    wuT_rel = [None, None, None]
    silu_rel = [None, None]
    hstg_rel = [None, None]
    ev_h = None
    ev_hstore = None

    e_sg = p.emit(VE, lambda e: e.tensor_scalar(sxswg[:, :], swg_all[:, :],
                                                scal[:, 1:2], None, ALU.mult),
                  waits=[ev_scales, ev_sx])
    e_su = p.emit(VE, lambda e: e.tensor_scalar(sxswu[:, :], swu_all[:, :],
                                                scal[:, 1:2], None, ALU.mult),
                  waits=[ev_scales, ev_sx])

    for ic in range(NI):
        slot = ic % 3
        cr = ic % 8
        sl3 = ic // 8
        ldg = dma(SY, wgT[slot][:, :],
                  d_wqo[128 * cr:128 * (cr + 1), 1024 * sl3:1024 * (sl3 + 1)],
                  waits=[cc_w, wgT_rel[slot]], stream="D2")
        ldu = dma(SY, wuT[slot][:, :],
                  d_wqo[128 * cr:128 * (cr + 1), 3072 + 1024 * sl3:3072 + 1024 * (sl3 + 1)],
                  waits=[cc_w, wuT_rel[slot]], stream="D3")
        h_tile_evs = []
        gmm = None
        umm = None
        for j in range(NJ):
            gb = (2 * (ic * NJ + j)) % 4
            ub = gb + 1
            for kc in range(NC):
                st = kc == 0
                w = [ldg, xint_jev[j]]
                if st:
                    w.append(brd.get(gb))
                gmm = p.emit(TE, lambda e, b3=gb, kk=kc, jj=j, sl=slot, s=st:
                             e.matmul(bank[b3][:, :],
                                      wgT[sl][:, 128 * kk:128 * (kk + 1)],
                                      xint_fm[:, 2048 * kk + 512 * jj:2048 * kk + 512 * (jj + 1)],
                                      start=s, stop=(kk == NC - 1)),
                             waits=w, inc=(kc == NC - 1))
            bwr[gb] = gmm
            for kc in range(NC):
                st = kc == 0
                w = [ldu, xint_jev[j]]
                if st:
                    w.append(brd.get(ub))
                umm = p.emit(TE, lambda e, b3=ub, kk=kc, jj=j, sl=slot, s=st:
                             e.matmul(bank[b3][:, :],
                                      wuT[sl][:, 128 * kk:128 * (kk + 1)],
                                      xint_fm[:, 2048 * kk + 512 * jj:2048 * kk + 512 * (jj + 1)],
                                      start=s, stop=(kk == NC - 1)),
                             waits=w, inc=(kc == NC - 1))
            bwr[ub] = umm
            sr = j % 2
            es1 = p.emit(SE, lambda e, b3=gb, c2=ic, sl2=sr:
                         e.activation(silu_t[sl2][:, :], bank[b3][:, :], AF.Silu,
                                      bias=0.0, scale=sxswg[:, c2:c2 + 1]),
                         waits=[gmm, e_sg, silu_rel[sr]])
            brd[gb] = es1
            hsl = ic % 2
            eh = p.emit(VE, lambda e, b3=ub, c2=ic, jj=j, sl2=sr, hs=hsl:
                        e.scalar_tensor_tensor(hstg[hs][:, 512 * jj:512 * (jj + 1)],
                                               bank[b3][:, :], sxswu[:, c2:c2 + 1],
                                               silu_t[sl2][:, :], ALU.mult, ALU.mult),
                        waits=[umm, es1, e_su] + ([hstg_rel[hsl]] if j == 0 else []))
            brd[ub] = eh
            silu_rel[sr] = eh
            h_tile_evs.append(eh)
            emx = p.emit(VE, lambda e, c2=ic, jj=j, hs=hsl:
                         e.reduce_max(runmax2[:, 4 * c2 + jj:4 * c2 + jj + 1],
                                      hstg[hs][:, 512 * jj:512 * (jj + 1)],
                                      axis=AX, apply_absolute_value=True),
                         waits=[eh])
            ev_h = emx
        ev_hstore = dma(SY, d_h[128 * ic:128 * (ic + 1), :], hstg[ic % 2][:, :],
                        waits=h_tile_evs + [ev_h], stream="D3")
        hstg_rel[ic % 2] = ev_hstore
        wgT_rel[slot] = gmm
        wuT_rel[slot] = umm
    if "hd" in dbg:
        dho = nc.dram_tensor("dbg_hd", [NI * 128, T], F32, kind="ExternalOutput").ap()
        dbg_evs.append(dma(G, dho[:, :], d_h[:, :], waits=[ev_hstore]))

    if upto == "GATE":
        _finish(p, nc, es, sems, inner_scopes)
        return nc

    rm1b = p.emit(VE, lambda e: e.reduce_max(rmx1[:, 1:2], runmax2[:, :], axis=AX),
                  waits=[ev_h])
    sm1b = dma(SY, d_mx2[:], rmx1[:, 1], waits=[rm1b])
    lm1b = dma(SY, mrow[0:1, :], d_mx2[None, :], waits=[sm1b, rm2])
    rm2b = p.emit(VE, lambda e: e.reduce_max(scal1[0:1, 4:5], mrow[0:1, :], axis=AX),
                  waits=[lm1b, t1, sm2])
    sm2b = dma(SY, h_ccmi2.ap()[None, :], scal1[0:1, 4:5], waits=[rm2b])
    cc3 = p.emit(G, lambda e: e.collective_compute(
        "AllReduce", ALU.max,
        replica_groups=[[0, 1, 2, 3, 4, 5, 6, 7]],
        ins=[h_ccmi2.ap()], outs=[h_ccmo2.ap()]),
        waits=[sm2b], sem="CC")
    lm2b = dma(SY, scal1[0:1, 5:6], h_ccmo2.ap()[None, :], waits=[cc3, t1])
    t1b = p.emit(VE, lambda e: e.tensor_tensor(scal1[0:1, 6:7], scal1[0:1, 5:6],
                                               sig_s[0:1, 1:2], ALU.mult),
                 waits=[lm2b, ev_sig, t2])
    t2b = p.emit(VE, lambda e: e.tensor_scalar(scal1[0:1, 2:3], scal1[0:1, 6:7],
                                               1.0 / 127.0, None, ALU.mult),
                 waits=[t1b])
    t3b = p.emit(VE, lambda e: e.reciprocal(scal1[0:1, 3:4], scal1[0:1, 2:3]),
                 waits=[t2b])
    sm3b = dma(SY, d_scal[None, 2:4], scal1[0:1, 2:4], waits=[t3b])
    bc3 = dma(SY, scal[:, 3:4], d_scal[None, 2:3].partition_broadcast(128), waits=[sm3b])
    bc4 = dma(SY, scal[:, 2:3], d_scal[None, 3:4].partition_broadcast(128), waits=[sm3b])
    ev_sh = bc3
    ev_shinv = bc4

    scope_F.close()

    scope_D = track(ExitStack())
    sbd = lambda name, shape, dt: scope_D.enter_context(
        nc.sbuf_tensor(name, shape, dt, side="right"))
    NDR = 4
    wdT_all = sbd("wdT_all", [128, NI * 1024], BF16)
    swd_row = sbd("swd_row", [128, 1024], F32)
    hint_tb = [sbd(f"hint_tb{r}", [128, NI * 512], BF16) for r in (0, 1)]
    du_t = [sbd(f"du{r}", [128, 512], F32) for r in range(NDR)]
    di_t = [sbd(f"di{r}", [128, 512], I32) for r in range(NDR)]
    df_t = [sbd(f"dfr{r}", [128, 512], BF16) for r in range(NDR)]
    dth_t = [sbd(f"dth{r}", [128, 512], BF16) for r in range(NDR)]
    dr1_t = [sbd(f"dr1{r}", [128, 512], F32) for r in range(NDR)]
    hsl_t = [sbd(f"hsl{r}", [128, 512], F32) for r in range(NDR)]
    hb16a = [sbd(f"hb16a{r}", [128, 512], BF16) for r in range(NDR)]
    hb16b = [sbd(f"hb16b{r}", [128, 512], BF16) for r in range(NDR)]
    y2s4 = [sbd(f"y2s{r}", [128, 1024], F32) for r in range(4)]
    x216 = [sbd(f"x216{r}", [128, 1024], F32) for r in range(4)]
    out_t = [sbd(f"outt{r}", [128, 1024], F32) for r in (0, 1)]

    smd = p.emit(SY, lambda e: e.dma_start(out=d_swd[:].rearrange("(c q) -> q c", q=128),
                                           in_=swd_all[:, :],
                                           allow_slow_non_contiguous=True),
                 waits=[ev_swd], dma=True, sem="D2")
    ev_wdT = None
    for cr in range(8):
        ev_wdT = dma(SY, wdT_all[:, :].rearrange("p (ic d) -> p ic d", d=1024)[:, :, 128 * cr:128 * (cr + 1)],
                     d_wqo[128 * cr:128 * (cr + 1), 6144:8960].rearrange("p (ic d) -> p ic d", d=128),
                     waits=[cc_w, ev_h, ev_hstore], stream="D2")
    DBG("wdT_all", wdT_all[:, :], [ev_wdT])
    bcd = dma(SY, swd_row[:, :], d_swd[None, :].partition_broadcast(128),
              waits=[smd, ev_hstore], stream="D2")
    ev_swdrow = bcd

    NRND = 4 * NI
    rev = [[None] * 8 for _ in range(NRND)]
    ld_ev = [None] * NRND
    dn_last = [None] * 4
    dmm_ev = [[None] * 8 for _ in range(4)]
    ey_ev = [[None] * 8 for _ in range(4)]
    x2_ld = [[None] * 4 for _ in range(4)]
    er_ev = [[None] * 4 for _ in range(4)]
    out_rel = [None, None]
    r2_rel = r_rel
    last_stat2 = [None]
    BPAIRS = ((0, 1), (2, 3), (4, 5))

    def emit_ld(r):
        tb, ic = r // NI, r % NI
        w = [ev_hstore]
        if r >= NDR:
            w.append(rev[r - NDR][0])
        ld_ev[r] = dma(SY, hsl_t[r % NDR][:, :],
                       d_h[128 * ic:128 * (ic + 1), 512 * tb:512 * (tb + 1)],
                       waits=w, stream=("D1", "D2", "D3", "D4")[r % 4])

    def st_h(r, s):
        tb, ic = r // NI, r % NI
        sl = r % NDR
        u, i_, nf, th, r1 = du_t[sl], di_t[sl], df_t[sl], dth_t[sl], dr1_t[sl]
        rel = lambda st2: (rev[r - NDR][st2] if r >= NDR else None)
        if s == 0:
            return p.emit(SE, lambda e: e.activation(u[:, :], hsl_t[sl][:, :], AF.Identity,
                                                     bias=cst[:, 3:4], scale=scal[:, 2:3]),
                          waits=[ld_ev[r], ev_sh, ev_shinv, rel(2)])
        if s == 1:
            return p.emit(G, lambda e: e.tensor_copy(i_[:, :], u[:, :]),
                          waits=[rev[r][0], rel(4)])
        if s == 2:
            return p.emit(VE, lambda e: e.scalar_tensor_tensor(nf[:, :], i_[:, :], 1.0,
                                                               u[:, :], ALU.mult, ALU.subtract),
                          waits=[rev[r][1], rel(3)])
        if s == 3:
            return p.emit(SE, lambda e: e.activation(th[:, :], nf[:, :], AF.Tanh,
                                                     bias=0.0, scale=TBIG),
                          waits=[rev[r][2]])
        if s == 4:
            return p.emit(VE, lambda e: e.scalar_tensor_tensor(r1[:, :], th[:, :], -0.5,
                                                               i_[:, :], ALU.mult, ALU.add),
                          waits=[rev[r][3], rel(5)])
        if s == 5:
            return p.emit(SE, lambda e: e.activation(hb16a[sl][:, :], r1[:, :], AF.Identity,
                                                     bias=cst[:, 4:5], scale=1.0),
                          waits=[rev[r][4], rel(6)])
        if s == 6:
            w6 = [rev[r][5]]
            if tb >= 2:
                w6.append(dn_last[tb - 2])
            return p.emit(VE, lambda e: e.tensor_scalar(
                hint_tb[tb % 2][:, 512 * ic:512 * (ic + 1)],
                hb16a[sl][:, :], 127.0, -128.0, ALU.min, ALU.max),
                          waits=w6)
        return rev[r][6]

    def mk_down_group(tb, k):
        def go():
            j, dg = k // 2, k % 2
            i = 4 * tb + j
            bidx = BPAIRS[i % 3][dg]
            mm = None
            for ic in range(NI):
                st = ic == 0
                w = [rev[NI * tb + ic][7], ev_wdT]
                if st:
                    w.append(brd.get(bidx))
                mm = p.emit(TE, lambda e, b3=bidx, c2=ic, g=dg, tb2=tb, j2=j, s=st:
                            e.matmul(bank[b3][:, :],
                                     hint_tb[tb2 % 2][:, 512 * c2 + 128 * j2:512 * c2 + 128 * (j2 + 1)],
                                     wdT_all[:, 1024 * c2 + 512 * g:1024 * c2 + 512 * (g + 1)],
                                     start=s, stop=(c2 == NI - 1)),
                            waits=w, inc=(ic == NI - 1))
            bwr[bidx] = mm
            dmm_ev[tb][k] = mm
            if k == 7:
                dn_last[tb] = mm
        return go

    def mk_ey(tb, k):
        def go():
            j, dg = k // 2, k % 2
            i = 4 * tb + j
            bidx = BPAIRS[i % 3][dg]
            w = [dmm_ev[tb][k], ev_sh, ev_swdrow]
            if tb >= 1:
                w.append(er_ev[tb - 1][j])
            ey = p.emit(VE, lambda e: e.scalar_tensor_tensor(y2s4[j][:, 512 * dg:512 * (dg + 1)],
                                                             bank[bidx][:, :], scal[:, 3:4],
                                                             swd_row[:, 512 * dg:512 * (dg + 1)],
                                                             ALU.mult, ALU.mult),
                        waits=w)
            brd[bidx] = ey
            ey_ev[tb][k] = ey
        return go

    def mk_ldx2(tb, j):
        def go():
            i = 4 * tb + j
            w = [ev_ln1_dma, ev_ln1_dma2, ev_hstore]
            if tb >= 1:
                w.append(er_ev[tb - 1][j])
            x2_ld[tb][j] = dma(SY, x216[j][:, :], d_x2[128 * i:128 * (i + 1), :],
                               waits=w, stream="D1")
        return go

    def mk_ln2(tb, j):
        def go():
            i = 4 * tb + j
            rr = i % 2
            er = p.emit(G, lambda e: e.tensor_tensor(r_t[rr][:, :], y2s4[j][:, :],
                                                     x216[j][:, :], ALU.add),
                        waits=[ey_ev[tb][2 * j], ey_ev[tb][2 * j + 1], x2_ld[tb][j],
                               r2_rel[rr]])
            er_ev[tb][j] = er
            s1 = p.emit(VE, lambda e: e.reduce_sum(st_t[:, 0:1], r_t[rr][:, :], axis=AX),
                        waits=[er, last_stat2[0]])
            sq = p.emit(SE, lambda e: e.activation(rsq_t[rr][:, :], r_t[rr][:, :],
                                                   AF.Square, bias=0.0, scale=1.0),
                        waits=[er])
            s2 = p.emit(VE, lambda e: e.reduce_sum(st_t[:, 1:2], rsq_t[rr][:, :], axis=AX),
                        waits=[sq])
            m1 = p.emit(VE, lambda e: e.tensor_scalar(st_t[:, 2:3], st_t[:, 0:1],
                                                      1.0 / 1024.0, None, ALU.mult),
                        waits=[s1])
            m2 = p.emit(VE, lambda e: e.tensor_scalar(st_t[:, 3:4], st_t[:, 1:2],
                                                      1.0 / 1024.0, None, ALU.mult),
                        waits=[s2])
            m3 = p.emit(VE, lambda e: e.tensor_tensor(st_t[:, 4:5], st_t[:, 2:3],
                                                      st_t[:, 2:3], ALU.mult),
                        waits=[m1])
            m4 = p.emit(VE, lambda e: e.tensor_tensor(st_t[:, 5:6], st_t[:, 3:4],
                                                      st_t[:, 4:5], ALU.subtract),
                        waits=[m2, m3])
            l1 = p.emit(SE, lambda e: e.activation(st_t[:, 6:7], st_t[:, 5:6], AF.Ln,
                                                   bias=cst[:, 2:3], scale=1.0),
                        waits=[m4])
            l2 = p.emit(SE, lambda e: e.activation(st_t[:, 7:8], st_t[:, 6:7], AF.Exp,
                                                   bias=0.0, scale=-0.5),
                        waits=[l1])
            mrs = p.emit(VE, lambda e: e.scalar_tensor_tensor(st_t[:, 4:5], st_t[:, 2:3],
                                                              -1.0, st_t[:, 7:8],
                                                              ALU.mult, ALU.mult),
                         waits=[l2])
            ap = p.emit(SE, lambda e: e.activation(out_t[i % 2][:, :], r_t[rr][:, :],
                                                   AF.Identity, bias=st_t[:, 4:5],
                                                   scale=st_t[:, 7:8]),
                        waits=[mrs, out_rel[i % 2]])
            last_stat2[0] = ap
            r2_rel[rr] = ap
            fs = dma(SY, d_out[128 * i:128 * (i + 1), :], out_t[i % 2][:, :],
                     waits=[ap], stream="D4")
            out_rel[i % 2] = fs
        return go

    trig = [NI * tb + NI + 7 for tb in range(4)]
    sched = {}

    def at(t, fn):
        sched.setdefault(t, []).append(fn)

    for tb in range(4):
        for k in range(8):
            at(trig[tb] + 2 * k, mk_down_group(tb, k))
            at(trig[tb] + 2 * k + 3, mk_ey(tb, k))
        for j in range(4):
            at(trig[tb] + 5 * j + 1, mk_ldx2(tb, j))
        for j in range(4):
            at(trig[tb] + NI + 5 * j, mk_ln2(tb, j))
    maxt = max(sched)
    for r0 in range(min(NDR, NRND)):
        emit_ld(r0)
    for t in range(max(NRND + 8, maxt + 1)):
        for fn in sched.get(t, ()):
            fn()
        for s in range(7, -1, -1):
            r = t - s
            if 0 <= r < NRND:
                rev[r][s] = st_h(r, s)
        if t + NDR < NRND:
            emit_ld(t + NDR)


    _finish(p, nc, es, sems, inner_scopes)
    return nc


def _finish(p, nc, es, sems, inner_scopes=()):
    for sc in reversed(list(inner_scopes)):
        sc.close()
    with nc.Block() as block:
        def mk_runner(engine_name):
            def _run(eng):
                for need, fn, sem, by in p.ops[engine_name]:
                    for s, v in need:
                        eng.wait_ge(sems[s], v)
                    ins = fn(eng)
                    if sem is not None:
                        ins.then_inc(sems[sem], by)
                if engine_name == "sync":
                    for s in ("DS", "DG", "D1", "D2", "D3", "D4"):
                        if p.counts.get(s):
                            eng.wait_ge(sems[s], p.counts[s])
            return _run
        block.tensor(mk_runner("tensor"))
        block.vector(mk_runner("vector"))
        block.scalar(mk_runner("scalar"))
        block.gpsimd(mk_runner("gpsimd"))
        block.sync(mk_runner("sync"))
    es.close()


_NC_CACHE = {}


def _get_nc(upto="ALL", dbg=()):
    key = (upto, tuple(sorted(dbg)))
    if key not in _NC_CACHE:
        _NC_CACHE[key] = build_nc(upto, dbg)
    return _NC_CACHE[key]


def _host_prep(inputs):
    x = np.asarray(inputs["L0_attn_x"], np.float32)
    Wq = np.asarray(inputs["L0_attn_Wq"], np.float32)
    Wk = np.asarray(inputs["L0_attn_Wk"], np.float32)
    Wv = np.asarray(inputs["L0_attn_Wv"], np.float32)
    Wo = np.asarray(inputs["L0_attn_Wo"], np.float32)
    orf = np.asarray(inputs["L0_attn_orf"], np.float32)

    def prep_wqk(W):
        a = W.T.reshape(8, 128, 8, 128)
        return np.ascontiguousarray(a.transpose(1, 2, 0, 3).reshape(128, 8192))

    def prep_wvo(W):
        a = W.T.reshape(8, 128, 1024)
        return np.ascontiguousarray(a.transpose(1, 0, 2).reshape(128, 8192))

    indh = np.zeros((128, 128), np.float32)
    for p_ in range(128):
        for c in range(8):
            indh[p_, 16 * c + 2 * c + (1 if p_ >= 64 else 0)] = 1.0

    shared = {
        "wqt": prep_wqk(Wq),
        "wkt": prep_wqk(Wk),
        "wvt": prep_wvo(Wv),
        "wot": prep_wvo(Wo),
        "orft": np.ascontiguousarray(orf.T),
        "identh": np.eye(128, dtype=np.float32),
        "indh": indh,
        "bqh": np.ascontiguousarray(np.asarray(inputs["L0_attn_bq"], np.float32).reshape(8, 128).T),
        "bkh": np.ascontiguousarray(np.asarray(inputs["L0_attn_bk"], np.float32).reshape(8, 128).T),
        "bvh": np.ascontiguousarray(np.asarray(inputs["L0_attn_bv"], np.float32).reshape(8, 128).T),
        "boh": np.asarray(inputs["L0_attn_bo"], np.float32).reshape(1, 1024),
        "clip_x": np.asarray(inputs["L0_ffn_clip_x"], np.float32).reshape(1, 1),
        "clip_down": np.asarray(inputs["L0_ffn_clip_down"], np.float32).reshape(1, 1),
    }
    wg = np.asarray(inputs["L0_ffn_w_gate"], np.float32)
    wu = np.asarray(inputs["L0_ffn_w_up"], np.float32)
    wd = np.asarray(inputs["L0_ffn_w_down"], np.float32)

    in_maps = []
    for core in range(8):
        b = core // 2
        hf = core % 2
        xs = np.ascontiguousarray(x[b, T * hf:T * (hf + 1), :])
        xfm = np.ascontiguousarray(xs.T.reshape(8, 128, T).transpose(1, 0, 2).reshape(128, 8 * T))
        m = dict(shared)
        m["x_tm"] = xs
        m["x_fm"] = xfm
        wgm = np.ones((384, 1024), np.float32)
        wum = np.ones((384, 1024), np.float32)
        for s in range(3):
            ic = core + 8 * s
            if ic < NI:
                wgm[128 * s:128 * (s + 1)] = wg[128 * ic:128 * (ic + 1)]
                wum[128 * s:128 * (s + 1)] = wu[128 * ic:128 * (ic + 1)]
        m["wgq_my"] = wgm
        m["wuq_my"] = wum
        m["wdq_my"] = np.ascontiguousarray(wd[128 * core:128 * (core + 1)])
        in_maps.append(m)
    return in_maps


def kernel(**inputs):
    in_maps = _host_prep(inputs)
    nc = _get_nc("ALL", ())
    res = run_bass_kernel_spmd(nc, in_maps, core_ids=list(range(8)))
    outs = [res.results[c]["out"] for c in range(8)]
    full = np.stack(outs, axis=0).reshape(4, 2, T, 1024).reshape(4, 4096, 1024)
    return np.ascontiguousarray(full.astype(np.float32))
